# version 02
# baseline (speedup 1.0000x reference)
import math
import numpy as np
import concourse.bass as bass
import concourse.mybir as mybir
from concourse.bass import IndirectOffsetOnAxis
from concourse.bass_utils import run_bass_kernel_spmd

F32 = mybir.dt.float32
BF16 = mybir.dt.bfloat16
I32 = mybir.dt.int32
ALU = mybir.AluOpType
AF = mybir.ActivationFunctionType
AXL = mybir.AxisListType

B, N, D, H, HD = 2, 8192, 1024, 16, 64
TOK, NTS = 2048, 16
QCAP, KCAP, KSL = 384, 1024, 256
ROUNDS, NTHR = 11, 9
LO0, D0 = -64.0, 128.0
PI = math.pi
TP = 2 * PI
C1 = 6.283203125
C2 = -1.7881393432617188e-05
C3 = 6.357301884918343e-08
LNI = math.log(10000.0) / 32.0
LNI_HI = float(np.float32(LNI))
LNI_LO = float(np.float32(LNI - LNI_HI))
EXPB = -20.0
GROUPS = [[0, 1, 2, 3], [4, 5, 6, 7]]


def build_nc():
    from contextlib import ExitStack
    nc = bass.Bass("TRN2", target_bir_lowering=False, debug=False)
    es = ExitStack()

    di = lambda nm, sh, dt=F32: nc.dram_tensor(nm, sh, dt, kind="ExternalInput").ap()
    x_T = di("x_T", [TOK, D])
    x_S = di("x_S", [TOK, D])
    x_full = di("x_full", [N, D])
    mask_f = di("mask_f", [4096, 4096])
    pos_f = di("pos_f", [1, 4096], I32)
    wqkvT_d = di("wqkvT", [D, 3 * D])
    bqkv_d = di("bqkv", [1, 3 * D])
    wprojT_d = di("wprojT", [D, D])
    bproj_d = di("bproj", [1, D])
    gamma_d = di("gamma_d", [1, D])
    beta_d = di("beta_d", [1, D])
    rw_d = di("rw_d", [128, 8])
    rb_d = di("rb_d", [1, 1])
    wq_d = di("wq_d", [D, D])
    wk_d = di("wk_d", [D, D])
    wv_d = di("wv_d", [D, D])
    wo_d = di("wo_d", [D, D])
    w1_d = di("w1_d", [D, 4 * D])
    w2_d = di("w2_d", [4 * D, D])
    ln1_d = di("ln1_d", [1, D])
    ln2_d = di("ln2_d", [1, D])
    g8_d = di("g8_d", [128, 8])
    lt_d = di("lt_d", [128, 128])
    onesr_d = di("onesr_d", [1, 128])
    riota_d = di("riota_d", [128, 8])
    dc_d = di("dc_d", [128, ROUNDS])
    sc_d = di("sc_d", [128, 8])
    member_d = di("member_d", [128, 1])
    bqq_d = di("bqq_d", [128, 8])

    out_d = nc.dram_tensor("out_sh", [TOK + 1, D], F32, kind="ExternalOutput").ap()

    dsc = lambda nm, sh, dt=F32: nc.dram_tensor(nm, sh, dt)
    w_ag_in = dsc("w_ag_in", [1, TOK])
    w_ag_out = dsc("w_ag_out", [4, TOK])
    kv_ar_in = dsc("kv_ar_in", [128, 520])
    kv_ar_out = dsc("kv_ar_out", [128, 520])
    kvf_ag_in = dsc("kvf_ag_in", [KSL, 2 * D], BF16)
    kvf_ag_out = dsc("kvf_ag_out", [KCAP, 2 * D], BF16)
    s_scr = dsc("s_scr", [1, N])
    scr_dram = dsc("scr_dram", [TOK, D], BF16)
    m_scr1 = dsc("m_scr1", [QCAP, 4096], BF16)
    m_scr2 = dsc("m_scr2", [4097, QCAP], BF16)
    t_scr = dsc("t_scr", [QCAP, D], BF16)
    tk_scr = dsc("tk_scr", [KSL, D], BF16)

    sbt = lambda nm, sh, dt=F32: es.enter_context(nc.sbuf_tensor(nm, sh, dt))
    pst = lambda nm, sh, dt=F32: es.enter_context(nc.psum_tensor(nm, sh, dt))

    c_g8 = sbt("c_g8", [128, 8])
    c_lt = sbt("c_lt", [128, 128])
    c_ones = sbt("c_ones", [1, 128])
    c_onesb = sbt("c_onesb", [1, 128], BF16)
    c_riota = sbt("c_riota", [128, 8])
    c_dc = sbt("c_dc", [128, ROUNDS])
    c_sc = sbt("c_sc", [128, 8])
    c_member = sbt("c_member", [128, 1])
    c_bqkv = sbt("c_bqkv", [1, 3 * D], BF16)
    c_bqq = sbt("c_bqq", [128, 8])
    c_bqqn = sbt("c_bqqn", [128, 8])
    c_bproj = sbt("c_bproj", [1, D], BF16)
    c_rw = sbt("c_rw", [128, 8])
    c_rw_hi = sbt("c_rw_hi", [128, 8], BF16)
    c_rw_lo = sbt("c_rw_lo", [128, 8], BF16)
    c_rb = sbt("c_rb", [1, 1])
    gam_rep = sbt("gam_rep", [128, D])
    bet_rep = sbt("bet_rep", [128, D])
    ln1_rep = sbt("ln1_rep", [128, D])
    ln2_rep = sbt("ln2_rep", [128, D])
    inv_rep = sbt("inv_rep", [128, 32])

    xt_buf = sbt("xt_buf", [128, 2 * D])
    stat6 = sbt("stat6", [128, 12])
    stat2 = sbt("stat2", [128, 2])
    negm = sbt("negm", [128, 1])
    sqv = sbt("sqv", [128, 1])
    rstd = sbt("rstd", [128, 1])
    arenaA = sbt("arenaA", [128, 8 * TOK], BF16)
    arenaB = sbt("arenaB", [128, 8 * TOK], BF16)
    A3 = sbt("A3", [128, 16384], BF16)
    A4 = sbt("A4", [128, 8192])
    A5 = sbt("A5", [128, 4096])
    A7 = sbt("A7", [128, 8192], BF16)
    xn_f32 = A5[:, 0:D]
    xn_bf = A7[:, 0:D]
    lob_bf = A7[:, D:2 * D]
    loT_t = A7[:, 2 * D:4 * D]
    kphi_m = A5[:, 0:D]
    kphi_e = A5[:, D:2 * D]
    qphi_m = A5[:, 0:512]
    qphi_e = A5[:, 512:1024]
    qphi_t = A5[:, 1024:1536]
    u_t = A5[:, 0:D]
    xns2 = A5[:, D:2 * D]
    acc1 = A5[:, 2 * D:3 * D]
    acc2 = A5[:, 3 * D:4 * D]
    outrow = A5[:, D:2 * D]
    S_half = A4[:, 0:4096]
    S_junk = A4[:, 4096:8192]

    w_sb = A5[0:1, 0:TOK]
    rowb = A5[0:1, 2048:3072]
    cj512 = A4[:, 4096:4608]
    w_rep = A4[:, 4608:5120]
    w_nat = A4[:, 5120:5184]
    rrec = A5[0:1, 0:QCAP]
    rrep_sb = A5[0:64, 512:512 + QCAP]
    G_bf = A7[:, 0:D]
    hn_q = A7[:, 0:3 * D]
    hn_k = A7[:, 3 * D:5 * D]
    rop_a = A5[:, 2 * D:3 * D]
    rop_h1 = A5[:, 3 * D:4 * D]
    xnq_v = A4[:, 0:3 * D]
    xnk_v = A4[:, 3 * D:5 * D]
    qp_v = A4[:, 5 * D:8 * D]
    x1_v = A5[:, 0:3 * D]
    mga_v = A5[:, 0:4096]
    qr_v = A7[:, 0:3 * D]
    kr_v = A7[:, 3 * D:5 * D]
    vp_v = A7[:, 5 * D:7 * D]
    h1b_v = A7[:, 0:3 * D]
    kT_v = arenaB[:, 0:8 * KCAP]
    hnTq_v = arenaB[:, 8 * KCAP:8 * KCAP + 8 * QCAP]
    hnTk_v = arenaB[:, 8 * KCAP + 8 * QCAP:8 * KCAP + 8 * QCAP + 8 * KSL]
    expT_v = arenaB[:, 8 * KCAP:8 * KCAP + 8 * QCAP]
    e01T_v = arenaB[:, 8 * KCAP + 8 * QCAP:8 * KCAP + 8 * QCAP + 8 * QCAP]
    w1b_v = arenaB[:, 0:8192]
    w2b_v = arenaB[:, 8192:10240]
    vA_v = arenaA[:, 0:8 * 1040]
    qT_v = arenaA[:, 8 * 1040:8 * 1040 + 8 * QCAP]
    oT_v = arenaA[:, 8 * 1040 + 8 * QCAP:8 * 1040 + 16 * QCAP]
    qtz_e = arenaA[:, 8 * 1040 + 16 * QCAP:8 * 1040 + 17 * QCAP]
    qtz_o = arenaA[:, 8 * 1040 + 17 * QCAP:8 * 1040 + 18 * QCAP]
    gT_v = A3[:, 0:32 * QCAP]
    h1T_v = A3[:, 32 * QCAP:32 * QCAP + 8 * QCAP]
    lo_s = sbt("lo_s", [1, 1])
    T_s = sbt("T_s", [128, 1])
    cj8 = sbt("cj8", [1, 8])
    cnt_s = sbt("cnt_s", [128, 1])
    nge_s = sbt("nge_s", [1, 1])
    thr_s = sbt("thr_s", [1, 1])
    keep_s = sbt("keep_s", [128, 64])
    keepq_s = sbt("keepq_s", [128, 64])
    zer64 = sbt("zer64", [128, 64])
    Sp_s = sbt("Sp_s", [128, 64])
    Sqp_s = sbt("Sqp_s", [128, 64])
    tot_s = sbt("tot_s", [128, 1])
    totq_s = sbt("totq_s", [128, 1])
    Sf_s = sbt("Sf_s", [128, 64])
    Sqf_s = sbt("Sqf_s", [128, 64])
    minrow = sbt("minrow", [1, 128])
    mv_s = sbt("mv_s", [128, 64])
    qsel = sbt("qsel", [128, 3])
    ksel = sbt("ksel", [128, 2])
    kself = sbt("kself", [128, 8])
    kkeepf = sbt("kkeepf", [128, 8])
    ckf_i = sbt("ckf_i", [128, 8], I32)
    tmpf8 = sbt("tmpf8", [128, 8])
    mtr = sbt("mtr", [128, QCAP], BF16)
    qkeep = sbt("qkeep", [128, 3])
    kkeep = sbt("kkeep", [128, 2])
    qn_i = sbt("qn_i", [128, 3], I32)
    kn_i = sbt("kn_i", [128, 2], I32)
    cq_i = sbt("cq_i", [128, 3], I32)
    ck_i = sbt("ck_i", [128, 2], I32)
    qrho_i = sbt("qrho_i", [128, 3], I32)
    tmpi = sbt("tmpi", [128, 3], I32)
    tmpf = sbt("tmpf", [128, 3])
    tmpf2 = sbt("tmpf2", [128, 3])
    wself = sbt("wself", [128, 3])
    posq_i = sbt("posq_i", [128, 3], I32)
    posk_i = sbt("posk_i", [128, 2], I32)
    posq_f = sbt("posq_f", [128, 3])
    posk_f = sbt("posk_f", [128, 2])

    k_bf = sbt("k_bf", [128, D], BF16)
    v_bf = sbt("v_bf", [128, H * 65], BF16)
    wkv_v = A3
    kv_sb = A5[:, 0:520]
    wproj_sb = A3[:, 8192:16384]
    kv2x = sbt("kv2x", [128, 1040], BF16)
    recips = sbt("recips", [128, 16])

    cosq = sbt("cosq", [128, 3 * 64])
    sinq = sbt("sinq", [128, 3 * 64])
    cosk = sbt("cosk", [128, 2 * 64])
    sink = sbt("sink", [128, 2 * 64])
    angf = sbt("angf", [128, 32])
    angr = sbt("angr", [128, 32])
    angm = sbt("angm", [128, 32])
    angk_i = sbt("angk_i", [128, 32], I32)
    angkf = sbt("angkf", [128, 32])
    ang2 = sbt("ang2", [128, 32])

    P1 = pst("P1", [128, 512])
    P2 = pst("P2", [128, 2048])
    P3 = pst("P3", [128, 1024])
    P4 = pst("P4", [128, 512])

    ld = es.enter_context(nc.semaphore("ld"))
    sd = es.enter_context(nc.semaphore("sd"))
    gd = es.enter_context(nc.semaphore("gd"))
    vx = es.enter_context(nc.semaphore("vx"))
    sx = es.enter_context(nc.semaphore("sx"))
    tx = es.enter_context(nc.semaphore("tx"))
    gx = es.enter_context(nc.semaphore("gx"))
    cc = es.enter_context(nc.semaphore("cc"))
    wr = es.enter_context(nc.semaphore("wr"))
    ws = es.enter_context(nc.semaphore("ws"))
    n = {"ld": 0, "sd": 0, "gd": 0, "vx": 0, "sx": 0, "tx": 0, "gx": 0, "cc": 0, "wr": 0, "ws": 0}
    sems = {"ld": ld, "sd": sd, "gd": gd, "vx": vx, "sx": sx, "tx": tx, "gx": gx, "cc": cc, "wr": wr, "ws": ws}

    def inc(key, inst, k=1):
        inst.then_inc(sems[key], k)
        n[key] += k
        return n[key]

    blk = es.enter_context(nc.Block())

    def V(v, inst):
        inc("vx", inst)
        v.wait_ge(vx, n["vx"])

    def SS(s, inst):
        inc("sx", inst)
        s.wait_ge(sx, n["sx"])

    @blk.sync
    def _(s):
        for a, b_ in [(c_g8, g8_d), (c_lt, lt_d), (c_ones, onesr_d), (c_riota, riota_d),
                      (c_dc, dc_d), (c_sc, sc_d), (c_member, member_d), (c_rw, rw_d),
                      (c_rb, rb_d)]:
            inc("ld", s.dma_start(out=a[:, :], in_=b_), 16)
        inc("ld", s.dma_start(out=c_bqq[:, :], in_=bqq_d), 16)
    l_consts = n["ld"]

    @blk.gpsimd
    def _(g):
        inc("gd", g.dma_start(out=c_bqkv[:, :], in_=bqkv_d), 16)
        inc("gd", g.dma_start(out=c_bproj[:, :], in_=bproj_d), 16)
        inc("gd", g.dma_start(out=c_rw_hi[:, :], in_=rw_d), 16)
        inc("gx", g.iota(angf[:, :], pattern=[[1, 32]], base=0, channel_multiplier=0,
                         allow_small_or_imprecise_dtypes=True))
    g_consts = n["gd"]
    g_iota = n["gx"]

    bstage = arenaB[:, :].bitcast(F32)
    a3kv_casts = []
    for c in range(8):
        @blk.sync
        def _(s, c=c):
            if c >= 2:
                s.wait_ge(vx, a3kv_casts[c - 2])
            inc("ld", s.dma_start(out=bstage[:, (c % 2) * 2048:(c % 2) * 2048 + 2048],
                                  in_=wqkvT_d[128 * c:128 * (c + 1), D:3 * D]), 16)

        @blk.vector
        def _(v, c=c, lm=n["ld"]):
            v.wait_ge(ld, lm)
            V(v, v.tensor_copy(A3[:, c * 2 * D:(c + 1) * 2 * D],
                               bstage[:, (c % 2) * 2048:(c % 2) * 2048 + 2048]))
        a3kv_casts.append(n["vx"])
    v_a3kv = n["vx"]

    @blk.vector
    def _(v):
        v.wait_ge(ld, l_consts)
        v.wait_ge(gd, g_consts)
        v.wait_ge(gx, g_iota)
        V(v, v.memset(c_onesb[:, :], 1.0))
        V(v, v.memset(lo_s[:, :], LO0))
        V(v, v.memset(zer64[:, :], 0.0))
        V(v, v.tensor_tensor(cj512[:, 0:8], c_rw[:, :], c_rw_hi[:, :], op=ALU.subtract))
        V(v, v.tensor_copy(c_rw_lo[:, :], cj512[:, 0:8]))
        V(v, v.tensor_scalar(c_bqqn[:, :], c_bqq[:, :], -1.0, None, op0=ALU.mult))
        V(v, v.tensor_scalar(angr[:, :], angf[:, :], -LNI_HI, None, op0=ALU.mult))
        V(v, v.tensor_scalar(angm[:, :], angf[:, :], -LNI_LO, None, op0=ALU.mult))
        V(v, v.tensor_tensor(angr[:, :], angr[:, :], angm[:, :], op=ALU.add))
    v_init = n["vx"]

    @blk.scalar
    def _(s):
        s.wait_ge(vx, v_init)
        SS(s, s.activation(inv_rep[:, :], angr[:, :], AF.Exp))
    s_inv = n["sx"]

    for src_d, dst in [(gamma_d, gam_rep), (beta_d, bet_rep), (ln1_d, ln1_rep), (ln2_d, ln2_rep)]:
        @blk.sync
        def _(s, src_d=src_d, vv=n["vx"]):
            s.wait_ge(vx, vv)
            inc("ld", s.dma_start(out=rowb[:, :], in_=src_d), 16)
        l_row = n["ld"]
        for half in range(2):
            @blk.tensor
            def _(t, l_row=l_row, half=half, vv=n["vx"]):
                t.wait_ge(ld, l_row)
                t.wait_ge(vx, max(vv, v_init))
                inc("tx", t.matmul(P1[:, :], c_ones[:, :], rowb[:, 512 * half:512 * (half + 1)],
                                   start=True, stop=True))
            t_h = n["tx"]

            @blk.vector
            def _(v, dst=dst, half=half, t_h=t_h):
                v.wait_ge(tx, t_h)
                V(v, v.tensor_copy(dst[:, 512 * half:512 * (half + 1)], P1[:, :]))
    v_reps = n["vx"]

    tile_ld = []

    @blk.sync
    def _(s):
        for t in range(2):
            inc("ld", s.dma_start(out=xt_buf[:, (t % 2) * D:(t % 2 + 1) * D],
                                  in_=x_T[128 * t:128 * (t + 1), :]), 16)
            tile_ld.append(n["ld"])

    s_tr_done = []
    rt_done = []
    for t in range(NTS):
        buf = (t % 2) * D

        @blk.vector
        def _(v, t=t, buf=buf):
            v.wait_ge(ld, tile_ld[t])
            if t == 0:
                v.wait_ge(vx, v_reps)
            xt = xt_buf[:, buf:buf + D]
            V(v, v.bn_stats(stat6[:, 0:6], (xt)[:, 0:512]))
            V(v, v.bn_stats(stat6[:, 6:12], (xt)[:, 512:1024]))
            V(v, v.bn_aggr(stat2[:, :], stat6[:, :]))
            V(v, v.tensor_scalar(negm[:, :], stat2[:, 0:1], -1.0, None, op0=ALU.mult))
        v_st = n["vx"]

        @blk.scalar
        def _(s, v_st=v_st):
            s.wait_ge(vx, v_st)
            SS(s, s.activation(sqv[:, :], stat2[:, 1:2], AF.Sqrt, bias=c_sc[:, 4:5]))
        s_sq = n["sx"]

        xnb_t = A7[:, (t % 2) * 4 * D:(t % 2) * 4 * D + D]
        lob_t = A7[:, (t % 2) * 4 * D + D:(t % 2) * 4 * D + 2 * D]

        @blk.vector
        def _(v, t=t, buf=buf, s_sq=s_sq, xnb_t=xnb_t, lob_t=lob_t):
            v.wait_ge(sx, s_sq)
            if t >= 2:
                v.wait_ge(sd, s_tr_done[t - 2])
            V(v, v.reciprocal(rstd[:, :], sqv[:, :]))
            xt = xt_buf[:, buf:buf + D]
            V(v, v.tensor_scalar(xn_f32[:, :], xt, negm[:, :], rstd[:, :], op0=ALU.add, op1=ALU.mult))
            V(v, v.tensor_copy(xnb_t[:, :], xn_f32[:, :]))
            V(v, v.tensor_tensor(lob_t[:, :], xn_f32[:, :], xnb_t[:, :], op=ALU.subtract))
        v_xn = n["vx"]

        @blk.scalar
        def _(s, t=t, v_xn=v_xn, xnb_t=xnb_t, lob_t=lob_t):
            s.wait_ge(vx, v_xn)
            if t >= 2:
                s.wait_ge(tx, rt_done[t - 2])
            dst = arenaA[:, :].rearrange("p (c k) -> p c k", k=TOK)[:, :, 128 * t:128 * (t + 1)]
            inc("sd", s.dma_start_transpose(dst, xnb_t[:, :]), 16)
            dst2 = loT_t[:, (t % 2) * D:(t % 2 + 1) * D].rearrange("p (c k) -> p c k", k=128)
            inc("sd", s.dma_start_transpose(dst2, lob_t[:, :]), 16)
        s_tr_done.append(n["sd"])

        @blk.tensor
        def _(tt, t=t, s_tr=s_tr_done[t]):
            tt.wait_ge(sd, s_tr)
            for c in range(8):
                hi = arenaA[:, :].rearrange("p (c k) -> p c k", k=TOK)[:, c, 128 * t:128 * (t + 1)]
                lo_ = loT_t[:, (t % 2) * D:(t % 2 + 1) * D].rearrange("p (c k) -> p c k", k=128)[:, c, :]
                inc("tx", tt.matmul(P2[0:1, 128 * t:128 * (t + 1)], c_rw_hi[:, c:c + 1], hi,
                                    start=(c == 0), stop=False, skip_group_check=True))
                inc("tx", tt.matmul(P2[0:1, 128 * t:128 * (t + 1)], c_rw_hi[:, c:c + 1], lo_,
                                    start=False, stop=False, skip_group_check=True))
                inc("tx", tt.matmul(P2[0:1, 128 * t:128 * (t + 1)], c_rw_lo[:, c:c + 1], hi,
                                    start=False, stop=(c == 7), skip_group_check=True))
        rt_done.append(n["tx"])

        if t + 2 < NTS:
            @blk.sync
            def _(s, t=t, v_xn=v_xn):
                s.wait_ge(vx, v_xn)
                inc("ld", s.dma_start(out=xt_buf[:, (t % 2) * D:(t % 2 + 1) * D],
                                      in_=x_T[128 * (t + 2):128 * (t + 3), :]), 16)
                tile_ld.append(n["ld"])
    t_router = n["tx"]

    @blk.vector
    def _(v):
        v.wait_ge(tx, t_router)
        V(v, v.tensor_scalar(w_sb[:, :], P2[0:1, :], c_rb[:, :], None, op0=ALU.add))
    v_w = n["vx"]

    @blk.sync
    def _(s):
        s.wait_ge(vx, v_w)
        inc("ld", s.dma_start(out=w_ag_in.ap(), in_=w_sb[:, :]), 16)
    l_wag = n["ld"]

    @blk.gpsimd
    def _(g):
        g.wait_ge(ld, l_wag)
        inc("cc", g.collective_compute("AllGather", ALU.bypass, replica_groups=GROUPS,
                                       ins=[w_ag_in.ap().opt()], outs=[w_ag_out.ap().opt()]))
    c_w = n["cc"]

    wflat = w_ag_out.ap().rearrange("a b -> (a b)").rearrange("(x m) -> x m", x=1)

    @blk.sync
    def _(s):
        s.wait_ge(cc, c_w)
        for k in range(8):
            inc("ld", s.dma_start(out=w_rep[16 * k:16 * (k + 1), :],
                                  in_=wflat.rearrange("x (a b) -> (x a) b", a=16)), 16)
        inc("ld", s.dma_start(out=w_nat[:, :], in_=wflat.rearrange("x (a b) -> (x a) b", a=128)), 16)
    l_wload = n["ld"]

    for r in range(ROUNDS):
        v_now = n["vx"]

        @blk.tensor
        def _(t, v_now=v_now, r=r):
            t.wait_ge(vx, v_now)
            if r == 0:
                t.wait_ge(ld, l_wload)
            inc("tx", t.matmul(P1[:, 0:1], c_ones[:, :], lo_s[:, :], start=True, stop=True))
        t_lo = n["tx"]

        @blk.vector
        def _(v, r=r, t_lo=t_lo):
            v.wait_ge(tx, t_lo)
            if r == 0:
                v.wait_ge(ld, l_wload)
            V(v, v.tensor_tensor(T_s[:, :], P1[:, 0:1], c_dc[:, r:r + 1], op=ALU.add))
            V(v, v.tensor_scalar(cj512[:, :], w_rep[:, :], T_s[:, :], None, op0=ALU.is_gt))
            V(v, v.tensor_reduce(cnt_s[:, :], cj512[:, :], axis=AXL.X, op=ALU.add))
        v_cnt = n["vx"]

        @blk.tensor
        def _(t, v_cnt=v_cnt):
            t.wait_ge(vx, v_cnt)
            inc("tx", t.matmul(P1[0:1, 8:16], cnt_s[:, :], c_g8[:, :], start=True, stop=True))
        t_c8 = n["tx"]

        @blk.vector
        def _(v, r=r, t_c8=t_c8):
            v.wait_ge(tx, t_c8)
            V(v, v.tensor_scalar(cj8[:, :], P1[0:1, 8:16], 1024.0, None, op0=ALU.is_ge))
            V(v, v.tensor_reduce(nge_s[:, :], cj8[:, :], axis=AXL.X, op=ALU.add))
            V(v, v.tensor_scalar(lo_s[:, :], nge_s[:, :], D0 / (NTHR ** (r + 1)), lo_s[:, :],
                                 op0=ALU.mult, op1=ALU.add))
    v_bis = n["vx"]

    @blk.tensor
    def _(t):
        t.wait_ge(vx, v_bis)
        inc("tx", t.matmul(P1[:, 0:1], c_ones[:, :], lo_s[:, :], start=True, stop=True))
    t_lof = n["tx"]

    @blk.vector
    def _(v):
        v.wait_ge(tx, t_lof)
        V(v, v.tensor_scalar(keep_s[:, :], w_nat[:, :], P1[:, 0:1], None, op0=ALU.is_gt))
        V(v, v.tensor_tensor(mv_s[:, :], w_nat[:, :], keep_s[:, :], op=ALU.mult))
        V(v, v.tensor_scalar(keepq_s[:, :], keep_s[:, :], -1e30, 1e30, op0=ALU.mult, op1=ALU.add))
        V(v, v.tensor_tensor(mv_s[:, :], mv_s[:, :], keepq_s[:, :], op=ALU.add))
        V(v, v.tensor_reduce(tot_s[:, :], mv_s[:, :], axis=AXL.X, op=ALU.min))
    v_min = n["vx"]

    @blk.sync
    def _(s):
        s.wait_ge(vx, v_min)
        inc("ld", s.dma_start(out=s_scr[:, 0:128].rearrange("x (a b) -> (x a) b", a=128), in_=tot_s[:, :]), 16)
        s.wait_ge(ld, n["ld"])
        inc("ld", s.dma_start(out=minrow[:, :], in_=s_scr[:, 0:128]), 16)
    l_minr = n["ld"]

    @blk.vector
    def _(v):
        v.wait_ge(ld, l_minr)
        V(v, v.tensor_reduce(thr_s[:, :], minrow[:, :], axis=AXL.X, op=ALU.min))
    v_thr = n["vx"]

    @blk.tensor
    def _(t):
        t.wait_ge(vx, v_thr)
        inc("tx", t.matmul(P1[:, 0:1], c_ones[:, :], thr_s[:, :], start=True, stop=True))
    t_thr = n["tx"]

    @blk.vector
    def _(v):
        v.wait_ge(tx, t_thr)
        V(v, v.tensor_scalar(keep_s[:, :], w_nat[:, :], P1[:, 0:1], None, op0=ALU.is_gt))
        V(v, v.tensor_tensor(keepq_s[:, :], keep_s[:, :], c_member[:, :].to_broadcast([128, 64]), op=ALU.mult))
        V(v, v.tensor_tensor_scan(Sp_s[:, :], keep_s[:, :], zer64[:, :], 0.0, op0=ALU.add, op1=ALU.add))
        V(v, v.tensor_tensor_scan(Sqp_s[:, :], keepq_s[:, :], zer64[:, :], 0.0, op0=ALU.add, op1=ALU.add))
        V(v, v.tensor_copy(tot_s[:, :], Sp_s[:, 63:64]))
        V(v, v.tensor_copy(totq_s[:, :], Sqp_s[:, 63:64]))
    v_scan = n["vx"]

    @blk.tensor
    def _(t):
        t.wait_ge(vx, v_scan)
        inc("tx", t.matmul(P1[:, 1:2], c_lt[:, :], tot_s[:, :], start=True, stop=True))
    t_off = n["tx"]

    @blk.vector
    def _(v):
        v.wait_ge(tx, t_off)
        V(v, v.tensor_scalar(Sf_s[:, :], Sp_s[:, :], P1[:, 1:2], None, op0=ALU.add))
    v_sf = n["vx"]

    @blk.tensor
    def _(t):
        t.wait_ge(vx, v_sf)
        inc("tx", t.matmul(P1[:, 1:2], c_lt[:, :], totq_s[:, :], start=True, stop=True))
    t_offq = n["tx"]

    @blk.vector
    def _(v):
        v.wait_ge(tx, t_offq)
        V(v, v.tensor_scalar(Sqf_s[:, :], Sqp_s[:, :], P1[:, 1:2], None, op0=ALU.add))
    v_sqf = n["vx"]

    ksel_parts = []
    for half in range(2):
        @blk.sync
        def _(s, half=half, vv=n["vx"]):
            s.wait_ge(vx, max(vv, v_sf))
            inc("ld", s.dma_start(out=s_scr.rearrange("x (a b) -> (x a) b", a=128), in_=Sf_s[:, :]), 16)
            s.wait_ge(ld, n["ld"])
            for k in range(4):
                inc("ld", s.dma_start(out=S_half[32 * k:32 * (k + 1), :],
                                      in_=s_scr[:, 4096 * half:4096 * (half + 1)].to_broadcast([32, 4096])), 16)
        l_h = n["ld"]

        @blk.vector
        def _(v, half=half, l_h=l_h):
            v.wait_ge(ld, l_h)
            for c in range(8):
                V(v, v.tensor_scalar(S_junk[:, :], S_half[:, :], c_riota[:, c:c + 1], None, op0=ALU.is_le))
                dst = kself[:, c:c + 1] if half == 0 else cnt_s[:, :]
                V(v, v.tensor_reduce(dst, S_junk[:, :], axis=AXL.X, op=ALU.add))
                if half == 1:
                    V(v, v.tensor_tensor(kself[:, c:c + 1], kself[:, c:c + 1], cnt_s[:, :], op=ALU.add))
            for c in range(2):
                V(v, v.tensor_tensor(T_s[:, :], c_riota[:, c:c + 1], c_sc[:, 0:1], op=ALU.add))
                V(v, v.tensor_scalar(S_junk[:, :], S_half[:, :], T_s[:, :], None, op0=ALU.is_le))
                dst = ksel[:, c:c + 1] if half == 0 else cnt_s[:, :]
                V(v, v.tensor_reduce(dst, S_junk[:, :], axis=AXL.X, op=ALU.add))
                if half == 1:
                    V(v, v.tensor_tensor(ksel[:, c:c + 1], ksel[:, c:c + 1], cnt_s[:, :], op=ALU.add))
    @blk.vector
    def _(v):
        V(v, v.tensor_scalar(kkeep[:, :], ksel[:, :], float(N), None, op0=ALU.is_lt))
        V(v, v.tensor_scalar(ksel[:, :], ksel[:, :], float(N - 1), None, op0=ALU.min))
        V(v, v.tensor_scalar(kkeepf[:, :], kself[:, :], float(N), None, op0=ALU.is_lt))
        V(v, v.tensor_scalar(kself[:, :], kself[:, :], float(N - 1), None, op0=ALU.min))
        V(v, v.tensor_scalar(tmpf8[:, :], kself[:, :], 4095.0, None, op0=ALU.min))
        V(v, v.tensor_tensor(tmpf8[:, :], tmpf8[:, :], kkeepf[:, :], op=ALU.mult))
        V(v, v.tensor_scalar(cj8[0:1, :], kkeepf[0:1, :], 0.0, None, op0=ALU.mult))
        V(v, v.tensor_scalar(cnt_s[:, 0:1], kkeepf[:, 0:1], 0.0, None, op0=ALU.mult))
        V(v, v.tensor_scalar(kkeepf[:, :], kkeepf[:, :], -4096.0, 4096.0, op0=ALU.mult, op1=ALU.add))
        V(v, v.tensor_tensor(tmpf8[:, :], tmpf8[:, :], kkeepf[:, :], op=ALU.add))
    v_ksel = n["vx"]

    for half in range(2):
        @blk.sync
        def _(s, half=half, vv=n["vx"]):
            s.wait_ge(vx, max(vv, v_sqf))
            inc("ld", s.dma_start(out=s_scr.rearrange("x (a b) -> (x a) b", a=128), in_=Sqf_s[:, :]), 16)
            s.wait_ge(ld, n["ld"])
            for k in range(4):
                inc("ld", s.dma_start(out=S_half[32 * k:32 * (k + 1), :],
                                      in_=s_scr[:, 4096 * half:4096 * (half + 1)].to_broadcast([32, 4096])), 16)
        l_h = n["ld"]

        @blk.vector
        def _(v, half=half, l_h=l_h):
            v.wait_ge(ld, l_h)
            for c in range(3):
                V(v, v.tensor_scalar(S_junk[:, :], S_half[:, :], c_riota[:, c:c + 1], None, op0=ALU.is_le))
                dst = qsel[:, c:c + 1] if half == 0 else cnt_s[:, :]
                V(v, v.tensor_reduce(dst, S_junk[:, :], axis=AXL.X, op=ALU.add))
                if half == 1:
                    V(v, v.tensor_tensor(qsel[:, c:c + 1], qsel[:, c:c + 1], cnt_s[:, :], op=ALU.add))
    @blk.vector
    def _(v):
        V(v, v.tensor_scalar(qkeep[:, :], qsel[:, :], float(N), None, op0=ALU.is_lt))
        V(v, v.tensor_scalar(qsel[:, :], qsel[:, :], float(N - 1), None, op0=ALU.min))
    v_qsel = n["vx"]

    @blk.gpsimd
    def _(g):
        g.wait_ge(vx, max(v_qsel, v_ksel))
        inc("gx", g.tensor_copy(qn_i[:, :], qsel[:, :]))
        inc("gx", g.tensor_copy(kn_i[:, :], ksel[:, :]))
    g_cast1 = n["gx"]

    @blk.vector
    def _(v):
        v.wait_ge(gx, g_cast1)
        V(v, v.tensor_scalar(cq_i[:, :], qn_i[:, :], 4095, None, op0=ALU.min))
        V(v, v.tensor_scalar(ck_i[:, :], kn_i[:, :], 4095, None, op0=ALU.min))
        V(v, v.tensor_scalar(tmpi[:, :], qn_i[:, :], 9, None, op0=ALU.logical_shift_right))
        V(v, v.tensor_scalar(tmpi[:, :], tmpi[:, :], 7, None, op0=ALU.logical_shift_left))
        V(v, v.tensor_scalar(qrho_i[:, :], qn_i[:, :], 511, None, op0=ALU.bitwise_and))
        V(v, v.tensor_tensor(qrho_i[:, :], qrho_i[:, :], tmpi[:, :], op=ALU.add))
    v_der1 = n["vx"]

    @blk.gpsimd
    def _(g):
        g.wait_ge(vx, v_der1)
        inc("gx", g.tensor_copy(ckf_i[:, :], tmpf8[:, :]))
        inc("gx", g.tensor_copy(tmpf2[:, :], qrho_i[:, :]))
    g_cast2 = n["gx"]

    @blk.vector
    def _(v):
        v.wait_ge(gx, g_cast2)
        V(v, v.tensor_scalar(tmpf2[:, :], tmpf2[:, :], c_sc[:, 1:2], None, op0=ALU.subtract))
        V(v, v.tensor_scalar(tmpf2[:, :], tmpf2[:, :], 0.0, 2047.0, op0=ALU.max, op1=ALU.min))
        V(v, v.tensor_scalar(tmpf2[:, :], tmpf2[:, :], float(TOK), None, op0=ALU.subtract))
        V(v, v.tensor_tensor(tmpf2[:, :], tmpf2[:, :], qkeep[:, :], op=ALU.mult))
        V(v, v.tensor_scalar(tmpf2[:, :], tmpf2[:, :], float(TOK), None, op0=ALU.add))
    v_rho = n["vx"]

    @blk.gpsimd
    def _(g):
        g.wait_ge(vx, v_rho)
        inc("gx", g.tensor_copy(qrho_i[:, :], tmpf2[:, :]))
        inc("gx", g.tensor_copy(posq_f[:, 0:1], qrho_i[:, 0:1]))
        for c in range(3):
            inc("gd", g.indirect_dma_start(wself[:, c:c + 1], None, wflat,
                                           IndirectOffsetOnAxis(ap=qn_i[:, c:c + 1], axis=1)), 16)
            inc("gd", g.indirect_dma_start(posq_i[:, c:c + 1], None, pos_f,
                                           IndirectOffsetOnAxis(ap=cq_i[:, c:c + 1], axis=1)), 16)
            inc("gd", g.indirect_dma_start(A4[:, 0:3 * D][:, c * D:(c + 1) * D], None, x_full,
                                           IndirectOffsetOnAxis(ap=qn_i[:, c:c + 1], axis=0)), 16)
        for c in range(2):
            inc("gd", g.indirect_dma_start(posk_i[:, c:c + 1], None, pos_f,
                                           IndirectOffsetOnAxis(ap=ck_i[:, c:c + 1], axis=1)), 16)
            inc("gd", g.indirect_dma_start(A4[:, 3 * D:5 * D][:, c * D:(c + 1) * D], None, x_full,
                                           IndirectOffsetOnAxis(ap=kn_i[:, c:c + 1], axis=0)), 16)
    g_gath = n["gd"]
    g_rhoi = n["gx"]

    kvts_v = []
    kvts_s = []
    for ts in range(NTS):
        @blk.tensor
        def _(t, ts=ts):
            if ts == 0:
                t.wait_ge(sd, s_tr_done[NTS - 1])
                t.wait_ge(gd, g_consts)
                t.wait_ge(vx, max(v_w, v_a3kv))
            else:
                t.wait_ge(vx, kvts_v[ts - 1])
                t.wait_ge(sx, kvts_s[ts - 1])
            for c in range(8):
                xs = arenaA[:, :].rearrange("p (c k) -> p c k", k=TOK)[:, c, 128 * ts:128 * (ts + 1)]
                for j in range(4):
                    rhs = A3[:, c * 2 * D + 512 * j: c * 2 * D + 512 * (j + 1)]
                    inc("tx", t.matmul(P2[:, 512 * j:512 * (j + 1)], xs, rhs,
                                       start=(c == 0), stop=False, skip_group_check=True))
            for j in range(4):
                inc("tx", t.matmul(P2[:, 512 * j:512 * (j + 1)], c_onesb[:, :],
                                   c_bqkv[:, D + 512 * j:D + 512 * (j + 1)], start=False, stop=True,
                                   skip_group_check=True))
        t_kvp = n["tx"]

        @blk.scalar
        def _(s, ts=ts, t_kvp=t_kvp):
            s.wait_ge(tx, t_kvp)
            if ts == 0:
                s.wait_ge(ld, l_wag)
                s.wait_ge(cc, c_w)
            SS(s, s.activation(kphi_m[:, :], P2[:, 0:D], AF.Relu, scale=-1.0))
            SS(s, s.activation(kphi_e[:, :], kphi_m[:, :], AF.Exp, scale=-1.0))
            SS(s, s.activation(kphi_m[:, :], P2[:, 0:D], AF.Relu))
            vdst = v_bf[:, :].rearrange("p (h c) -> p h c", c=65)[:, :, 0:64]
            SS(s, s.activation(vdst, P2[:, D:2 * D].rearrange("p (h c) -> p h c", c=64), AF.Copy))
        kvts_s.append(n["sx"])

        @blk.vector
        def _(v, ts=ts, s_now=kvts_s[ts]):
            v.wait_ge(sx, s_now)
            V(v, v.tensor_tensor(k_bf[:, :], kphi_m[:, :], kphi_e[:, :], op=ALU.add))
            ones_cols = v_bf[:, :].rearrange("p (h c) -> p h c", c=65)[:, :, 64:65]
            V(v, v.memset(ones_cols, 1.0))
        kvts_v.append(n["vx"])

        @blk.tensor
        def _(t, ts=ts, v_now=kvts_v[ts]):
            t.wait_ge(vx, v_now)
            for h in range(H):
                pair = h // 2
                odd = h % 2
                coff = (512 if pair >= 4 else 0) + (pair % 4) * 65
                lhs = k_bf[:, 64 * h:64 * (h + 1)]
                rhs = v_bf[:, 65 * h:65 * (h + 1)]
                tp = (0, 64 * odd) if odd else None
                inc("tx", t.matmul(P3[64 * odd:64 * (odd + 1), coff:coff + 65], lhs, rhs,
                                   start=(ts == 0 and pair % 4 == 0), stop=(ts == NTS - 1),
                                   tile_position=tp, skip_group_check=True))
    t_kvall = n["tx"]

    @blk.vector
    def _(v):
        v.wait_ge(tx, t_kvall)
        V(v, v.tensor_copy(kv_sb[:, 0:260], P3[:, 0:260]))
        V(v, v.tensor_copy(kv_sb[:, 260:520], P3[:, 512:772]))
    v_kvev = n["vx"]

    @blk.sync
    def _(s):
        s.wait_ge(vx, v_kvev)
        inc("ld", s.dma_start(out=kv_ar_in.ap(), in_=kv_sb[:, :]), 16)
    l_kvar = n["ld"]

    @blk.gpsimd
    def _(g):
        g.wait_ge(ld, l_kvar)
        inc("cc", g.collective_compute("AllReduce", ALU.add, replica_groups=GROUPS,
                                       ins=[kv_ar_in.ap().opt()], outs=[kv_ar_out.ap().opt()]))
    c_kv = n["cc"]

    @blk.vector
    def _(v):
        V(v, v.memset(kv2x[64:128, 0:520], 0.0))
        V(v, v.memset(kv2x[0:64, 520:1040], 0.0))
    v_kv2z = n["vx"]

    @blk.gpsimd
    def _(g):
        g.wait_ge(cc, c_kv)
        g.wait_ge(vx, v_kv2z)
        inc("gd", g.dma_start(out=kv2x[0:64, 0:520], in_=kv_ar_out.ap()[0:64, :]), 16)
        inc("gd", g.dma_start(out=kv2x[64:128, 520:1040], in_=kv_ar_out.ap()[64:128, :]), 16)
    g_kv2x = n["gd"]

    qcol_casts = []
    for c in range(8):
        @blk.sync
        def _(s, c=c):
            if c < 2:
                s.wait_ge(tx, t_kvall)
            if c >= 2:
                s.wait_ge(vx, qcol_casts[c - 2])
            inc("ld", s.dma_start(out=bstage[:, (c % 2) * 1024:(c % 2) * 1024 + 1024],
                                  in_=wqkvT_d[128 * c:128 * (c + 1), 0:D]), 16)

        @blk.vector
        def _(v, c=c, lm=n["ld"]):
            v.wait_ge(ld, lm)
            if c < 2:
                v.wait_ge(tx, t_kvall)
            V(v, v.tensor_copy(A3[:, c * D:(c + 1) * D],
                               bstage[:, (c % 2) * 1024:(c % 2) * 1024 + 1024]))
        qcol_casts.append(n["vx"])
    g_wq = n["vx"]

    q_ec_v = []
    for ec in range(8):
        @blk.tensor
        def _(t, ec=ec):
            if ec == 0:
                t.wait_ge(tx, t_kvall)
                t.wait_ge(vx, g_wq)
                t.wait_ge(sx, kvts_s[NTS - 1])
            else:
                t.wait_ge(vx, q_ec_v[ec - 1])
            for c in range(8):
                lhs = A3[:, c * D + 128 * ec: c * D + 128 * (ec + 1)]
                for j in range(4):
                    rhs = arenaA[:, :].rearrange("p (c k) -> p c k", k=TOK)[:, c, 512 * j:512 * (j + 1)]
                    inc("tx", t.matmul(P2[:, 512 * j:512 * (j + 1)], lhs, rhs,
                                       start=(c == 0), stop=(c == 7), skip_group_check=True))
        t_qp = n["tx"]

        for j in range(4):
            @blk.scalar
            def _(s, ec=ec, j=j, t_qp=t_qp):
                s.wait_ge(tx, t_qp)
                if ec == 0 and j == 0:
                    s.wait_ge(ld, l_kvar)
                psl = P2[:, 512 * j:512 * (j + 1)]
                SS(s, s.activation(qphi_m[:, :], psl, AF.Relu, scale=-1.0, bias=c_bqqn[:, ec:ec + 1]))
                SS(s, s.activation(qphi_e[:, :], qphi_m[:, :], AF.Exp, scale=-1.0))
                SS(s, s.activation(qphi_t[:, :], psl, AF.Relu, bias=c_bqq[:, ec:ec + 1]))
            s_qj = n["sx"]

            @blk.vector
            def _(v, ec=ec, j=j, s_qj=s_qj):
                v.wait_ge(sx, s_qj)
                dst = arenaB[:, ec * TOK + 512 * j: ec * TOK + 512 * (j + 1)]
                V(v, v.tensor_tensor(dst, qphi_t[:, :], qphi_e[:, :], op=ALU.add))
        q_ec_v.append(n["vx"])
    t_qall = n["tx"]
    v_qall = n["vx"]

    qo_v = []
    scr_l = []
    for ts in range(NTS):
        Gb = A7[:, (ts % 2) * D:(ts % 2) * D + D]

        @blk.tensor
        def _(t, ts=ts):
            if ts == 0:
                t.wait_ge(vx, v_qall)
                t.wait_ge(gd, g_kv2x)
            else:
                t.wait_ge(vx, qo_v[ts - 1])
            for h in range(H):
                ec = h // 2
                odd = h % 2
                p0 = h // 2
                c0 = 520 * odd + (260 if (p0 >= 4) else 0) + (p0 % 4) * 65
                lhs = arenaB[:, ec * TOK + 128 * ts: ec * TOK + 128 * (ts + 1)]
                rhs = kv2x[:, c0:c0 + 65]
                g4 = h // 4
                off = 512 * g4 + 65 * (h % 4)
                inc("tx", t.matmul(P2[:, off:off + 65], lhs, rhs, start=True, stop=True,
                                   skip_group_check=True))
        t_qo = n["tx"]

        @blk.vector
        def _(v, ts=ts, t_qo=t_qo, Gb=Gb):
            v.wait_ge(tx, t_qo)
            if ts >= 2:
                v.wait_ge(ld, scr_l[ts - 2])
            for g4 in range(4):
                qk1c = P2[:, 512 * g4:512 * g4 + 260].rearrange("p (s c) -> p s c", c=65)[:, :, 64:65]
                V(v, v.tensor_scalar(recips[:, 4 * g4:4 * g4 + 4],
                                     qk1c.rearrange("p s c -> p (s c)"), 1e-6, None, op0=ALU.max))
            V(v, v.reciprocal(recips[:, :], recips[:, :]))
            for g4 in range(4):
                src = P2[:, :].rearrange("p (g q) -> p g q", q=512)[:, g4, 0:260]
                src = src.rearrange("p (s c) -> p s c", c=65)[:, :, 0:64]
                dst = Gb[:, 256 * g4:256 * (g4 + 1)].rearrange("p (s e) -> p s e", e=64)
                rb_ = recips[:, 4 * g4:4 * g4 + 4].rearrange("p (s o) -> p s o", o=1).to_broadcast([128, 4, 64])
                V(v, v.tensor_tensor(dst, src, rb_, op=ALU.mult))
        qo_v.append(n["vx"])

        @blk.sync
        def _(s, ts=ts, v_now=qo_v[ts], Gb=Gb):
            s.wait_ge(vx, v_now)
            for a in range(8):
                srcA = Gb[16 * a:16 * (a + 1), :].rearrange("p (h e) -> p h e", e=64)
                dstA = scr_dram.ap().rearrange("(h ra) (j e) -> h ra j e", h=16, j=16)[:, 8 * ts + a]
                dstA = dstA.transpose([1, 0, 2])
                inc("ld", s.dma_start(out=dstA, in_=srcA, allow_slow_non_contiguous=True), 16)
        scr_l.append(n["ld"])
    l_scr = n["ld"]
    t_qoall = n["tx"]

    @blk.scalar
    def _(s):
        s.wait_ge(ld, l_scr)
        s.wait_ge(tx, t_qoall)
        for c in range(8):
            inc("sd", s.dma_start_transpose(arenaB[:, c * TOK:(c + 1) * TOK],
                                            scr_dram.ap()[:, 128 * c:128 * (c + 1)]), 16)
    s_scrT = n["sd"]

    astage = arenaA[:, :].bitcast(F32)
    wproj_casts = []
    for c in range(8):
        @blk.sync
        def _(s, c=c):
            if c < 2:
                s.wait_ge(tx, t_qall)
            if c >= 2:
                s.wait_ge(vx, wproj_casts[c - 2])
            inc("ld", s.dma_start(out=astage[:, (c % 2) * 1024:(c % 2) * 1024 + 1024],
                                  in_=wprojT_d[128 * c:128 * (c + 1), :]), 16)

        @blk.vector
        def _(v, c=c, lm=n["ld"]):
            v.wait_ge(ld, lm)
            if c < 2:
                v.wait_ge(tx, t_kvall)
            V(v, v.tensor_copy(wproj_sb[:, c * D:(c + 1) * D],
                               astage[:, (c % 2) * 1024:(c % 2) * 1024 + 1024]))
        wproj_casts.append(n["vx"])
    g_wproj = n["vx"]

    xs_ld = []
    wr_marks = []
    @blk.sync
    def _(s):
        for t in range(2):
            inc("ld", s.dma_start(out=xt_buf[:, (t % 2) * D:(t % 2 + 1) * D],
                                  in_=x_S[128 * t:128 * (t + 1), :]), 16)
            xs_ld.append(n["ld"])

    pr_v = []
    for rs in range(NTS):
        pofs = (rs % 2) * 1024

        @blk.tensor
        def _(t, rs=rs, pofs=pofs):
            if rs == 0:
                t.wait_ge(sd, s_scrT)
                t.wait_ge(vx, g_wproj)
            if rs >= 2:
                t.wait_ge(vx, pr_v[rs - 2])
            for c in range(8):
                lhs = arenaB[:, :].rearrange("p (c k) -> p c k", k=TOK)[:, c, 128 * rs:128 * (rs + 1)]
                for j in range(2):
                    rhs = wproj_sb[:, c * D + 512 * j:c * D + 512 * (j + 1)]
                    inc("tx", t.matmul(P2[:, pofs + 512 * j:pofs + 512 * (j + 1)], lhs, rhs,
                                       start=(c == 0), stop=False, skip_group_check=True))
            for j in range(2):
                inc("tx", t.matmul(P2[:, pofs + 512 * j:pofs + 512 * (j + 1)], c_onesb[:, :],
                                   c_bproj[:, 512 * j:512 * (j + 1)], start=False, stop=True,
                                   skip_group_check=True))
        t_pr = n["tx"]

        @blk.vector
        def _(v, rs=rs, pofs=pofs, t_pr=t_pr):
            v.wait_ge(tx, t_pr)
            v.wait_ge(ld, xs_ld[rs])
            psl = P2[:, pofs:pofs + 1024]
            V(v, v.bn_stats(stat6[:, 0:6], (psl)[:, 0:512]))
            V(v, v.bn_stats(stat6[:, 6:12], (psl)[:, 512:1024]))
            V(v, v.bn_aggr(stat2[:, :], stat6[:, :]))
            V(v, v.tensor_scalar(negm[:, :], stat2[:, 0:1], -1.0, None, op0=ALU.mult))
        v_s1 = n["vx"]

        @blk.scalar
        def _(s, v_s1=v_s1):
            s.wait_ge(vx, v_s1)
            SS(s, s.activation(sqv[:, :], stat2[:, 1:2], AF.Sqrt, bias=c_sc[:, 4:5]))
        s_s1 = n["sx"]

        @blk.vector
        def _(v, rs=rs, pofs=pofs, s_s1=s_s1):
            v.wait_ge(sx, s_s1)
            xt = xt_buf[:, (rs % 2) * D:(rs % 2 + 1) * D]
            psl = P2[:, pofs:pofs + 1024]
            V(v, v.reciprocal(rstd[:, :], sqv[:, :]))
            V(v, v.tensor_scalar(u_t[:, :], psl, negm[:, :], rstd[:, :], op0=ALU.add, op1=ALU.mult))
            V(v, v.bn_stats(stat6[:, 0:6], (xt)[:, 0:512]))
            V(v, v.bn_stats(stat6[:, 6:12], (xt)[:, 512:1024]))
            V(v, v.bn_aggr(stat2[:, :], stat6[:, :]))
            V(v, v.tensor_scalar(negm[:, :], stat2[:, 0:1], -1.0, None, op0=ALU.mult))
        v_s2 = n["vx"]

        @blk.scalar
        def _(s, v_s2=v_s2):
            s.wait_ge(vx, v_s2)
            SS(s, s.activation(sqv[:, :], stat2[:, 1:2], AF.Sqrt, bias=c_sc[:, 4:5]))
        s_s2 = n["sx"]

        orow = acc1 if rs % 2 == 0 else acc2

        @blk.vector
        def _(v, rs=rs, pofs=pofs, s_s2=s_s2, orow=orow):
            v.wait_ge(sx, s_s2)
            if rs >= 2:
                v.wait_ge(wr, wr_marks[rs - 2])
            xt = xt_buf[:, (rs % 2) * D:(rs % 2 + 1) * D]
            V(v, v.reciprocal(rstd[:, :], sqv[:, :]))
            V(v, v.tensor_scalar(rstd[:, :], rstd[:, :], 2.0, None, op0=ALU.mult))
            V(v, v.tensor_scalar(xns2[:, :], xt, negm[:, :], rstd[:, :], op0=ALU.add, op1=ALU.mult))
            V(v, v.tensor_tensor(xns2[:, :], xns2[:, :], xt, op=ALU.add))
            V(v, v.tensor_tensor(xns2[:, :], xns2[:, :], bet_rep[:, :], op=ALU.add))
            V(v, v.tensor_tensor(orow[:, :], u_t[:, :], gam_rep[:, :], op=ALU.mult))
            V(v, v.tensor_tensor(orow[:, :], orow[:, :], xns2[:, :], op=ALU.add))
        pr_v.append(n["vx"])

        @blk.sync
        def _(s, rs=rs, v_now=pr_v[rs], orow=orow):
            s.wait_ge(vx, v_now)
            inc("wr", s.dma_start(out=out_d[128 * rs:128 * (rs + 1), :], in_=orow[:, :]), 16)
            wr_marks.append(n["wr"])
            if rs + 2 < NTS:
                inc("ld", s.dma_start(out=xt_buf[:, (rs % 2) * D:(rs % 2 + 1) * D],
                                      in_=x_S[128 * (rs + 2):128 * (rs + 3), :]), 16)
                xs_ld.append(n["ld"])
    w_base = n["wr"]


    @blk.gpsimd
    def _(g):
        g.wait_ge(gd, g_gath)
        inc("gx", g.tensor_copy(posq_f[:, :], posq_i[:, :]))
        inc("gx", g.tensor_copy(posk_f[:, :], posk_i[:, :]))
    g_posf = n["gx"]

    def rope_tables(tag, posf, col, cosT, sinT, scale):
        gprev0 = n["gx"]
        sprev0 = n["sx"]
        @blk.vector
        def _(v, posf=posf, col=col, gprev0=gprev0):
            v.wait_ge(gx, max(gprev0, g_posf))
            v.wait_ge(sx, s_inv)
            V(v, v.tensor_scalar(angf[:, :], inv_rep[:, :], posf[:, col:col + 1], None, op0=ALU.mult))
            V(v, v.tensor_scalar(angm[:, :], angf[:, :], 1.0 / TP, None, op0=ALU.mult))
        v_a = n["vx"]

        @blk.gpsimd
        def _(g, v_a=v_a):
            g.wait_ge(vx, v_a)
            inc("gx", g.tensor_copy(angk_i[:, :], angm[:, :]))
            inc("gx", g.tensor_copy(angkf[:, :], angk_i[:, :]))
        g_a = n["gx"]

        @blk.vector
        def _(v, g_a=g_a, sprev0=sprev0):
            v.wait_ge(gx, g_a)
            v.wait_ge(sx, sprev0)
            V(v, v.tensor_scalar(ang2[:, :], angkf[:, :], -C1, None, op0=ALU.mult))
            V(v, v.tensor_tensor(angr[:, :], ang2[:, :], angf[:, :], op=ALU.add))
            V(v, v.tensor_scalar(ang2[:, :], angkf[:, :], -C2, None, op0=ALU.mult))
            V(v, v.tensor_tensor(angr[:, :], ang2[:, :], angr[:, :], op=ALU.add))
            V(v, v.tensor_scalar(ang2[:, :], angkf[:, :], -C3, None, op0=ALU.mult))
            V(v, v.tensor_tensor(angr[:, :], ang2[:, :], angr[:, :], op=ALU.add))
            V(v, v.tensor_scalar(angm[:, :], angr[:, :], PI, None, op0=ALU.is_gt))
            V(v, v.tensor_scalar(angm[:, :], angm[:, :], -TP, None, op0=ALU.mult))
            V(v, v.tensor_tensor(ang2[:, :], angm[:, :], angr[:, :], op=ALU.add))
        v_b = n["vx"]

        @blk.scalar
        def _(s, cosT=cosT, col=col, v_b=v_b):
            s.wait_ge(vx, v_b)
            SS(s, s.activation(sinT[:, 64 * col:64 * col + 32], ang2[:, :], AF.Sin))
            SS(s, s.activation(sinT[:, 64 * col + 32:64 * col + 64], ang2[:, :], AF.Sin))
        s_b = n["sx"]

        @blk.vector
        def _(v, v_b=v_b, s_b=s_b):
            v.wait_ge(sx, s_b)
            V(v, v.tensor_scalar(ang2[:, :], angr[:, :], PI / 2, None, op0=ALU.add))
            V(v, v.tensor_scalar(angm[:, :], ang2[:, :], PI, None, op0=ALU.is_gt))
            V(v, v.tensor_scalar(angm[:, :], angm[:, :], -TP, None, op0=ALU.mult))
            V(v, v.tensor_tensor(ang2[:, :], angm[:, :], ang2[:, :], op=ALU.add))
        v_c = n["vx"]

        @blk.scalar
        def _(s, cosT=cosT, col=col, v_c=v_c):
            s.wait_ge(vx, v_c)
            SS(s, s.activation(cosT[:, 64 * col:64 * col + 32], ang2[:, :], AF.Sin))
            SS(s, s.activation(cosT[:, 64 * col + 32:64 * col + 64], ang2[:, :], AF.Sin))
        s_c = n["sx"]

        if scale != 1.0:
            @blk.vector
            def _(v, cosT=cosT, col=col, s_c=s_c, scale=scale):
                v.wait_ge(sx, s_c)
                V(v, v.tensor_scalar(cosT[:, 64 * col:64 * (col + 1)], cosT[:, 64 * col:64 * (col + 1)],
                                     scale, None, op0=ALU.mult))
                V(v, v.tensor_scalar(sinT[:, 64 * col:64 * (col + 1)], sinT[:, 64 * col:64 * (col + 1)],
                                     scale, None, op0=ALU.mult))
        return n["vx"], n["sx"]

    for col in range(3):
        rope_tables("q", posq_f, col, cosq, sinq, 0.125)
    for col in range(2):
        rope_tables("k", posk_f, col, cosk, sink, 1.0)
    v_rope_t = n["vx"]
    s_rope_t = n["sx"]

    def ln_rms(xgv, hnv, ncols, lnrep):
        marks = []
        for c in range(ncols):
            @blk.vector
            def _(v, c=c, xgv=xgv):
                v.wait_ge(gd, g_gath)
                v.wait_ge(tx, t_qoall)
                v.wait_ge(ld, l_scr)
                v.wait_ge(wr, w_base)
                xt = xgv[:, c * D:(c + 1) * D]
                V(v, v.bn_stats(stat6[:, 0:6], (xt)[:, 0:512]))
                V(v, v.bn_stats(stat6[:, 6:12], (xt)[:, 512:1024]))
                V(v, v.bn_aggr(stat2[:, :], stat6[:, :]))
                V(v, v.tensor_scalar(negm[:, :], stat2[:, 0:1], -1.0, None, op0=ALU.mult))
            v_1 = n["vx"]

            @blk.scalar
            def _(s, v_1=v_1):
                s.wait_ge(vx, v_1)
                SS(s, s.activation(sqv[:, :], stat2[:, 1:2], AF.Sqrt, bias=c_sc[:, 4:5]))
            s_1 = n["sx"]

            @blk.vector
            def _(v, c=c, xgv=xgv, s_1=s_1):
                v.wait_ge(sx, s_1)
                xt = xgv[:, c * D:(c + 1) * D]
                V(v, v.reciprocal(rstd[:, :], sqv[:, :]))
                V(v, v.tensor_scalar(xt, xt, negm[:, :], rstd[:, :], op0=ALU.add, op1=ALU.mult))
                V(v, v.bn_stats(stat6[:, 0:6], (xt)[:, 0:512]))
                V(v, v.bn_stats(stat6[:, 6:12], (xt)[:, 512:1024]))
                V(v, v.bn_aggr(stat2[:, :], stat6[:, :]))
                V(v, v.tensor_tensor(negm[:, :], stat2[:, 0:1], stat2[:, 0:1], op=ALU.mult))
                V(v, v.tensor_tensor(negm[:, :], negm[:, :], stat2[:, 1:2], op=ALU.add))
            v_2 = n["vx"]

            @blk.scalar
            def _(s, v_2=v_2):
                s.wait_ge(vx, v_2)
                SS(s, s.activation(sqv[:, :], negm[:, :], AF.Sqrt, bias=c_sc[:, 5:6]))
            s_2 = n["sx"]

            @blk.vector
            def _(v, c=c, xgv=xgv, hnv=hnv, lnrep=lnrep, s_2=s_2):
                v.wait_ge(sx, s_2)
                xt = xgv[:, c * D:(c + 1) * D]
                V(v, v.reciprocal(rstd[:, :], sqv[:, :]))
                V(v, v.tensor_scalar(rop_a[:, :], xt, rstd[:, :], None, op0=ALU.mult))
                V(v, v.tensor_tensor(hnv[:, c * D:(c + 1) * D], rop_a[:, :], lnrep[:, :], op=ALU.mult))
            marks.append(n["vx"])
        return marks

    hnq_marks = ln_rms(xnq_v, hn_q, 3, ln1_rep)
    hnk_marks = ln_rms(xnk_v, hn_k, 2, ln1_rep)
    v_hn = n["vx"]

    @blk.sync
    def _(s):
        s.wait_ge(vx, v_hn)
        for c in range(3):
            inc("ld", s.dma_start(out=t_scr.ap()[128 * c:128 * (c + 1), :], in_=hn_q[:, c * D:(c + 1) * D]), 16)
        for c in range(2):
            inc("ld", s.dma_start(out=tk_scr.ap()[128 * c:128 * (c + 1), :], in_=hn_k[:, c * D:(c + 1) * D]), 16)
    l_hnw = n["ld"]

    @blk.scalar
    def _(s):
        s.wait_ge(ld, l_hnw)
        s.wait_ge(tx, n["tx"])
        for c in range(8):
            inc("sd", s.dma_start_transpose(hnTq_v[:, c * QCAP:(c + 1) * QCAP],
                                            t_scr.ap()[:, 128 * c:128 * (c + 1)]), 16)
            inc("sd", s.dma_start_transpose(hnTk_v[:, c * KSL:(c + 1) * KSL],
                                            tk_scr.ap()[:, 128 * c:128 * (c + 1)]), 16)
    s_hnT = n["sd"]

    sqstage = A3[:, 0:8192].bitcast(F32)
    sq_prev = [None]

    def load_sq_weight(wd, marker):
        casts = []
        for c in range(8):
            @blk.sync
            def _(s, wd=wd, c=c, marker=marker, prev=sq_prev[0]):
                if c == 0:
                    if marker is not None:
                        s.wait_ge(tx, marker)
                    if prev is not None:
                        s.wait_ge(vx, prev)
                if c >= 2:
                    s.wait_ge(vx, casts[c - 2])
                inc("ld", s.dma_start(out=sqstage[:, (c % 2) * 2048:(c % 2) * 2048 + 1024],
                                      in_=wd[128 * c:128 * (c + 1), :]), 16)

            @blk.vector
            def _(v, c=c, lm=n["ld"], marker=marker):
                v.wait_ge(ld, lm)
                if c == 0 and marker is not None:
                    v.wait_ge(tx, marker)
                V(v, v.tensor_copy(wproj_sb[:, c * D:(c + 1) * D],
                                   sqstage[:, (c % 2) * 2048:(c % 2) * 2048 + 1024]))
            casts.append(n["vx"])
        sq_prev[0] = n["vx"]
        return n["vx"]

    g_wqld = load_sq_weight(wq_d, n["tx"])

    qproj_v = []
    for sub in range(3):
        @blk.tensor
        def _(t, sub=sub):
            t.wait_ge(sd, s_hnT)
            t.wait_ge(vx, g_wqld)
            if sub >= 2:
                t.wait_ge(vx, qproj_v[sub - 2])
            pofs = (sub % 2) * 1024
            for c in range(8):
                lhs = hnTq_v[:, c * QCAP + 128 * sub: c * QCAP + 128 * (sub + 1)]
                for j in range(2):
                    inc("tx", t.matmul(P2[:, pofs + 512 * j:pofs + 512 * (j + 1)], lhs,
                                       wproj_sb[:, c * D + 512 * j:c * D + 512 * (j + 1)],
                                       start=(c == 0), stop=(c == 7), skip_group_check=True))
        t_q1 = n["tx"]

        @blk.vector
        def _(v, sub=sub, t_q1=t_q1):
            v.wait_ge(tx, t_q1)
            pofs = (sub % 2) * 1024
            V(v, v.tensor_copy(qp_v[:, sub * D:(sub + 1) * D], P2[:, pofs:pofs + 1024]))
        qproj_v.append(n["vx"])
    t_qproj = n["tx"]

    rope_q_v = []
    for sub in range(3):
        @blk.vector
        def _(v, sub=sub):
            v.wait_ge(vx, max(qproj_v[sub], v_rope_t))
            qt = qp_v[:, sub * D:(sub + 1) * D]
            ct = cosq[:, 64 * sub:64 * (sub + 1)]
            st_ = sinq[:, 64 * sub:64 * (sub + 1)]
            ctb = ct.rearrange("p (o e) -> p o e", o=1).to_broadcast([128, H, 64])
            V(v, v.tensor_tensor(rop_a[:, :].rearrange("p (h e) -> p h e", e=64),
                                 qt.rearrange("p (h e) -> p h e", e=64), ctb, op=ALU.mult))
            q4 = qt.rearrange("p (h a e) -> p h a e", a=2, e=32)
            a4 = rop_a[:, :].rearrange("p (h a e) -> p h a e", a=2, e=32)
            o4 = qr_v[:, sub * D:(sub + 1) * D].rearrange("p (h a e) -> p h a e", a=2, e=32)
            slb = st_[:, 0:32].rearrange("p (o e) -> p o e", o=1).to_broadcast([128, H, 32])
            shb = st_[:, 32:64].rearrange("p (o e) -> p o e", o=1).to_broadcast([128, H, 32])
            V(v, v.tensor_tensor(ang2[:, :].rearrange("p (o e) -> p o e", o=1)[:, 0:1, 0:32], angf[:, 0:1].rearrange("p (o e) -> p o e", o=1)[:, :, 0:1].to_broadcast([128, 1, 32]), angf[:, 0:1].rearrange("p (o e) -> p o e", o=1)[:, :, 0:1].to_broadcast([128, 1, 32]), op=ALU.mult)) if False else None
            V(v, v.tensor_tensor(xns2[:, 0:512].rearrange("p (h e) -> p h e", e=32),
                                 q4[:, :, 1, :], slb, op=ALU.mult))
            V(v, v.tensor_tensor(o4[:, :, 0, :], a4[:, :, 0, :],
                                 xns2[:, 0:512].rearrange("p (h e) -> p h e", e=32), op=ALU.subtract))
            V(v, v.tensor_tensor(xns2[:, 0:512].rearrange("p (h e) -> p h e", e=32),
                                 q4[:, :, 0, :], shb, op=ALU.mult))
            V(v, v.tensor_tensor(o4[:, :, 1, :], a4[:, :, 1, :],
                                 xns2[:, 0:512].rearrange("p (h e) -> p h e", e=32), op=ALU.add))
        rope_q_v.append(n["vx"])
    v_ropeq = n["vx"]

    g_wkld = load_sq_weight(wk_d, t_qproj)
    kproj_v = []
    for sub in range(2):
        @blk.tensor
        def _(t, sub=sub):
            t.wait_ge(vx, g_wkld)
            t.wait_ge(tx, t_qproj)
            if sub >= 2:
                t.wait_ge(vx, kproj_v[sub - 2])
            pofs = (sub % 2) * 1024
            for c in range(8):
                lhs = hnTk_v[:, c * KSL + 128 * sub: c * KSL + 128 * (sub + 1)]
                for j in range(2):
                    inc("tx", t.matmul(P2[:, pofs + 512 * j:pofs + 512 * (j + 1)], lhs,
                                       wproj_sb[:, c * D + 512 * j:c * D + 512 * (j + 1)],
                                       start=(c == 0), stop=(c == 7), skip_group_check=True))
        t_k1 = n["tx"]

        @blk.vector
        def _(v, sub=sub, t_k1=t_k1):
            v.wait_ge(tx, t_k1)
            v.wait_ge(vx, v_rope_t)
            pofs = (sub % 2) * 1024
            ct = cosk[:, 64 * sub:64 * (sub + 1)]
            st_ = sink[:, 64 * sub:64 * (sub + 1)]
            ctb = ct.rearrange("p (o e) -> p o e", o=1).to_broadcast([128, H, 64])
            V(v, v.tensor_copy(rop_h1[:, :], P2[:, pofs:pofs + 1024]))
            ps4 = rop_h1[:, :]
            V(v, v.tensor_tensor(rop_a[:, :].rearrange("p (h e) -> p h e", e=64),
                                 ps4.rearrange("p (h e) -> p h e", e=64), ctb, op=ALU.mult))
            q4 = ps4.rearrange("p (h a e) -> p h a e", a=2, e=32)
            a4 = rop_a[:, :].rearrange("p (h a e) -> p h a e", a=2, e=32)
            o4 = kr_v[:, sub * D:(sub + 1) * D].rearrange("p (h a e) -> p h a e", a=2, e=32)
            slb = st_[:, 0:32].rearrange("p (o e) -> p o e", o=1).to_broadcast([128, H, 32])
            shb = st_[:, 32:64].rearrange("p (o e) -> p o e", o=1).to_broadcast([128, H, 32])
            V(v, v.tensor_tensor(xns2[:, 0:512].rearrange("p (h e) -> p h e", e=32),
                                 q4[:, :, 1, :], slb, op=ALU.mult))
            V(v, v.tensor_tensor(o4[:, :, 0, :], a4[:, :, 0, :],
                                 xns2[:, 0:512].rearrange("p (h e) -> p h e", e=32), op=ALU.subtract))
            V(v, v.tensor_tensor(xns2[:, 0:512].rearrange("p (h e) -> p h e", e=32),
                                 q4[:, :, 0, :], shb, op=ALU.mult))
            V(v, v.tensor_tensor(o4[:, :, 1, :], a4[:, :, 1, :],
                                 xns2[:, 0:512].rearrange("p (h e) -> p h e", e=32), op=ALU.add))
        kproj_v.append(n["vx"])
    t_kproj = n["tx"]
    v_kr = n["vx"]

    g_wvld = load_sq_weight(wv_d, t_kproj)
    vproj_v = []
    for sub in range(2):
        @blk.tensor
        def _(t, sub=sub):
            t.wait_ge(vx, g_wvld)
            t.wait_ge(vx, v_kr)
            pofs = (sub % 2) * 1024
            for c in range(8):
                lhs = hnTk_v[:, c * KSL + 128 * sub: c * KSL + 128 * (sub + 1)]
                for j in range(2):
                    inc("tx", t.matmul(P2[:, pofs + 512 * j:pofs + 512 * (j + 1)], lhs,
                                       wproj_sb[:, c * D + 512 * j:c * D + 512 * (j + 1)],
                                       start=(c == 0), stop=(c == 7), skip_group_check=True))
        t_v1 = n["tx"]

        @blk.vector
        def _(v, sub=sub, t_v1=t_v1):
            v.wait_ge(tx, t_v1)
            pofs = (sub % 2) * 1024
            V(v, v.tensor_copy(vp_v[:, sub * D:(sub + 1) * D], P2[:, pofs:pofs + 1024]))
        vproj_v.append(n["vx"])
    t_vproj = n["tx"]
    v_vp = n["vx"]

    @blk.sync
    def _(s):
        s.wait_ge(vx, max(v_kr, v_vp))
        for c in range(2):
            inc("ld", s.dma_start(out=kvf_ag_in.ap()[128 * c:128 * (c + 1), 0:D], in_=kr_v[:, c * D:(c + 1) * D]), 16)
            inc("ld", s.dma_start(out=kvf_ag_in.ap()[128 * c:128 * (c + 1), D:2 * D], in_=vp_v[:, c * D:(c + 1) * D]), 16)
    l_kvfin = n["ld"]

    @blk.gpsimd
    def _(g):
        g.wait_ge(ld, l_kvfin)
        inc("cc", g.collective_compute("AllGather", ALU.bypass, replica_groups=GROUPS,
                                       ins=[kvf_ag_in.ap().opt()], outs=[kvf_ag_out.ap().opt()]))
    c_kvf = n["cc"]

    m_marks = []
    for c in range(3):
        @blk.gpsimd
        def _(g, c=c, pv=n["gx"]):
            g.wait_ge(tx, t_vproj)
            g.wait_ge(wr, w_base)
            g.wait_ge(gx, g_rhoi)
            if c > 0:
                g.wait_ge(gd, m_marks[c - 1][0])
            inc("gd", g.indirect_dma_start(mga_v, None, mask_f,
                                           IndirectOffsetOnAxis(ap=cq_i[:, c:c + 1], axis=0)), 16)
        g_m1 = n["gd"]

        @blk.gpsimd
        def _(g, c=c, g_m1=g_m1):
            g.wait_ge(gd, g_m1)
            inc("gd", g.dma_start(out=m_scr1.ap()[128 * c:128 * (c + 1), :], in_=mga_v), 16)
        m_marks.append((n["gd"], 0))
    g_mrows = n["gd"]

    @blk.scalar
    def _(s):
        s.wait_ge(gd, g_mrows)
        for cc_ in range(32):
            inc("sd", s.dma_start_transpose(mtr[:, :], m_scr1.ap()[:, 128 * cc_:128 * (cc_ + 1)]), 16)
            s.wait_ge(sd, n["sd"])
            inc("sd", s.dma_start(out=m_scr2.ap()[128 * cc_:128 * (cc_ + 1), :], in_=mtr[:, :]), 16)
            s.wait_ge(sd, n["sd"])
    s_mtr = n["sd"]

    @blk.vector
    def _(v):
        v.wait_ge(sd, s_mtr)
        V(v, v.memset(mtr[0:1, :], -1e9))
    v_mneg = n["vx"]

    @blk.sync
    def _(s):
        s.wait_ge(vx, v_mneg)
        s.wait_ge(sd, s_mtr)
        inc("ld", s.dma_start(out=m_scr2.ap()[4096:4097, :], in_=mtr[0:1, :]), 16)
    l_mneg = n["ld"]

    @blk.gpsimd
    def _(g):
        g.wait_ge(ld, l_mneg)
        for c in range(8):
            inc("gd", g.indirect_dma_start(e01T_v[:, c * QCAP:(c + 1) * QCAP], None, m_scr2.ap(),
                                           IndirectOffsetOnAxis(ap=ckf_i[:, c:c + 1], axis=0)), 16)
    g_mselT = n["gd"]

    @blk.scalar
    def _(s):
        s.wait_ge(gd, g_mselT)
        SS(s, s.activation(e01T_v, e01T_v, AF.Exp))
    s_e01 = n["sx"]

    @blk.scalar
    def _(s):
        s.wait_ge(cc, c_kvf)
        s.wait_ge(tx, t_vproj)
        for c in range(8):
            inc("sd", s.dma_start_transpose(kT_v[:, c * KCAP:(c + 1) * KCAP],
                                            kvf_ag_out.ap()[:, 128 * c:128 * (c + 1)]), 16)
    s_kT = n["sd"]

    @blk.vector
    def _(v):
        v.wait_ge(tx, t_qoall)
        ones_cols = vA_v.rearrange("p (t h c) -> p t h c", h=16, c=65)[:, :, :, 64:65]
        V(v, v.memset(ones_cols, 1.0))
    v_vones = n["vx"]

    @blk.sync
    def _(s):
        s.wait_ge(cc, c_kvf)
        s.wait_ge(vx, v_vones)
        for t in range(8):
            src = kvf_ag_out.ap()[128 * t:128 * (t + 1), D:2 * D].rearrange("p (h c) -> p h c", c=64)
            dst = vA_v.rearrange("p (t h c) -> p t h c", h=16, c=65)[:, t, :, 0:64]
            inc("ld", s.dma_start(out=dst, in_=src), 16)
    g_vA = n["ld"]

    @blk.sync
    def _(s):
        s.wait_ge(vx, v_ropeq)
        for c in range(3):
            inc("ld", s.dma_start(out=t_scr.ap()[128 * c:128 * (c + 1), :], in_=qr_v[:, c * D:(c + 1) * D]), 16)
    l_qw = n["ld"]

    @blk.scalar
    def _(s):
        s.wait_ge(ld, l_qw)
        for c in range(8):
            inc("sd", s.dma_start_transpose(qT_v[:, c * QCAP:(c + 1) * QCAP],
                                            t_scr.ap()[:, 128 * c:128 * (c + 1)]), 16)
    s_qT = n["sd"]

    @blk.vector
    def _(v):
        v.wait_ge(tx, t_qoall)
        V(v, v.memset(qtz_e[64:128, :], 0.0))
        V(v, v.memset(qtz_o[0:64, :], 0.0))
    v_qtz0 = n["vx"]

    head_v = []
    head_s = []
    for h in range(H):
        ec = h // 2
        odd = h % 2
        qtz = qtz_o if odd else qtz_e

        @blk.vector
        def _(v, h=h, ec=ec, odd=odd, qtz=qtz):
            v.wait_ge(vx, v_qtz0)
            if h < 2:
                v.wait_ge(sd, s_qT)
            else:
                v.wait_ge(sx, head_s[h - 1])
            V(v, v.tensor_copy(qtz[64 * odd:64 * odd + 64, :],
                               qT_v[64 * odd:64 * odd + 64, ec * QCAP:(ec + 1) * QCAP]))
        v_qtzh = n["vx"]

        exp_marks = []
        for kt in range(8):
            @blk.tensor
            def _(t, h=h, kt=kt, ec=ec, odd=odd, qtz=qtz, v_qtzh=v_qtzh):
                if h == 0 and kt == 0:
                    t.wait_ge(sd, max(s_kT, s_qT))
                    t.wait_ge(ld, g_vA)
                    t.wait_ge(sx, s_e01)
                if kt == 0:
                    t.wait_ge(vx, v_qtzh)
                if kt >= 2:
                    t.wait_ge(sx, exp_marks[kt - 2])
                if h > 0 and kt < 2:
                    t.wait_ge(sx, head_s[h - 1])
                lhs = kT_v[:, ec * KCAP + 128 * kt: ec * KCAP + 128 * (kt + 1)]
                rhs = qtz[:, :]
                pofs = (kt % 2) * 512
                inc("tx", t.matmul(P3[:, pofs:pofs + QCAP], lhs, rhs, start=True, stop=True,
                                   skip_group_check=True))
            t_sc = n["tx"]

            @blk.scalar
            def _(s, h=h, kt=kt, t_sc=t_sc):
                s.wait_ge(tx, t_sc)
                pofs = (kt % 2) * 512
                SS(s, s.activation(expT_v[:, kt * QCAP:(kt + 1) * QCAP], P3[:, pofs:pofs + QCAP],
                                   AF.Exp, bias=c_sc[:, 6:7]))
            exp_marks.append(n["sx"])

        @blk.vector
        def _(v, h=h, em=exp_marks[7]):
            v.wait_ge(sx, em)
            V(v, v.tensor_tensor(expT_v, expT_v, e01T_v, op=ALU.mult))
        v_em = n["vx"]

        av_done = []
        for kt in range(8):
            @blk.tensor
            def _(t, h=h, kt=kt, v_em=v_em):
                t.wait_ge(vx, v_em)
                lhs = vA_v[:, kt * 1040 + 65 * h: kt * 1040 + 65 * (h + 1)]
                rhs = expT_v[:, kt * QCAP:(kt + 1) * QCAP]
                inc("tx", t.matmul(P4[0:65, 0:QCAP], lhs, rhs, start=(kt == 0), stop=(kt == 7),
                                   skip_group_check=True))
        t_av = n["tx"]

        @blk.vector
        def _(v, h=h, t_av=t_av):
            v.wait_ge(tx, t_av)
            V(v, v.reciprocal(rrec[:, :], P4[64:65, 0:QCAP]))
        v_rr = n["vx"]

        @blk.tensor
        def _(t, v_rr=v_rr):
            t.wait_ge(vx, v_rr)
            inc("tx", t.matmul(P1[0:64, 0:QCAP], c_ones[:, 0:64], rrec[:, :], start=True, stop=True))
        t_rr = n["tx"]

        @blk.vector
        def _(v, h=h, ec=ec, odd=odd, t_rr=t_rr):
            v.wait_ge(tx, t_rr)
            V(v, v.tensor_copy(rrep_sb[:, :], P1[0:64, 0:QCAP]))
            dst = oT_v[64 * odd:64 * odd + 64, ec * QCAP:(ec + 1) * QCAP]
            V(v, v.tensor_tensor(dst, P4[0:64, 0:QCAP], rrep_sb[:, :], op=ALU.mult))
        head_v.append(n["vx"])
        head_s.append(n["sx"])
    v_heads = n["vx"]

    g_wold = load_sq_weight(wo_d, n["tx"])
    wo_v = []
    for sub in range(3):
        @blk.tensor
        def _(t, sub=sub):
            t.wait_ge(vx, g_wold)
            t.wait_ge(vx, v_heads)
            if sub >= 2:
                t.wait_ge(vx, wo_v[sub - 2])
            pofs = (sub % 2) * 1024
            for c in range(8):
                lhs = oT_v[:, c * QCAP + 128 * sub: c * QCAP + 128 * (sub + 1)]
                for j in range(2):
                    inc("tx", t.matmul(P2[:, pofs + 512 * j:pofs + 512 * (j + 1)], lhs,
                                       wproj_sb[:, c * D + 512 * j:c * D + 512 * (j + 1)],
                                       start=(c == 0), stop=(c == 7), skip_group_check=True))
        t_wo = n["tx"]

        @blk.vector
        def _(v, sub=sub, t_wo=t_wo):
            v.wait_ge(tx, t_wo)
            pofs = (sub % 2) * 1024
            V(v, v.tensor_tensor(x1_v[:, sub * D:(sub + 1) * D], P2[:, pofs:pofs + 1024],
                                 xnq_v[:, sub * D:(sub + 1) * D], op=ALU.add))
        wo_v.append(n["vx"])
    t_woall = n["tx"]

    h1_marks = []
    for c in range(3):
        @blk.vector
        def _(v, c=c):
            v.wait_ge(vx, wo_v[c])
            xt = x1_v[:, c * D:(c + 1) * D]
            V(v, v.bn_stats(stat6[:, 0:6], (xt)[:, 0:512]))
            V(v, v.bn_stats(stat6[:, 6:12], (xt)[:, 512:1024]))
            V(v, v.bn_aggr(stat2[:, :], stat6[:, :]))
            V(v, v.tensor_tensor(negm[:, :], stat2[:, 0:1], stat2[:, 0:1], op=ALU.mult))
            V(v, v.tensor_tensor(negm[:, :], negm[:, :], stat2[:, 1:2], op=ALU.add))
        v_1 = n["vx"]

        @blk.scalar
        def _(s, v_1=v_1):
            s.wait_ge(vx, v_1)
            SS(s, s.activation(sqv[:, :], negm[:, :], AF.Sqrt, bias=c_sc[:, 5:6]))
        s_1 = n["sx"]

        @blk.vector
        def _(v, c=c, s_1=s_1):
            v.wait_ge(sx, s_1)
            xt = x1_v[:, c * D:(c + 1) * D]
            V(v, v.reciprocal(rstd[:, :], sqv[:, :]))
            V(v, v.tensor_scalar(rop_h1[:, :], xt, rstd[:, :], None, op0=ALU.mult))
            V(v, v.tensor_tensor(h1b_v[:, c * D:(c + 1) * D], rop_h1[:, :], ln2_rep[:, :], op=ALU.mult))
        h1_marks.append(n["vx"])
    v_h1 = n["vx"]

    @blk.sync
    def _(s):
        s.wait_ge(vx, v_h1)
        for c in range(3):
            inc("ld", s.dma_start(out=t_scr.ap()[128 * c:128 * (c + 1), :], in_=h1b_v[:, c * D:(c + 1) * D]), 16)
    l_h1w = n["ld"]

    @blk.scalar
    def _(s):
        s.wait_ge(ld, l_h1w)
        for c in range(8):
            inc("sd", s.dma_start_transpose(h1T_v[:, c * QCAP:(c + 1) * QCAP],
                                            t_scr.ap()[:, 128 * c:128 * (c + 1)]), 16)
    s_h1T = n["sd"]

    w1stage = astage
    w1cast = []
    w1_g = []
    gelu_marks = []
    for fb in range(8):
        bofs = (fb % 2) * 4096

        @blk.sync
        def _(s, fb=fb):
            if fb < 2:
                s.wait_ge(tx, t_woall)
                s.wait_ge(vx, v_heads)
            if fb >= 2:
                s.wait_ge(vx, w1cast[fb - 2])
            for c in range(8):
                inc("ld", s.dma_start(out=w1stage[:, (fb % 2) * 4096 + 512 * c:
                                                  (fb % 2) * 4096 + 512 * (c + 1)],
                                      in_=w1_d[128 * c:128 * (c + 1), 512 * fb:512 * (fb + 1)]), 16)

        @blk.vector
        def _(v, fb=fb, bofs=bofs, lm=n["ld"]):
            v.wait_ge(ld, lm)
            if fb < 2:
                v.wait_ge(tx, t_woall)
            if fb >= 2:
                v.wait_ge(tx, w1_g[4 * (fb - 2) + 3])
            V(v, v.tensor_copy(w1b_v[:, bofs:bofs + 4096],
                               w1stage[:, (fb % 2) * 4096:(fb % 2) * 4096 + 4096]))
        w1cast.append(n["vx"])
        g_w1 = n["vx"]

        for fc in range(4):
            @blk.tensor
            def _(t, fb=fb, fc=fc, bofs=bofs, g_w1=g_w1):
                t.wait_ge(vx, g_w1)
                t.wait_ge(sd, s_h1T)
                if len(gelu_marks) >= 4:
                    t.wait_ge(sx, gelu_marks[-4])
                pofs = 512 * ((fb * 4 + fc) % 4)
                for c in range(8):
                    lhs = w1b_v[:, bofs + 512 * c + 128 * fc: bofs + 512 * c + 128 * (fc + 1)]
                    inc("tx", t.matmul(P2[:, pofs:pofs + QCAP], lhs, h1T_v[:, c * QCAP:(c + 1) * QCAP],
                                       start=(c == 0), stop=(c == 7), skip_group_check=True))
            t_g = n["tx"]
            w1_g.append(t_g)

            @blk.scalar
            def _(s, fb=fb, fc=fc, t_g=t_g):
                s.wait_ge(tx, t_g)
                pofs = 512 * ((fb * 4 + fc) % 4)
                fcg = fb * 4 + fc
                SS(s, s.activation(gT_v[:, fcg * QCAP:(fcg + 1) * QCAP], P2[:, pofs:pofs + QCAP],
                                   AF.Gelu_apprx_tanh))
            gelu_marks.append(n["sx"])
    s_gelu = n["sx"]
    t_w1 = n["tx"]

    w2cast = []
    w2_g = []
    for fc in range(32):
        @blk.sync
        def _(s, fc=fc):
            if fc < 2:
                s.wait_ge(vx, w1cast[-1])
            if fc >= 2:
                s.wait_ge(vx, w2cast[fc - 2])
            inc("ld", s.dma_start(out=w1stage[:, (fc % 2) * 1024:(fc % 2) * 1024 + 1024],
                                  in_=w2_d[128 * fc:128 * (fc + 1), :]), 16)

        @blk.vector
        def _(v, fc=fc, lm=n["ld"]):
            v.wait_ge(ld, lm)
            if fc < 2:
                v.wait_ge(tx, t_w1)
            if fc >= 2:
                v.wait_ge(tx, w2_g[fc - 2])
            bofs = (fc % 2) * 1024
            V(v, v.tensor_copy(w2b_v[:, bofs:bofs + 1024],
                               w1stage[:, (fc % 2) * 1024:(fc % 2) * 1024 + 1024]))
        w2cast.append(n["vx"])
        g_w2 = n["vx"]

        @blk.tensor
        def _(t, fc=fc, g_w2=g_w2):
            t.wait_ge(vx, g_w2)
            t.wait_ge(sx, s_gelu)
            bofs = (fc % 2) * 1024
            for sub in range(3):
                psl = P2[:, 1024 * (sub % 2):1024 * (sub % 2) + 1024] if sub < 2 else P3[:, 0:1024]
                lhs = gT_v[:, fc * QCAP + 128 * sub: fc * QCAP + 128 * (sub + 1)]
                for j in range(2):
                    inc("tx", t.matmul(psl[:, 512 * j:512 * (j + 1)], lhs,
                                       w2b_v[:, bofs + 512 * j:bofs + 512 * (j + 1)],
                                       start=(fc == 0), stop=(fc == 31), skip_group_check=True))
        w2_g.append(n["tx"])
    t_w2 = n["tx"]

    corr_v = []
    for sub in range(3):
        @blk.vector
        def _(v, sub=sub):
            v.wait_ge(tx, t_w2)
            psl = P2[:, 1024 * (sub % 2):1024 * (sub % 2) + 1024] if sub < 2 else P3[:, 0:1024]
            xt = x1_v[:, sub * D:(sub + 1) * D]
            V(v, v.tensor_tensor(xt, xt, psl, op=ALU.add))
            V(v, v.tensor_scalar(xt, xt, wself[:, sub:sub + 1], None, op0=ALU.mult))
            V(v, v.tensor_tensor(xt, xt, xnq_v[:, sub * D:(sub + 1) * D], op=ALU.subtract))
            V(v, v.tensor_scalar(xt, xt, qkeep[:, sub:sub + 1], None, op0=ALU.mult))
        corr_v.append(n["vx"])
    v_corr = n["vx"]

    @blk.gpsimd
    def _(g):
        g.wait_ge(vx, v_corr)
        g.wait_ge(wr, w_base)
        for sub in range(3):
            inc("ws", g.indirect_dma_start(out_d, IndirectOffsetOnAxis(ap=qrho_i[:, sub:sub + 1], axis=0),
                                           x1_v[:, sub * D:(sub + 1) * D], None, compute_op=ALU.add), 16)
    w_all = n["ws"]

    @blk.sync
    def _(s):
        s.wait_ge(wr, w_base)
        s.wait_ge(ws, w_all)
        s.wait_ge(gd, n["gd"])
        s.wait_ge(sd, n["sd"])
        s.wait_ge(ld, n["ld"])
        s.wait_ge(vx, n["vx"])
        s.wait_ge(sx, n["sx"])
        s.wait_ge(tx, n["tx"])
        s.wait_ge(gx, n["gx"])
        s.wait_ge(cc, n["cc"])

    es.close()
    return nc


def _host_constants(s):
    g8 = np.zeros((128, 8), np.float32)
    for p in range(128):
        g8[p, p // 16] = 1.0
    lt = ((np.arange(128)[:, None]) < (np.arange(128)[None, :])).astype(np.float32)
    onesr = np.ones((1, 128), np.float32)
    riota = np.stack([128.0 * c + np.arange(128) for c in range(8)], 1).astype(np.float32)
    dc = np.zeros((128, ROUNDS), np.float32)
    for r in range(ROUNDS):
        d = D0 / (NTHR ** r)
        for p in range(128):
            dc[p, r] = np.float32(d * ((p // 16) + 1) / NTHR)
    sc = np.zeros((128, 8), np.float32)
    sc[:, 0] = 256.0 * s
    sc[:, 1] = 128.0 * s
    sc[:, 2] = float(s)
    sc[:, 4] = 1e-5
    sc[:, 5] = 1e-6
    sc[:, 6] = EXPB
    member = np.array([1.0 if ((p % 8) // 2) == s else 0.0 for p in range(128)], np.float32).reshape(128, 1)
    return g8, lt, onesr, riota, dc, sc, member


_NC_CACHE = [None]


def kernel(**inputs):
    x = np.ascontiguousarray(np.asarray(inputs["L0_attn_H"], np.float32))
    Wqkv = np.asarray(inputs["L0_attn_W_qkv"], np.float32)
    bqkv = np.asarray(inputs["L0_attn_b_qkv"], np.float32).reshape(1, 3 * D)
    Wproj = np.asarray(inputs["L0_attn_W_proj"], np.float32)
    bproj = np.asarray(inputs["L0_attn_b_proj"], np.float32).reshape(1, D)
    gamma = np.asarray(inputs["L0_attn_gamma"], np.float32).reshape(1, D)
    beta = np.asarray(inputs["L0_attn_beta"], np.float32).reshape(1, D)
    mask = np.asarray(inputs["L0_ffn_attention_mask"], np.float32)
    pos = np.asarray(inputs["L0_ffn_position_ids"], np.int32)
    rw = np.asarray(inputs["L0_ffn_router_w"], np.float32)
    rb = np.asarray(inputs["L0_ffn_router_b"], np.float32).reshape(1, 1)
    wq = np.ascontiguousarray(np.asarray(inputs["L0_ffn_wq"], np.float32))
    wk = np.ascontiguousarray(np.asarray(inputs["L0_ffn_wk"], np.float32))
    wv = np.ascontiguousarray(np.asarray(inputs["L0_ffn_wv"], np.float32))
    wo = np.ascontiguousarray(np.asarray(inputs["L0_ffn_wo"], np.float32))
    w1 = np.ascontiguousarray(np.asarray(inputs["L0_ffn_w1"], np.float32))
    w2 = np.ascontiguousarray(np.asarray(inputs["L0_ffn_w2"], np.float32))
    ln1 = np.asarray(inputs["L0_ffn_ln1"], np.float32).reshape(1, D)
    ln2 = np.asarray(inputs["L0_ffn_ln2"], np.float32).reshape(1, D)

    wqkvT = np.ascontiguousarray(Wqkv.T)
    wprojT = np.ascontiguousarray(Wproj.T)
    rw8 = np.ascontiguousarray(rw.reshape(8, 128).T)

    if _NC_CACHE[0] is None:
        _NC_CACHE[0] = build_nc()
    nc = _NC_CACHE[0]

    srows = {}
    in_maps = []
    for core in range(8):
        b = core // 4
        s = core % 4
        g8, lt, onesr, riota, dc, sc, member = _host_constants(s)
        rows = np.array([512 * h + 128 * s + r for h in range(H) for r in range(128)], np.int64)
        srows[core] = rows
        im = {
            "x_T": np.ascontiguousarray(x[b, 2048 * s:2048 * (s + 1)]),
            "x_S": np.ascontiguousarray(x[b][rows]),
            "x_full": x[b],
            "mask_f": np.ascontiguousarray(mask[b, 0]),
            "pos_f": np.ascontiguousarray(pos[b].reshape(1, 4096)),
            "wqkvT": wqkvT, "bqkv": bqkv, "wprojT": wprojT, "bproj": bproj,
            "gamma_d": gamma, "beta_d": beta,
            "rw_d": rw8, "rb_d": rb,
            "wq_d": wq, "wk_d": wk, "wv_d": wv, "wo_d": wo, "w1_d": w1, "w2_d": w2,
            "ln1_d": ln1, "ln2_d": ln2,
            "g8_d": g8, "lt_d": lt, "onesr_d": onesr, "riota_d": riota,
            "dc_d": dc, "sc_d": sc, "member_d": member,
            "bqq_d": np.ascontiguousarray(bqkv[0, 0:D].reshape(8, 128).T),
        }
        in_maps.append(im)

    res = run_bass_kernel_spmd(nc, in_maps, core_ids=list(range(8)))
    out = np.empty((B, N, D), np.float32)
    for core in range(8):
        b = core // 4
        out[b][srows[core]] = res.results[core]["out_sh"][:2048]
    return out


# version 03
# speedup vs baseline: 1.0046x; 1.0046x over previous
import math
import numpy as np
import concourse.bass as bass
import concourse.mybir as mybir
from concourse.bass import IndirectOffsetOnAxis
from concourse.bass_utils import run_bass_kernel_spmd

F32 = mybir.dt.float32
BF16 = mybir.dt.bfloat16
I32 = mybir.dt.int32
ALU = mybir.AluOpType
AF = mybir.ActivationFunctionType
AXL = mybir.AxisListType

B, N, D, H, HD = 2, 8192, 1024, 16, 64
TOK, NTS = 2048, 16
QCAP, KCAP, KSL = 384, 1024, 256
ROUNDS, NTHR = 11, 9
LO0, D0 = -64.0, 128.0
PI = math.pi
TP = 2 * PI
C1 = 6.283203125
C2 = -1.7881393432617188e-05
C3 = 6.357301884918343e-08
LNI = math.log(10000.0) / 32.0
LNI_HI = float(np.float32(LNI))
LNI_LO = float(np.float32(LNI - LNI_HI))
EXPB = -20.0
GROUPS = [[0, 1, 2, 3], [4, 5, 6, 7]]


def build_nc():
    from contextlib import ExitStack
    nc = bass.Bass("TRN2", target_bir_lowering=False, debug=False)
    es = ExitStack()

    di = lambda nm, sh, dt=F32: nc.dram_tensor(nm, sh, dt, kind="ExternalInput").ap()
    x_T = di("x_T", [TOK, D])
    x_S = di("x_S", [TOK, D])
    x_full = di("x_full", [N, D])
    mask_f = di("mask_f", [4096, 4096])
    pos_f = di("pos_f", [1, 4096], I32)
    wqkvT_d = di("wqkvT", [D, 3 * D])
    bqkv_d = di("bqkv", [1, 3 * D])
    wprojT_d = di("wprojT", [D, D])
    bproj_d = di("bproj", [1, D])
    gamma_d = di("gamma_d", [1, D])
    beta_d = di("beta_d", [1, D])
    rw_d = di("rw_d", [128, 8])
    rb_d = di("rb_d", [1, 1])
    wq_d = di("wq_d", [D, D])
    wk_d = di("wk_d", [D, D])
    wv_d = di("wv_d", [D, D])
    wo_d = di("wo_d", [D, D])
    w1_d = di("w1_d", [D, 4 * D])
    w2_d = di("w2_d", [4 * D, D])
    ln1_d = di("ln1_d", [1, D])
    ln2_d = di("ln2_d", [1, D])
    g8_d = di("g8_d", [128, 8])
    lt_d = di("lt_d", [128, 128])
    onesr_d = di("onesr_d", [1, 128])
    riota_d = di("riota_d", [128, 8])
    dc_d = di("dc_d", [128, ROUNDS])
    sc_d = di("sc_d", [128, 8])
    member_d = di("member_d", [128, 1])
    bqq_d = di("bqq_d", [128, 8])

    out_d = nc.dram_tensor("out_sh", [TOK + 1, D], F32, kind="ExternalOutput").ap()

    dsc = lambda nm, sh, dt=F32: nc.dram_tensor(nm, sh, dt)
    w_ag_in = dsc("w_ag_in", [1, TOK])
    w_ag_out = dsc("w_ag_out", [4, TOK])
    kv_ar_in = dsc("kv_ar_in", [128, 520])
    kv_ar_out = dsc("kv_ar_out", [128, 520])
    kvf_ag_in = dsc("kvf_ag_in", [KSL, 2 * D], BF16)
    kvf_ag_out = dsc("kvf_ag_out", [KCAP, 2 * D], BF16)
    s_scr = dsc("s_scr", [1, N])
    scr_dram = dsc("scr_dram", [TOK, D], BF16)
    m_scr1 = dsc("m_scr1", [QCAP, 4096], BF16)
    m_scr2 = dsc("m_scr2", [4097, QCAP], BF16)
    t_scr = dsc("t_scr", [QCAP, D], BF16)
    tk_scr = dsc("tk_scr", [KSL, D], BF16)

    sbt = lambda nm, sh, dt=F32: es.enter_context(nc.sbuf_tensor(nm, sh, dt))
    pst = lambda nm, sh, dt=F32: es.enter_context(nc.psum_tensor(nm, sh, dt))

    c_g8 = sbt("c_g8", [128, 8])
    c_lt = sbt("c_lt", [128, 128])
    c_ones = sbt("c_ones", [1, 128])
    c_onesb = sbt("c_onesb", [1, 128], BF16)
    c_riota = sbt("c_riota", [128, 8])
    c_dc = sbt("c_dc", [128, ROUNDS])
    c_sc = sbt("c_sc", [128, 8])
    c_member = sbt("c_member", [128, 1])
    c_bqkv = sbt("c_bqkv", [1, 3 * D], BF16)
    c_bqq = sbt("c_bqq", [128, 8])
    c_bqqn = sbt("c_bqqn", [128, 8])
    c_bproj = sbt("c_bproj", [1, D], BF16)
    c_rw = sbt("c_rw", [128, 8])
    c_rw_hi = sbt("c_rw_hi", [128, 8], BF16)
    c_rw_lo = sbt("c_rw_lo", [128, 8], BF16)
    c_rb = sbt("c_rb", [1, 1])
    gam_rep = sbt("gam_rep", [128, D])
    bet_rep = sbt("bet_rep", [128, D])
    ln1_rep = sbt("ln1_rep", [128, D])
    ln2_rep = sbt("ln2_rep", [128, D])
    inv_rep = sbt("inv_rep", [128, 32])

    xt_buf = sbt("xt_buf", [128, 2 * D])
    stat6 = sbt("stat6", [128, 12])
    stat2 = sbt("stat2", [128, 2])
    negm = sbt("negm", [128, 1])
    sqv = sbt("sqv", [128, 1])
    rstd = sbt("rstd", [128, 1])
    arenaA = sbt("arenaA", [128, 8 * TOK], BF16)
    arenaB = sbt("arenaB", [128, 8 * TOK], BF16)
    A3 = sbt("A3", [128, 16384], BF16)
    A4 = sbt("A4", [128, 8192])
    A5 = sbt("A5", [128, 4096])
    A7 = sbt("A7", [128, 8192], BF16)
    xn_f32 = A5[:, 0:D]
    xn_bf = A7[:, 0:D]
    lob_bf = A7[:, D:2 * D]
    loT_t = A7[:, 2 * D:4 * D]
    kphi_m = A5[:, 0:D]
    kphi_e = A5[:, D:2 * D]
    qphi_m = A5[:, 0:512]
    qphi_e = A5[:, 512:1024]
    qphi_t = A5[:, 1024:1536]
    u_t = A5[:, 0:D]
    xns2 = A5[:, D:2 * D]
    acc1 = A5[:, 2 * D:3 * D]
    acc2 = A5[:, 3 * D:4 * D]
    outrow = A5[:, D:2 * D]
    S_half = A4[:, 0:4096]
    S_junk = A4[:, 4096:8192]

    w_sb = A5[0:1, 0:TOK]
    rowb = A5[0:1, 2048:3072]
    cj512 = A4[:, 4096:4608]
    w_rep = A4[:, 4608:5120]
    w_nat = A4[:, 5120:5184]
    rrec = A5[0:1, 0:QCAP]
    rrep_sb = A5[0:64, 512:512 + QCAP]
    G_bf = A7[:, 0:D]
    hn_q = A7[:, 0:3 * D]
    hn_k = A7[:, 3 * D:5 * D]
    rop_a = A5[:, 2 * D:3 * D]
    rop_h1 = A5[:, 3 * D:4 * D]
    xnq_v = A4[:, 0:3 * D]
    xnk_v = A4[:, 3 * D:5 * D]
    qp_v = A4[:, 5 * D:8 * D]
    x1_v = A5[:, 0:3 * D]
    mga_v = A5[:, 0:4096]
    qr_v = A7[:, 0:3 * D]
    kr_v = A7[:, 3 * D:5 * D]
    vp_v = A7[:, 5 * D:7 * D]
    h1b_v = A7[:, 0:3 * D]
    kT_v = arenaB[:, 0:8 * KCAP]
    hnTq_v = arenaB[:, 8 * KCAP:8 * KCAP + 8 * QCAP]
    hnTk_v = arenaB[:, 8 * KCAP + 8 * QCAP:8 * KCAP + 8 * QCAP + 8 * KSL]
    expT_v = arenaB[:, 8 * KCAP:8 * KCAP + 8 * QCAP]
    e01T_v = arenaB[:, 8 * KCAP + 8 * QCAP:8 * KCAP + 8 * QCAP + 8 * QCAP]
    w1b_v = arenaB[:, 0:8192]
    w2b_v = arenaB[:, 8192:10240]
    vA_v = arenaA[:, 0:8 * 1040]
    qT_v = arenaA[:, 8 * 1040:8 * 1040 + 8 * QCAP]
    oT_v = arenaA[:, 8 * 1040 + 8 * QCAP:8 * 1040 + 16 * QCAP]
    qtz_e = arenaA[:, 8 * 1040 + 16 * QCAP:8 * 1040 + 17 * QCAP]
    qtz_o = arenaA[:, 8 * 1040 + 17 * QCAP:8 * 1040 + 18 * QCAP]
    gT_v = A3[:, 0:32 * QCAP]
    h1T_v = A3[:, 32 * QCAP:32 * QCAP + 8 * QCAP]
    lo_s = sbt("lo_s", [1, 1])
    T_s = sbt("T_s", [128, 1])
    cj8 = sbt("cj8", [1, 8])
    cnt_s = sbt("cnt_s", [128, 1])
    nge_s = sbt("nge_s", [1, 1])
    thr_s = sbt("thr_s", [1, 1])
    keep_s = sbt("keep_s", [128, 64])
    keepq_s = sbt("keepq_s", [128, 64])
    zer64 = sbt("zer64", [128, 64])
    Sp_s = sbt("Sp_s", [128, 64])
    Sqp_s = sbt("Sqp_s", [128, 64])
    tot_s = sbt("tot_s", [128, 1])
    totq_s = sbt("totq_s", [128, 1])
    Sf_s = sbt("Sf_s", [128, 64])
    Sqf_s = sbt("Sqf_s", [128, 64])
    minrow = sbt("minrow", [1, 128])
    mv_s = sbt("mv_s", [128, 64])
    qsel = sbt("qsel", [128, 3])
    ksel = sbt("ksel", [128, 2])
    kself = sbt("kself", [128, 8])
    kkeepf = sbt("kkeepf", [128, 8])
    ckf_i = sbt("ckf_i", [128, 8], I32)
    tmpf8 = sbt("tmpf8", [128, 8])
    mtr = sbt("mtr", [128, QCAP], BF16)
    qkeep = sbt("qkeep", [128, 3])
    kkeep = sbt("kkeep", [128, 2])
    qn_i = sbt("qn_i", [128, 3], I32)
    kn_i = sbt("kn_i", [128, 2], I32)
    cq_i = sbt("cq_i", [128, 3], I32)
    ck_i = sbt("ck_i", [128, 2], I32)
    qrho_i = sbt("qrho_i", [128, 3], I32)
    tmpi = sbt("tmpi", [128, 3], I32)
    tmpf = sbt("tmpf", [128, 3])
    tmpf2 = sbt("tmpf2", [128, 3])
    wself = sbt("wself", [128, 3])
    posq_i = sbt("posq_i", [128, 3], I32)
    posk_i = sbt("posk_i", [128, 2], I32)
    posq_f = sbt("posq_f", [128, 3])
    posk_f = sbt("posk_f", [128, 2])

    k_bf = sbt("k_bf", [128, D], BF16)
    v_bf = sbt("v_bf", [128, H * 65], BF16)
    wkv_v = A3
    kv_sb = A5[:, 0:520]
    wproj_sb = A3[:, 8192:16384]
    kv2x = sbt("kv2x", [128, 1040], BF16)
    recips = sbt("recips", [128, 16])

    cosq = sbt("cosq", [128, 3 * 64])
    sinq = sbt("sinq", [128, 3 * 64])
    cosk = sbt("cosk", [128, 2 * 64])
    sink = sbt("sink", [128, 2 * 64])
    angf = sbt("angf", [128, 32])
    angr = sbt("angr", [128, 32])
    angm = sbt("angm", [128, 32])
    angk_i = sbt("angk_i", [128, 32], I32)
    angkf = sbt("angkf", [128, 32])
    ang2 = sbt("ang2", [128, 32])

    P1 = pst("P1", [128, 512])
    P2 = pst("P2", [128, 2048])
    P3 = pst("P3", [128, 1024])
    P4 = pst("P4", [128, 512])

    ld = es.enter_context(nc.semaphore("ld"))
    sd = es.enter_context(nc.semaphore("sd"))
    gd = es.enter_context(nc.semaphore("gd"))
    vx = es.enter_context(nc.semaphore("vx"))
    sx = es.enter_context(nc.semaphore("sx"))
    tx = es.enter_context(nc.semaphore("tx"))
    gx = es.enter_context(nc.semaphore("gx"))
    cc = es.enter_context(nc.semaphore("cc"))
    wr = es.enter_context(nc.semaphore("wr"))
    ws = es.enter_context(nc.semaphore("ws"))
    n = {"ld": 0, "sd": 0, "gd": 0, "vx": 0, "sx": 0, "tx": 0, "gx": 0, "cc": 0, "wr": 0, "ws": 0}
    sems = {"ld": ld, "sd": sd, "gd": gd, "vx": vx, "sx": sx, "tx": tx, "gx": gx, "cc": cc, "wr": wr, "ws": ws}

    def inc(key, inst, k=1):
        inst.then_inc(sems[key], k)
        n[key] += k
        return n[key]

    blk = es.enter_context(nc.Block())

    def V(v, inst):
        inc("vx", inst)
        v.wait_ge(vx, n["vx"])

    def SS(s, inst):
        inc("sx", inst)
        s.wait_ge(sx, n["sx"])

    @blk.sync
    def _(s):
        for a, b_ in [(c_g8, g8_d), (c_lt, lt_d), (c_ones, onesr_d), (c_riota, riota_d),
                      (c_dc, dc_d), (c_sc, sc_d), (c_member, member_d), (c_rw, rw_d),
                      (c_rb, rb_d)]:
            inc("ld", s.dma_start(out=a[:, :], in_=b_), 16)
        inc("ld", s.dma_start(out=c_bqq[:, :], in_=bqq_d), 16)
    l_consts = n["ld"]

    @blk.gpsimd
    def _(g):
        inc("gd", g.dma_start(out=c_bqkv[:, :], in_=bqkv_d), 16)
        inc("gd", g.dma_start(out=c_bproj[:, :], in_=bproj_d), 16)
        inc("gd", g.dma_start(out=c_rw_hi[:, :], in_=rw_d), 16)
        inc("gx", g.iota(angf[:, :], pattern=[[1, 32]], base=0, channel_multiplier=0,
                         allow_small_or_imprecise_dtypes=True))
    g_consts = n["gd"]
    g_iota = n["gx"]

    bstage = arenaB[:, :].bitcast(F32)
    a3kv_casts = []
    for c in range(8):
        @blk.sync
        def _(s, c=c):
            if c >= 2:
                s.wait_ge(vx, a3kv_casts[c - 2])
            inc("ld", s.dma_start(out=bstage[:, (c % 2) * 2048:(c % 2) * 2048 + 2048],
                                  in_=wqkvT_d[128 * c:128 * (c + 1), D:3 * D]), 16)

        @blk.vector
        def _(v, c=c, lm=n["ld"]):
            v.wait_ge(ld, lm)
            V(v, v.tensor_copy(A3[:, c * 2 * D:(c + 1) * 2 * D],
                               bstage[:, (c % 2) * 2048:(c % 2) * 2048 + 2048]))
        a3kv_casts.append(n["vx"])
    v_a3kv = n["vx"]

    @blk.vector
    def _(v):
        v.wait_ge(ld, l_consts)
        v.wait_ge(gd, g_consts)
        v.wait_ge(gx, g_iota)
        V(v, v.memset(c_onesb[:, :], 1.0))
        V(v, v.memset(lo_s[:, :], LO0))
        V(v, v.memset(zer64[:, :], 0.0))
        V(v, v.tensor_tensor(cj512[:, 0:8], c_rw[:, :], c_rw_hi[:, :], op=ALU.subtract))
        V(v, v.tensor_copy(c_rw_lo[:, :], cj512[:, 0:8]))
        V(v, v.tensor_scalar(c_bqqn[:, :], c_bqq[:, :], -1.0, None, op0=ALU.mult))
        V(v, v.tensor_scalar(angr[:, :], angf[:, :], -LNI_HI, None, op0=ALU.mult))
        V(v, v.tensor_scalar(angm[:, :], angf[:, :], -LNI_LO, None, op0=ALU.mult))
        V(v, v.tensor_tensor(angr[:, :], angr[:, :], angm[:, :], op=ALU.add))
    v_init = n["vx"]

    @blk.scalar
    def _(s):
        s.wait_ge(vx, v_init)
        SS(s, s.activation(inv_rep[:, :], angr[:, :], AF.Exp))
    s_inv = n["sx"]

    for src_d, dst in [(gamma_d, gam_rep), (beta_d, bet_rep), (ln1_d, ln1_rep), (ln2_d, ln2_rep)]:
        @blk.sync
        def _(s, src_d=src_d, vv=n["vx"]):
            s.wait_ge(vx, vv)
            inc("ld", s.dma_start(out=rowb[:, :], in_=src_d), 16)
        l_row = n["ld"]
        for half in range(2):
            @blk.tensor
            def _(t, l_row=l_row, half=half, vv=n["vx"]):
                t.wait_ge(ld, l_row)
                t.wait_ge(vx, max(vv, v_init))
                inc("tx", t.matmul(P1[:, :], c_ones[:, :], rowb[:, 512 * half:512 * (half + 1)],
                                   start=True, stop=True))
            t_h = n["tx"]

            @blk.vector
            def _(v, dst=dst, half=half, t_h=t_h):
                v.wait_ge(tx, t_h)
                V(v, v.tensor_copy(dst[:, 512 * half:512 * (half + 1)], P1[:, :]))
    v_reps = n["vx"]

    tile_ld = []

    @blk.sync
    def _(s):
        for t in range(2):
            inc("ld", s.dma_start(out=xt_buf[:, (t % 2) * D:(t % 2 + 1) * D],
                                  in_=x_T[128 * t:128 * (t + 1), :]), 16)
            tile_ld.append(n["ld"])

    s_tr_done = []
    rt_done = []
    for t in range(NTS):
        buf = (t % 2) * D

        @blk.vector
        def _(v, t=t, buf=buf):
            v.wait_ge(ld, tile_ld[t])
            if t == 0:
                v.wait_ge(vx, v_reps)
            xt = xt_buf[:, buf:buf + D]
            V(v, v.bn_stats(stat6[:, 0:6], (xt)[:, 0:512]))
            V(v, v.bn_stats(stat6[:, 6:12], (xt)[:, 512:1024]))
            V(v, v.bn_aggr(stat2[:, :], stat6[:, :]))
            V(v, v.tensor_scalar(negm[:, :], stat2[:, 0:1], -1.0, None, op0=ALU.mult))
        v_st = n["vx"]

        @blk.scalar
        def _(s, v_st=v_st):
            s.wait_ge(vx, v_st)
            SS(s, s.activation(sqv[:, :], stat2[:, 1:2], AF.Sqrt, bias=c_sc[:, 4:5]))
        s_sq = n["sx"]

        xnb_t = A7[:, (t % 2) * 4 * D:(t % 2) * 4 * D + D]
        lob_t = A7[:, (t % 2) * 4 * D + D:(t % 2) * 4 * D + 2 * D]

        @blk.vector
        def _(v, t=t, buf=buf, s_sq=s_sq, xnb_t=xnb_t, lob_t=lob_t):
            v.wait_ge(sx, s_sq)
            if t >= 2:
                v.wait_ge(sd, s_tr_done[t - 2])
            V(v, v.reciprocal(rstd[:, :], sqv[:, :]))
            xt = xt_buf[:, buf:buf + D]
            V(v, v.tensor_scalar(xn_f32[:, :], xt, negm[:, :], rstd[:, :], op0=ALU.add, op1=ALU.mult))
            V(v, v.tensor_copy(xnb_t[:, :], xn_f32[:, :]))
            V(v, v.tensor_tensor(lob_t[:, :], xn_f32[:, :], xnb_t[:, :], op=ALU.subtract))
        v_xn = n["vx"]

        @blk.scalar
        def _(s, t=t, v_xn=v_xn, xnb_t=xnb_t, lob_t=lob_t):
            s.wait_ge(vx, v_xn)
            if t >= 2:
                s.wait_ge(tx, rt_done[t - 2])
            dst = arenaA[:, :].rearrange("p (c k) -> p c k", k=TOK)[:, :, 128 * t:128 * (t + 1)]
            inc("sd", s.dma_start_transpose(dst, xnb_t[:, :]), 16)
            dst2 = loT_t[:, (t % 2) * D:(t % 2 + 1) * D].rearrange("p (c k) -> p c k", k=128)
            inc("sd", s.dma_start_transpose(dst2, lob_t[:, :]), 16)
        s_tr_done.append(n["sd"])

        @blk.tensor
        def _(tt, t=t, s_tr=s_tr_done[t]):
            tt.wait_ge(sd, s_tr)
            for c in range(8):
                hi = arenaA[:, :].rearrange("p (c k) -> p c k", k=TOK)[:, c, 128 * t:128 * (t + 1)]
                lo_ = loT_t[:, (t % 2) * D:(t % 2 + 1) * D].rearrange("p (c k) -> p c k", k=128)[:, c, :]
                inc("tx", tt.matmul(P2[0:1, 128 * t:128 * (t + 1)], c_rw_hi[:, c:c + 1], hi,
                                    start=(c == 0), stop=False, skip_group_check=True))
                inc("tx", tt.matmul(P2[0:1, 128 * t:128 * (t + 1)], c_rw_hi[:, c:c + 1], lo_,
                                    start=False, stop=False, skip_group_check=True))
                inc("tx", tt.matmul(P2[0:1, 128 * t:128 * (t + 1)], c_rw_lo[:, c:c + 1], hi,
                                    start=False, stop=(c == 7), skip_group_check=True))
        rt_done.append(n["tx"])

        if t + 2 < NTS:
            @blk.sync
            def _(s, t=t, v_xn=v_xn):
                s.wait_ge(vx, v_xn)
                inc("ld", s.dma_start(out=xt_buf[:, (t % 2) * D:(t % 2 + 1) * D],
                                      in_=x_T[128 * (t + 2):128 * (t + 3), :]), 16)
                tile_ld.append(n["ld"])
    t_router = n["tx"]

    @blk.vector
    def _(v):
        v.wait_ge(tx, t_router)
        V(v, v.tensor_scalar(w_sb[:, :], P2[0:1, :], c_rb[:, :], None, op0=ALU.add))
    v_w = n["vx"]

    @blk.sync
    def _(s):
        s.wait_ge(vx, v_w)
        inc("ld", s.dma_start(out=w_ag_in.ap(), in_=w_sb[:, :]), 16)
    l_wag = n["ld"]

    @blk.gpsimd
    def _(g):
        g.wait_ge(ld, l_wag)
        inc("cc", g.collective_compute("AllGather", ALU.bypass, replica_groups=GROUPS,
                                       ins=[w_ag_in.ap().opt()], outs=[w_ag_out.ap().opt()]))
    c_w = n["cc"]

    wflat = w_ag_out.ap().rearrange("a b -> (a b)").rearrange("(x m) -> x m", x=1)

    @blk.sync
    def _(s):
        s.wait_ge(cc, c_w)
        for k in range(8):
            inc("ld", s.dma_start(out=w_rep[16 * k:16 * (k + 1), :],
                                  in_=wflat.rearrange("x (a b) -> (x a) b", a=16)), 16)
        inc("ld", s.dma_start(out=w_nat[:, :], in_=wflat.rearrange("x (a b) -> (x a) b", a=128)), 16)
    l_wload = n["ld"]

    for r in range(ROUNDS):
        v_now = n["vx"]

        @blk.tensor
        def _(t, v_now=v_now, r=r):
            t.wait_ge(vx, v_now)
            if r == 0:
                t.wait_ge(ld, l_wload)
            inc("tx", t.matmul(P1[:, 0:1], c_ones[:, :], lo_s[:, :], start=True, stop=True))
        t_lo = n["tx"]

        @blk.vector
        def _(v, r=r, t_lo=t_lo):
            v.wait_ge(tx, t_lo)
            if r == 0:
                v.wait_ge(ld, l_wload)
            V(v, v.tensor_tensor(T_s[:, :], P1[:, 0:1], c_dc[:, r:r + 1], op=ALU.add))
            V(v, v.tensor_scalar(cj512[:, :], w_rep[:, :], T_s[:, :], None, op0=ALU.is_gt))
            V(v, v.tensor_reduce(cnt_s[:, :], cj512[:, :], axis=AXL.X, op=ALU.add))
        v_cnt = n["vx"]

        @blk.tensor
        def _(t, v_cnt=v_cnt):
            t.wait_ge(vx, v_cnt)
            inc("tx", t.matmul(P1[0:1, 8:16], cnt_s[:, :], c_g8[:, :], start=True, stop=True))
        t_c8 = n["tx"]

        @blk.vector
        def _(v, r=r, t_c8=t_c8):
            v.wait_ge(tx, t_c8)
            V(v, v.tensor_scalar(cj8[:, :], P1[0:1, 8:16], 1024.0, None, op0=ALU.is_ge))
            V(v, v.tensor_reduce(nge_s[:, :], cj8[:, :], axis=AXL.X, op=ALU.add))
            V(v, v.tensor_scalar(lo_s[:, :], nge_s[:, :], D0 / (NTHR ** (r + 1)), lo_s[:, :],
                                 op0=ALU.mult, op1=ALU.add))
    v_bis = n["vx"]

    @blk.tensor
    def _(t):
        t.wait_ge(vx, v_bis)
        inc("tx", t.matmul(P1[:, 0:1], c_ones[:, :], lo_s[:, :], start=True, stop=True))
    t_lof = n["tx"]

    @blk.vector
    def _(v):
        v.wait_ge(tx, t_lof)
        V(v, v.tensor_scalar(keep_s[:, :], w_nat[:, :], P1[:, 0:1], None, op0=ALU.is_gt))
        V(v, v.tensor_tensor(mv_s[:, :], w_nat[:, :], keep_s[:, :], op=ALU.mult))
        V(v, v.tensor_scalar(keepq_s[:, :], keep_s[:, :], -1e30, 1e30, op0=ALU.mult, op1=ALU.add))
        V(v, v.tensor_tensor(mv_s[:, :], mv_s[:, :], keepq_s[:, :], op=ALU.add))
        V(v, v.tensor_reduce(tot_s[:, :], mv_s[:, :], axis=AXL.X, op=ALU.min))
    v_min = n["vx"]

    @blk.sync
    def _(s):
        s.wait_ge(vx, v_min)
        inc("ld", s.dma_start(out=s_scr[:, 0:128].rearrange("x (a b) -> (x a) b", a=128), in_=tot_s[:, :]), 16)
        s.wait_ge(ld, n["ld"])
        inc("ld", s.dma_start(out=minrow[:, :], in_=s_scr[:, 0:128]), 16)
    l_minr = n["ld"]

    @blk.vector
    def _(v):
        v.wait_ge(ld, l_minr)
        V(v, v.tensor_reduce(thr_s[:, :], minrow[:, :], axis=AXL.X, op=ALU.min))
    v_thr = n["vx"]

    @blk.tensor
    def _(t):
        t.wait_ge(vx, v_thr)
        inc("tx", t.matmul(P1[:, 0:1], c_ones[:, :], thr_s[:, :], start=True, stop=True))
    t_thr = n["tx"]

    @blk.vector
    def _(v):
        v.wait_ge(tx, t_thr)
        V(v, v.tensor_scalar(keep_s[:, :], w_nat[:, :], P1[:, 0:1], None, op0=ALU.is_gt))
        V(v, v.tensor_tensor(keepq_s[:, :], keep_s[:, :], c_member[:, :].to_broadcast([128, 64]), op=ALU.mult))
        V(v, v.tensor_tensor_scan(Sp_s[:, :], keep_s[:, :], zer64[:, :], 0.0, op0=ALU.add, op1=ALU.add))
        V(v, v.tensor_tensor_scan(Sqp_s[:, :], keepq_s[:, :], zer64[:, :], 0.0, op0=ALU.add, op1=ALU.add))
        V(v, v.tensor_copy(tot_s[:, :], Sp_s[:, 63:64]))
        V(v, v.tensor_copy(totq_s[:, :], Sqp_s[:, 63:64]))
    v_scan = n["vx"]

    @blk.tensor
    def _(t):
        t.wait_ge(vx, v_scan)
        inc("tx", t.matmul(P1[:, 1:2], c_lt[:, :], tot_s[:, :], start=True, stop=True))
    t_off = n["tx"]

    @blk.vector
    def _(v):
        v.wait_ge(tx, t_off)
        V(v, v.tensor_scalar(Sf_s[:, :], Sp_s[:, :], P1[:, 1:2], None, op0=ALU.add))
    v_sf = n["vx"]

    @blk.tensor
    def _(t):
        t.wait_ge(vx, v_sf)
        inc("tx", t.matmul(P1[:, 1:2], c_lt[:, :], totq_s[:, :], start=True, stop=True))
    t_offq = n["tx"]

    @blk.vector
    def _(v):
        v.wait_ge(tx, t_offq)
        V(v, v.tensor_scalar(Sqf_s[:, :], Sqp_s[:, :], P1[:, 1:2], None, op0=ALU.add))
    v_sqf = n["vx"]

    ksel_parts = []
    for half in range(2):
        @blk.sync
        def _(s, half=half, vv=n["vx"]):
            s.wait_ge(vx, max(vv, v_sf))
            inc("ld", s.dma_start(out=s_scr.rearrange("x (a b) -> (x a) b", a=128), in_=Sf_s[:, :]), 16)
            s.wait_ge(ld, n["ld"])
            for k in range(4):
                inc("ld", s.dma_start(out=S_half[32 * k:32 * (k + 1), :],
                                      in_=s_scr[:, 4096 * half:4096 * (half + 1)].to_broadcast([32, 4096])), 16)
        l_h = n["ld"]

        @blk.vector
        def _(v, half=half, l_h=l_h):
            v.wait_ge(ld, l_h)
            for c in range(8):
                V(v, v.tensor_scalar(S_junk[:, :], S_half[:, :], c_riota[:, c:c + 1], None, op0=ALU.is_le))
                dst = kself[:, c:c + 1] if half == 0 else cnt_s[:, :]
                V(v, v.tensor_reduce(dst, S_junk[:, :], axis=AXL.X, op=ALU.add))
                if half == 1:
                    V(v, v.tensor_tensor(kself[:, c:c + 1], kself[:, c:c + 1], cnt_s[:, :], op=ALU.add))
            for c in range(2):
                V(v, v.tensor_tensor(T_s[:, :], c_riota[:, c:c + 1], c_sc[:, 0:1], op=ALU.add))
                V(v, v.tensor_scalar(S_junk[:, :], S_half[:, :], T_s[:, :], None, op0=ALU.is_le))
                dst = ksel[:, c:c + 1] if half == 0 else cnt_s[:, :]
                V(v, v.tensor_reduce(dst, S_junk[:, :], axis=AXL.X, op=ALU.add))
                if half == 1:
                    V(v, v.tensor_tensor(ksel[:, c:c + 1], ksel[:, c:c + 1], cnt_s[:, :], op=ALU.add))
    @blk.vector
    def _(v):
        V(v, v.tensor_scalar(kkeep[:, :], ksel[:, :], float(N), None, op0=ALU.is_lt))
        V(v, v.tensor_scalar(ksel[:, :], ksel[:, :], float(N - 1), None, op0=ALU.min))
        V(v, v.tensor_scalar(kkeepf[:, :], kself[:, :], float(N), None, op0=ALU.is_lt))
        V(v, v.tensor_scalar(kself[:, :], kself[:, :], float(N - 1), None, op0=ALU.min))
        V(v, v.tensor_scalar(tmpf8[:, :], kself[:, :], 4095.0, None, op0=ALU.min))
        V(v, v.tensor_tensor(tmpf8[:, :], tmpf8[:, :], kkeepf[:, :], op=ALU.mult))
        V(v, v.tensor_scalar(cj8[0:1, :], kkeepf[0:1, :], 0.0, None, op0=ALU.mult))
        V(v, v.tensor_scalar(cnt_s[:, 0:1], kkeepf[:, 0:1], 0.0, None, op0=ALU.mult))
        V(v, v.tensor_scalar(kkeepf[:, :], kkeepf[:, :], -4096.0, 4096.0, op0=ALU.mult, op1=ALU.add))
        V(v, v.tensor_tensor(tmpf8[:, :], tmpf8[:, :], kkeepf[:, :], op=ALU.add))
    v_ksel = n["vx"]

    for half in range(2):
        @blk.sync
        def _(s, half=half, vv=n["vx"]):
            s.wait_ge(vx, max(vv, v_sqf))
            inc("ld", s.dma_start(out=s_scr.rearrange("x (a b) -> (x a) b", a=128), in_=Sqf_s[:, :]), 16)
            s.wait_ge(ld, n["ld"])
            for k in range(4):
                inc("ld", s.dma_start(out=S_half[32 * k:32 * (k + 1), :],
                                      in_=s_scr[:, 4096 * half:4096 * (half + 1)].to_broadcast([32, 4096])), 16)
        l_h = n["ld"]

        @blk.vector
        def _(v, half=half, l_h=l_h):
            v.wait_ge(ld, l_h)
            for c in range(3):
                V(v, v.tensor_scalar(S_junk[:, :], S_half[:, :], c_riota[:, c:c + 1], None, op0=ALU.is_le))
                dst = qsel[:, c:c + 1] if half == 0 else cnt_s[:, :]
                V(v, v.tensor_reduce(dst, S_junk[:, :], axis=AXL.X, op=ALU.add))
                if half == 1:
                    V(v, v.tensor_tensor(qsel[:, c:c + 1], qsel[:, c:c + 1], cnt_s[:, :], op=ALU.add))
    @blk.vector
    def _(v):
        V(v, v.tensor_scalar(qkeep[:, :], qsel[:, :], float(N), None, op0=ALU.is_lt))
        V(v, v.tensor_scalar(qsel[:, :], qsel[:, :], float(N - 1), None, op0=ALU.min))
    v_qsel = n["vx"]

    @blk.gpsimd
    def _(g):
        g.wait_ge(vx, max(v_qsel, v_ksel))
        inc("gx", g.tensor_copy(qn_i[:, :], qsel[:, :]))
        inc("gx", g.tensor_copy(kn_i[:, :], ksel[:, :]))
    g_cast1 = n["gx"]

    @blk.vector
    def _(v):
        v.wait_ge(gx, g_cast1)
        V(v, v.tensor_scalar(cq_i[:, :], qn_i[:, :], 4095, None, op0=ALU.min))
        V(v, v.tensor_scalar(ck_i[:, :], kn_i[:, :], 4095, None, op0=ALU.min))
        V(v, v.tensor_scalar(tmpi[:, :], qn_i[:, :], 9, None, op0=ALU.logical_shift_right))
        V(v, v.tensor_scalar(tmpi[:, :], tmpi[:, :], 7, None, op0=ALU.logical_shift_left))
        V(v, v.tensor_scalar(qrho_i[:, :], qn_i[:, :], 511, None, op0=ALU.bitwise_and))
        V(v, v.tensor_tensor(qrho_i[:, :], qrho_i[:, :], tmpi[:, :], op=ALU.add))
    v_der1 = n["vx"]

    @blk.gpsimd
    def _(g):
        g.wait_ge(vx, v_der1)
        inc("gx", g.tensor_copy(ckf_i[:, :], tmpf8[:, :]))
        inc("gx", g.tensor_copy(tmpf2[:, :], qrho_i[:, :]))
    g_cast2 = n["gx"]

    @blk.vector
    def _(v):
        v.wait_ge(gx, g_cast2)
        V(v, v.tensor_scalar(tmpf2[:, :], tmpf2[:, :], c_sc[:, 1:2], None, op0=ALU.subtract))
        V(v, v.tensor_scalar(tmpf2[:, :], tmpf2[:, :], 0.0, 2047.0, op0=ALU.max, op1=ALU.min))
        V(v, v.tensor_scalar(tmpf2[:, :], tmpf2[:, :], float(TOK), None, op0=ALU.subtract))
        V(v, v.tensor_tensor(tmpf2[:, :], tmpf2[:, :], qkeep[:, :], op=ALU.mult))
        V(v, v.tensor_scalar(tmpf2[:, :], tmpf2[:, :], float(TOK), None, op0=ALU.add))
    v_rho = n["vx"]

    @blk.gpsimd
    def _(g):
        g.wait_ge(vx, v_rho)
        inc("gx", g.tensor_copy(qrho_i[:, :], tmpf2[:, :]))
        inc("gx", g.tensor_copy(posq_f[:, 0:1], qrho_i[:, 0:1]))
        for c in range(3):
            inc("gd", g.indirect_dma_start(wself[:, c:c + 1], None, wflat,
                                           IndirectOffsetOnAxis(ap=qn_i[:, c:c + 1], axis=1)), 16)
            inc("gd", g.indirect_dma_start(posq_i[:, c:c + 1], None, pos_f,
                                           IndirectOffsetOnAxis(ap=cq_i[:, c:c + 1], axis=1)), 16)
            inc("gd", g.indirect_dma_start(A4[:, 0:3 * D][:, c * D:(c + 1) * D], None, x_full,
                                           IndirectOffsetOnAxis(ap=qn_i[:, c:c + 1], axis=0)), 16)
        for c in range(2):
            inc("gd", g.indirect_dma_start(posk_i[:, c:c + 1], None, pos_f,
                                           IndirectOffsetOnAxis(ap=ck_i[:, c:c + 1], axis=1)), 16)
            inc("gd", g.indirect_dma_start(A4[:, 3 * D:5 * D][:, c * D:(c + 1) * D], None, x_full,
                                           IndirectOffsetOnAxis(ap=kn_i[:, c:c + 1], axis=0)), 16)
    g_gath = n["gd"]
    g_rhoi = n["gx"]

    kvts_v = []
    kvts_s = []
    for ts in range(NTS):
        @blk.tensor
        def _(t, ts=ts):
            if ts == 0:
                t.wait_ge(sd, s_tr_done[NTS - 1])
                t.wait_ge(gd, g_consts)
                t.wait_ge(vx, max(v_w, v_a3kv))
            else:
                t.wait_ge(vx, kvts_v[ts - 1])
                t.wait_ge(sx, kvts_s[ts - 1])
            for c in range(8):
                xs = arenaA[:, :].rearrange("p (c k) -> p c k", k=TOK)[:, c, 128 * ts:128 * (ts + 1)]
                for j in range(4):
                    rhs = A3[:, c * 2 * D + 512 * j: c * 2 * D + 512 * (j + 1)]
                    inc("tx", t.matmul(P2[:, 512 * j:512 * (j + 1)], xs, rhs,
                                       start=(c == 0), stop=False, skip_group_check=True))
            for j in range(4):
                inc("tx", t.matmul(P2[:, 512 * j:512 * (j + 1)], c_onesb[:, :],
                                   c_bqkv[:, D + 512 * j:D + 512 * (j + 1)], start=False, stop=True,
                                   skip_group_check=True))
        t_kvp = n["tx"]

        @blk.scalar
        def _(s, ts=ts, t_kvp=t_kvp):
            s.wait_ge(tx, t_kvp)
            if ts == 0:
                s.wait_ge(ld, l_wag)
                s.wait_ge(cc, c_w)
            SS(s, s.activation(kphi_m[:, :], P2[:, 0:D], AF.Relu, scale=-1.0))
            SS(s, s.activation(kphi_e[:, :], kphi_m[:, :], AF.Exp, scale=-1.0))
            SS(s, s.activation(kphi_m[:, :], P2[:, 0:D], AF.Relu))
            vdst = v_bf[:, :].rearrange("p (h c) -> p h c", c=65)[:, :, 0:64]
            SS(s, s.activation(vdst, P2[:, D:2 * D].rearrange("p (h c) -> p h c", c=64), AF.Copy))
        kvts_s.append(n["sx"])

        @blk.vector
        def _(v, ts=ts, s_now=kvts_s[ts]):
            v.wait_ge(sx, s_now)
            V(v, v.tensor_tensor(k_bf[:, :], kphi_m[:, :], kphi_e[:, :], op=ALU.add))
            ones_cols = v_bf[:, :].rearrange("p (h c) -> p h c", c=65)[:, :, 64:65]
            V(v, v.memset(ones_cols, 1.0))
        kvts_v.append(n["vx"])

        @blk.tensor
        def _(t, ts=ts, v_now=kvts_v[ts]):
            t.wait_ge(vx, v_now)
            for h in range(H):
                pair = h // 2
                odd = h % 2
                coff = (512 if pair >= 4 else 0) + (pair % 4) * 65
                lhs = k_bf[:, 64 * h:64 * (h + 1)]
                rhs = v_bf[:, 65 * h:65 * (h + 1)]
                tp = (0, 64 * odd) if odd else None
                inc("tx", t.matmul(P3[64 * odd:64 * (odd + 1), coff:coff + 65], lhs, rhs,
                                   start=(ts == 0 and pair % 4 == 0), stop=(ts == NTS - 1),
                                   tile_position=tp, skip_group_check=True))
    t_kvall = n["tx"]

    @blk.vector
    def _(v):
        v.wait_ge(tx, t_kvall)
        V(v, v.tensor_copy(kv_sb[:, 0:260], P3[:, 0:260]))
        V(v, v.tensor_copy(kv_sb[:, 260:520], P3[:, 512:772]))
    v_kvev = n["vx"]

    @blk.sync
    def _(s):
        s.wait_ge(vx, v_kvev)
        inc("ld", s.dma_start(out=kv_ar_in.ap(), in_=kv_sb[:, :]), 16)
    l_kvar = n["ld"]

    @blk.gpsimd
    def _(g):
        g.wait_ge(ld, l_kvar)
        inc("cc", g.collective_compute("AllReduce", ALU.add, replica_groups=GROUPS,
                                       ins=[kv_ar_in.ap().opt()], outs=[kv_ar_out.ap().opt()]))
    c_kv = n["cc"]

    @blk.vector
    def _(v):
        V(v, v.memset(kv2x[64:128, 0:520], 0.0))
        V(v, v.memset(kv2x[0:64, 520:1040], 0.0))
    v_kv2z = n["vx"]

    @blk.gpsimd
    def _(g):
        g.wait_ge(cc, c_kv)
        g.wait_ge(vx, v_kv2z)
        inc("gd", g.dma_start(out=kv2x[0:64, 0:520], in_=kv_ar_out.ap()[0:64, :]), 16)
        inc("gd", g.dma_start(out=kv2x[64:128, 520:1040], in_=kv_ar_out.ap()[64:128, :]), 16)
    g_kv2x = n["gd"]

    qcol_casts = []
    for c in range(8):
        @blk.sync
        def _(s, c=c):
            if c < 2:
                s.wait_ge(tx, t_kvall)
            if c >= 2:
                s.wait_ge(vx, qcol_casts[c - 2])
            inc("ld", s.dma_start(out=bstage[:, (c % 2) * 1024:(c % 2) * 1024 + 1024],
                                  in_=wqkvT_d[128 * c:128 * (c + 1), 0:D]), 16)

        @blk.vector
        def _(v, c=c, lm=n["ld"]):
            v.wait_ge(ld, lm)
            if c < 2:
                v.wait_ge(tx, t_kvall)
            V(v, v.tensor_copy(A3[:, c * D:(c + 1) * D],
                               bstage[:, (c % 2) * 1024:(c % 2) * 1024 + 1024]))
        qcol_casts.append(n["vx"])
    g_wq = n["vx"]

    q_ec_v = []
    for ec in range(8):
        @blk.tensor
        def _(t, ec=ec):
            if ec == 0:
                t.wait_ge(tx, t_kvall)
                t.wait_ge(vx, g_wq)
                t.wait_ge(sx, kvts_s[NTS - 1])
            else:
                t.wait_ge(vx, q_ec_v[ec - 1])
            for c in range(8):
                lhs = A3[:, c * D + 128 * ec: c * D + 128 * (ec + 1)]
                for j in range(4):
                    rhs = arenaA[:, :].rearrange("p (c k) -> p c k", k=TOK)[:, c, 512 * j:512 * (j + 1)]
                    inc("tx", t.matmul(P2[:, 512 * j:512 * (j + 1)], lhs, rhs,
                                       start=(c == 0), stop=(c == 7), skip_group_check=True))
        t_qp = n["tx"]

        for j in range(4):
            @blk.scalar
            def _(s, ec=ec, j=j, t_qp=t_qp):
                s.wait_ge(tx, t_qp)
                if ec == 0 and j == 0:
                    s.wait_ge(ld, l_kvar)
                psl = P2[:, 512 * j:512 * (j + 1)]
                SS(s, s.activation(qphi_m[:, :], psl, AF.Relu, scale=-1.0, bias=c_bqqn[:, ec:ec + 1]))
                SS(s, s.activation(qphi_e[:, :], qphi_m[:, :], AF.Exp, scale=-1.0))
                SS(s, s.activation(qphi_t[:, :], psl, AF.Relu, bias=c_bqq[:, ec:ec + 1]))
            s_qj = n["sx"]

            @blk.vector
            def _(v, ec=ec, j=j, s_qj=s_qj):
                v.wait_ge(sx, s_qj)
                dst = arenaB[:, ec * TOK + 512 * j: ec * TOK + 512 * (j + 1)]
                V(v, v.tensor_tensor(dst, qphi_t[:, :], qphi_e[:, :], op=ALU.add))
        q_ec_v.append(n["vx"])
    t_qall = n["tx"]
    v_qall = n["vx"]

    qo_v = []
    scr_l = []
    for ts in range(NTS):
        Gb = A7[:, (ts % 2) * D:(ts % 2) * D + D]

        @blk.tensor
        def _(t, ts=ts):
            if ts == 0:
                t.wait_ge(vx, v_qall)
                t.wait_ge(gd, g_kv2x)
            else:
                t.wait_ge(vx, qo_v[ts - 1])
            for h in range(H):
                ec = h // 2
                odd = h % 2
                p0 = h // 2
                c0 = 520 * odd + (260 if (p0 >= 4) else 0) + (p0 % 4) * 65
                lhs = arenaB[:, ec * TOK + 128 * ts: ec * TOK + 128 * (ts + 1)]
                rhs = kv2x[:, c0:c0 + 65]
                g4 = h // 4
                off = 512 * g4 + 65 * (h % 4)
                inc("tx", t.matmul(P2[:, off:off + 65], lhs, rhs, start=True, stop=True,
                                   skip_group_check=True))
        t_qo = n["tx"]

        @blk.vector
        def _(v, ts=ts, t_qo=t_qo, Gb=Gb):
            v.wait_ge(tx, t_qo)
            if ts >= 2:
                v.wait_ge(ld, scr_l[ts - 2])
            for g4 in range(4):
                qk1c = P2[:, 512 * g4:512 * g4 + 260].rearrange("p (s c) -> p s c", c=65)[:, :, 64:65]
                V(v, v.tensor_scalar(recips[:, 4 * g4:4 * g4 + 4],
                                     qk1c.rearrange("p s c -> p (s c)"), 1e-6, None, op0=ALU.max))
            V(v, v.reciprocal(recips[:, :], recips[:, :]))
            for g4 in range(4):
                src = P2[:, :].rearrange("p (g q) -> p g q", q=512)[:, g4, 0:260]
                src = src.rearrange("p (s c) -> p s c", c=65)[:, :, 0:64]
                dst = Gb[:, 256 * g4:256 * (g4 + 1)].rearrange("p (s e) -> p s e", e=64)
                rb_ = recips[:, 4 * g4:4 * g4 + 4].rearrange("p (s o) -> p s o", o=1).to_broadcast([128, 4, 64])
                V(v, v.tensor_tensor(dst, src, rb_, op=ALU.mult))
        qo_v.append(n["vx"])

        @blk.sync
        def _(s, ts=ts, v_now=qo_v[ts], Gb=Gb):
            s.wait_ge(vx, v_now)
            for a in range(8):
                srcA = Gb[16 * a:16 * (a + 1), :].rearrange("p (h e) -> p h e", e=64)
                dstA = scr_dram.ap().rearrange("(h ra) (j e) -> h ra j e", h=16, j=16)[:, 8 * ts + a]
                dstA = dstA.transpose([1, 0, 2])
                inc("ld", s.dma_start(out=dstA, in_=srcA, allow_slow_non_contiguous=True), 16)
        scr_l.append(n["ld"])
    l_scr = n["ld"]
    t_qoall = n["tx"]

    @blk.scalar
    def _(s):
        s.wait_ge(ld, l_scr)
        s.wait_ge(tx, t_qoall)
        for c in range(8):
            inc("sd", s.dma_start_transpose(arenaB[:, c * TOK:(c + 1) * TOK],
                                            scr_dram.ap()[:, 128 * c:128 * (c + 1)]), 16)
    s_scrT = n["sd"]

    astage = arenaA[:, :].bitcast(F32)
    wproj_casts = []
    for c in range(8):
        @blk.sync
        def _(s, c=c):
            if c < 2:
                s.wait_ge(tx, t_qall)
            if c >= 2:
                s.wait_ge(vx, wproj_casts[c - 2])
            inc("ld", s.dma_start(out=astage[:, (c % 2) * 1024:(c % 2) * 1024 + 1024],
                                  in_=wprojT_d[128 * c:128 * (c + 1), :]), 16)

        @blk.vector
        def _(v, c=c, lm=n["ld"]):
            v.wait_ge(ld, lm)
            if c < 2:
                v.wait_ge(tx, t_kvall)
            V(v, v.tensor_copy(wproj_sb[:, c * D:(c + 1) * D],
                               astage[:, (c % 2) * 1024:(c % 2) * 1024 + 1024]))
        wproj_casts.append(n["vx"])
    g_wproj = n["vx"]

    xs_ld = []
    wr_marks = []
    @blk.sync
    def _(s):
        for t in range(2):
            inc("ld", s.dma_start(out=xt_buf[:, (t % 2) * D:(t % 2 + 1) * D],
                                  in_=x_S[128 * t:128 * (t + 1), :]), 16)
            xs_ld.append(n["ld"])

    pr_v = []
    for rs in range(NTS):
        pofs = (rs % 2) * 1024

        @blk.tensor
        def _(t, rs=rs, pofs=pofs):
            if rs == 0:
                t.wait_ge(sd, s_scrT)
                t.wait_ge(vx, g_wproj)
            if rs >= 2:
                t.wait_ge(vx, pr_v[rs - 2])
            for c in range(8):
                lhs = arenaB[:, :].rearrange("p (c k) -> p c k", k=TOK)[:, c, 128 * rs:128 * (rs + 1)]
                for j in range(2):
                    rhs = wproj_sb[:, c * D + 512 * j:c * D + 512 * (j + 1)]
                    inc("tx", t.matmul(P2[:, pofs + 512 * j:pofs + 512 * (j + 1)], lhs, rhs,
                                       start=(c == 0), stop=False, skip_group_check=True))
            for j in range(2):
                inc("tx", t.matmul(P2[:, pofs + 512 * j:pofs + 512 * (j + 1)], c_onesb[:, :],
                                   c_bproj[:, 512 * j:512 * (j + 1)], start=False, stop=True,
                                   skip_group_check=True))
        t_pr = n["tx"]

        @blk.vector
        def _(v, rs=rs, pofs=pofs, t_pr=t_pr):
            v.wait_ge(tx, t_pr)
            v.wait_ge(ld, xs_ld[rs])
            psl = P2[:, pofs:pofs + 1024]
            V(v, v.bn_stats(stat6[:, 0:6], (psl)[:, 0:512]))
            V(v, v.bn_stats(stat6[:, 6:12], (psl)[:, 512:1024]))
            V(v, v.bn_aggr(stat2[:, :], stat6[:, :]))
            V(v, v.tensor_scalar(negm[:, :], stat2[:, 0:1], -1.0, None, op0=ALU.mult))
        v_s1 = n["vx"]

        @blk.scalar
        def _(s, v_s1=v_s1):
            s.wait_ge(vx, v_s1)
            SS(s, s.activation(sqv[:, :], stat2[:, 1:2], AF.Sqrt, bias=c_sc[:, 4:5]))
        s_s1 = n["sx"]

        @blk.vector
        def _(v, rs=rs, pofs=pofs, s_s1=s_s1):
            v.wait_ge(sx, s_s1)
            xt = xt_buf[:, (rs % 2) * D:(rs % 2 + 1) * D]
            psl = P2[:, pofs:pofs + 1024]
            V(v, v.reciprocal(rstd[:, :], sqv[:, :]))
            V(v, v.tensor_scalar(u_t[:, :], psl, negm[:, :], rstd[:, :], op0=ALU.add, op1=ALU.mult))
            V(v, v.bn_stats(stat6[:, 0:6], (xt)[:, 0:512]))
            V(v, v.bn_stats(stat6[:, 6:12], (xt)[:, 512:1024]))
            V(v, v.bn_aggr(stat2[:, :], stat6[:, :]))
            V(v, v.tensor_scalar(negm[:, :], stat2[:, 0:1], -1.0, None, op0=ALU.mult))
        v_s2 = n["vx"]

        @blk.scalar
        def _(s, v_s2=v_s2):
            s.wait_ge(vx, v_s2)
            SS(s, s.activation(sqv[:, :], stat2[:, 1:2], AF.Sqrt, bias=c_sc[:, 4:5]))
        s_s2 = n["sx"]

        orow = acc1 if rs % 2 == 0 else acc2

        @blk.vector
        def _(v, rs=rs, pofs=pofs, s_s2=s_s2, orow=orow):
            v.wait_ge(sx, s_s2)
            if rs >= 2:
                v.wait_ge(wr, wr_marks[rs - 2])
            xt = xt_buf[:, (rs % 2) * D:(rs % 2 + 1) * D]
            V(v, v.reciprocal(rstd[:, :], sqv[:, :]))
            V(v, v.tensor_scalar(rstd[:, :], rstd[:, :], 2.0, None, op0=ALU.mult))
            V(v, v.tensor_scalar(xns2[:, :], xt, negm[:, :], rstd[:, :], op0=ALU.add, op1=ALU.mult))
            V(v, v.tensor_tensor(xns2[:, :], xns2[:, :], xt, op=ALU.add))
            V(v, v.tensor_tensor(xns2[:, :], xns2[:, :], bet_rep[:, :], op=ALU.add))
            V(v, v.tensor_tensor(orow[:, :], u_t[:, :], gam_rep[:, :], op=ALU.mult))
            V(v, v.tensor_tensor(orow[:, :], orow[:, :], xns2[:, :], op=ALU.add))
        pr_v.append(n["vx"])

        @blk.sync
        def _(s, rs=rs, v_now=pr_v[rs], orow=orow):
            s.wait_ge(vx, v_now)
            inc("wr", s.dma_start(out=out_d[128 * rs:128 * (rs + 1), :], in_=orow[:, :]), 16)
            wr_marks.append(n["wr"])
            if rs + 2 < NTS:
                inc("ld", s.dma_start(out=xt_buf[:, (rs % 2) * D:(rs % 2 + 1) * D],
                                      in_=x_S[128 * (rs + 2):128 * (rs + 3), :]), 16)
                xs_ld.append(n["ld"])
    w_base = n["wr"]


    @blk.gpsimd
    def _(g):
        g.wait_ge(gd, g_gath)
        inc("gx", g.tensor_copy(posq_f[:, :], posq_i[:, :]))
        inc("gx", g.tensor_copy(posk_f[:, :], posk_i[:, :]))
    g_posf = n["gx"]

    def rope_tables(tag, posf, col, cosT, sinT, scale):
        gprev0 = n["gx"]
        sprev0 = n["sx"]
        @blk.vector
        def _(v, posf=posf, col=col, gprev0=gprev0):
            v.wait_ge(gx, max(gprev0, g_posf))
            v.wait_ge(sx, s_inv)
            V(v, v.tensor_scalar(angf[:, :], inv_rep[:, :], posf[:, col:col + 1], None, op0=ALU.mult))
            V(v, v.tensor_scalar(angm[:, :], angf[:, :], 1.0 / TP, None, op0=ALU.mult))
        v_a = n["vx"]

        @blk.gpsimd
        def _(g, v_a=v_a):
            g.wait_ge(vx, v_a)
            inc("gx", g.tensor_copy(angk_i[:, :], angm[:, :]))
            inc("gx", g.tensor_copy(angkf[:, :], angk_i[:, :]))
        g_a = n["gx"]

        @blk.vector
        def _(v, g_a=g_a, sprev0=sprev0):
            v.wait_ge(gx, g_a)
            v.wait_ge(sx, sprev0)
            V(v, v.tensor_scalar(ang2[:, :], angkf[:, :], -C1, None, op0=ALU.mult))
            V(v, v.tensor_tensor(angr[:, :], ang2[:, :], angf[:, :], op=ALU.add))
            V(v, v.tensor_scalar(ang2[:, :], angkf[:, :], -C2, None, op0=ALU.mult))
            V(v, v.tensor_tensor(angr[:, :], ang2[:, :], angr[:, :], op=ALU.add))
            V(v, v.tensor_scalar(ang2[:, :], angkf[:, :], -C3, None, op0=ALU.mult))
            V(v, v.tensor_tensor(angr[:, :], ang2[:, :], angr[:, :], op=ALU.add))
            V(v, v.tensor_scalar(angm[:, :], angr[:, :], PI, None, op0=ALU.is_gt))
            V(v, v.tensor_scalar(angm[:, :], angm[:, :], -TP, None, op0=ALU.mult))
            V(v, v.tensor_tensor(ang2[:, :], angm[:, :], angr[:, :], op=ALU.add))
        v_b = n["vx"]

        @blk.scalar
        def _(s, cosT=cosT, col=col, v_b=v_b):
            s.wait_ge(vx, v_b)
            SS(s, s.activation(sinT[:, 64 * col:64 * col + 32], ang2[:, :], AF.Sin))
            SS(s, s.activation(sinT[:, 64 * col + 32:64 * col + 64], ang2[:, :], AF.Sin))
        s_b = n["sx"]

        @blk.vector
        def _(v, v_b=v_b, s_b=s_b):
            v.wait_ge(sx, s_b)
            V(v, v.tensor_scalar(ang2[:, :], angr[:, :], PI / 2, None, op0=ALU.add))
            V(v, v.tensor_scalar(angm[:, :], ang2[:, :], PI, None, op0=ALU.is_gt))
            V(v, v.tensor_scalar(angm[:, :], angm[:, :], -TP, None, op0=ALU.mult))
            V(v, v.tensor_tensor(ang2[:, :], angm[:, :], ang2[:, :], op=ALU.add))
        v_c = n["vx"]

        @blk.scalar
        def _(s, cosT=cosT, col=col, v_c=v_c):
            s.wait_ge(vx, v_c)
            SS(s, s.activation(cosT[:, 64 * col:64 * col + 32], ang2[:, :], AF.Sin))
            SS(s, s.activation(cosT[:, 64 * col + 32:64 * col + 64], ang2[:, :], AF.Sin))
        s_c = n["sx"]

        if scale != 1.0:
            @blk.vector
            def _(v, cosT=cosT, col=col, s_c=s_c, scale=scale):
                v.wait_ge(sx, s_c)
                V(v, v.tensor_scalar(cosT[:, 64 * col:64 * (col + 1)], cosT[:, 64 * col:64 * (col + 1)],
                                     scale, None, op0=ALU.mult))
                V(v, v.tensor_scalar(sinT[:, 64 * col:64 * (col + 1)], sinT[:, 64 * col:64 * (col + 1)],
                                     scale, None, op0=ALU.mult))
        return n["vx"], n["sx"]

    for col in range(3):
        rope_tables("q", posq_f, col, cosq, sinq, 0.125)
    for col in range(2):
        rope_tables("k", posk_f, col, cosk, sink, 1.0)
    v_rope_t = n["vx"]
    s_rope_t = n["sx"]

    def ln_rms(xgv, hnv, ncols, lnrep):
        marks = []
        for c in range(ncols):
            @blk.vector
            def _(v, c=c, xgv=xgv):
                v.wait_ge(gd, g_gath)
                v.wait_ge(tx, t_qoall)
                v.wait_ge(ld, l_scr)
                v.wait_ge(wr, w_base)
                xt = xgv[:, c * D:(c + 1) * D]
                V(v, v.bn_stats(stat6[:, 0:6], (xt)[:, 0:512]))
                V(v, v.bn_stats(stat6[:, 6:12], (xt)[:, 512:1024]))
                V(v, v.bn_aggr(stat2[:, :], stat6[:, :]))
                V(v, v.tensor_scalar(negm[:, :], stat2[:, 0:1], -1.0, None, op0=ALU.mult))
            v_1 = n["vx"]

            @blk.scalar
            def _(s, v_1=v_1):
                s.wait_ge(vx, v_1)
                SS(s, s.activation(sqv[:, :], stat2[:, 1:2], AF.Sqrt, bias=c_sc[:, 4:5]))
            s_1 = n["sx"]

            @blk.vector
            def _(v, c=c, xgv=xgv, s_1=s_1):
                v.wait_ge(sx, s_1)
                xt = xgv[:, c * D:(c + 1) * D]
                V(v, v.reciprocal(rstd[:, :], sqv[:, :]))
                V(v, v.tensor_scalar(xt, xt, negm[:, :], rstd[:, :], op0=ALU.add, op1=ALU.mult))
                V(v, v.bn_stats(stat6[:, 0:6], (xt)[:, 0:512]))
                V(v, v.bn_stats(stat6[:, 6:12], (xt)[:, 512:1024]))
                V(v, v.bn_aggr(stat2[:, :], stat6[:, :]))
                V(v, v.tensor_tensor(negm[:, :], stat2[:, 0:1], stat2[:, 0:1], op=ALU.mult))
                V(v, v.tensor_tensor(negm[:, :], negm[:, :], stat2[:, 1:2], op=ALU.add))
            v_2 = n["vx"]

            @blk.scalar
            def _(s, v_2=v_2):
                s.wait_ge(vx, v_2)
                SS(s, s.activation(sqv[:, :], negm[:, :], AF.Sqrt, bias=c_sc[:, 5:6]))
            s_2 = n["sx"]

            @blk.vector
            def _(v, c=c, xgv=xgv, hnv=hnv, lnrep=lnrep, s_2=s_2):
                v.wait_ge(sx, s_2)
                xt = xgv[:, c * D:(c + 1) * D]
                V(v, v.reciprocal(rstd[:, :], sqv[:, :]))
                V(v, v.tensor_scalar(rop_a[:, :], xt, rstd[:, :], None, op0=ALU.mult))
                V(v, v.tensor_tensor(hnv[:, c * D:(c + 1) * D], rop_a[:, :], lnrep[:, :], op=ALU.mult))
            marks.append(n["vx"])
        return marks

    hnq_marks = ln_rms(xnq_v, hn_q, 3, ln1_rep)
    hnk_marks = ln_rms(xnk_v, hn_k, 2, ln1_rep)
    v_hn = n["vx"]

    @blk.sync
    def _(s):
        s.wait_ge(vx, v_hn)
        for c in range(3):
            inc("ld", s.dma_start(out=t_scr.ap()[128 * c:128 * (c + 1), :], in_=hn_q[:, c * D:(c + 1) * D]), 16)
        for c in range(2):
            inc("ld", s.dma_start(out=tk_scr.ap()[128 * c:128 * (c + 1), :], in_=hn_k[:, c * D:(c + 1) * D]), 16)
    l_hnw = n["ld"]

    @blk.scalar
    def _(s):
        s.wait_ge(ld, l_hnw)
        s.wait_ge(tx, n["tx"])
        for c in range(8):
            inc("sd", s.dma_start_transpose(hnTq_v[:, c * QCAP:(c + 1) * QCAP],
                                            t_scr.ap()[:, 128 * c:128 * (c + 1)]), 16)
            inc("sd", s.dma_start_transpose(hnTk_v[:, c * KSL:(c + 1) * KSL],
                                            tk_scr.ap()[:, 128 * c:128 * (c + 1)]), 16)
    s_hnT = n["sd"]

    sqstage = A3[:, 0:8192].bitcast(F32)
    sq_prev = [None]

    def load_sq_weight(wd, marker):
        casts = []
        for c in range(8):
            @blk.sync
            def _(s, wd=wd, c=c, marker=marker, prev=sq_prev[0]):
                if c == 0:
                    if marker is not None:
                        s.wait_ge(tx, marker)
                    if prev is not None:
                        s.wait_ge(vx, prev)
                if c >= 2:
                    s.wait_ge(vx, casts[c - 2])
                inc("ld", s.dma_start(out=sqstage[:, (c % 2) * 2048:(c % 2) * 2048 + 1024],
                                      in_=wd[128 * c:128 * (c + 1), :]), 16)

            @blk.vector
            def _(v, c=c, lm=n["ld"], marker=marker):
                v.wait_ge(ld, lm)
                if c == 0 and marker is not None:
                    v.wait_ge(tx, marker)
                V(v, v.tensor_copy(wproj_sb[:, c * D:(c + 1) * D],
                                   sqstage[:, (c % 2) * 2048:(c % 2) * 2048 + 1024]))
            casts.append(n["vx"])
        sq_prev[0] = n["vx"]
        return n["vx"]

    g_wqld = load_sq_weight(wq_d, n["tx"])

    qproj_v = []
    for sub in range(3):
        @blk.tensor
        def _(t, sub=sub):
            t.wait_ge(sd, s_hnT)
            t.wait_ge(vx, g_wqld)
            if sub >= 2:
                t.wait_ge(vx, qproj_v[sub - 2])
            pofs = (sub % 2) * 1024
            for c in range(8):
                lhs = hnTq_v[:, c * QCAP + 128 * sub: c * QCAP + 128 * (sub + 1)]
                for j in range(2):
                    inc("tx", t.matmul(P2[:, pofs + 512 * j:pofs + 512 * (j + 1)], lhs,
                                       wproj_sb[:, c * D + 512 * j:c * D + 512 * (j + 1)],
                                       start=(c == 0), stop=(c == 7), skip_group_check=True))
        t_q1 = n["tx"]

        @blk.vector
        def _(v, sub=sub, t_q1=t_q1):
            v.wait_ge(tx, t_q1)
            pofs = (sub % 2) * 1024
            V(v, v.tensor_copy(qp_v[:, sub * D:(sub + 1) * D], P2[:, pofs:pofs + 1024]))
        qproj_v.append(n["vx"])
    t_qproj = n["tx"]

    rope_q_v = []
    for sub in range(3):
        @blk.vector
        def _(v, sub=sub):
            v.wait_ge(vx, max(qproj_v[sub], v_rope_t))
            qt = qp_v[:, sub * D:(sub + 1) * D]
            ct = cosq[:, 64 * sub:64 * (sub + 1)]
            st_ = sinq[:, 64 * sub:64 * (sub + 1)]
            ctb = ct.rearrange("p (o e) -> p o e", o=1).to_broadcast([128, H, 64])
            V(v, v.tensor_tensor(rop_a[:, :].rearrange("p (h e) -> p h e", e=64),
                                 qt.rearrange("p (h e) -> p h e", e=64), ctb, op=ALU.mult))
            q4 = qt.rearrange("p (h a e) -> p h a e", a=2, e=32)
            a4 = rop_a[:, :].rearrange("p (h a e) -> p h a e", a=2, e=32)
            o4 = qr_v[:, sub * D:(sub + 1) * D].rearrange("p (h a e) -> p h a e", a=2, e=32)
            slb = st_[:, 0:32].rearrange("p (o e) -> p o e", o=1).to_broadcast([128, H, 32])
            shb = st_[:, 32:64].rearrange("p (o e) -> p o e", o=1).to_broadcast([128, H, 32])
            V(v, v.tensor_tensor(ang2[:, :].rearrange("p (o e) -> p o e", o=1)[:, 0:1, 0:32], angf[:, 0:1].rearrange("p (o e) -> p o e", o=1)[:, :, 0:1].to_broadcast([128, 1, 32]), angf[:, 0:1].rearrange("p (o e) -> p o e", o=1)[:, :, 0:1].to_broadcast([128, 1, 32]), op=ALU.mult)) if False else None
            V(v, v.tensor_tensor(xns2[:, 0:512].rearrange("p (h e) -> p h e", e=32),
                                 q4[:, :, 1, :], slb, op=ALU.mult))
            V(v, v.tensor_tensor(o4[:, :, 0, :], a4[:, :, 0, :],
                                 xns2[:, 0:512].rearrange("p (h e) -> p h e", e=32), op=ALU.subtract))
            V(v, v.tensor_tensor(xns2[:, 0:512].rearrange("p (h e) -> p h e", e=32),
                                 q4[:, :, 0, :], shb, op=ALU.mult))
            V(v, v.tensor_tensor(o4[:, :, 1, :], a4[:, :, 1, :],
                                 xns2[:, 0:512].rearrange("p (h e) -> p h e", e=32), op=ALU.add))
        rope_q_v.append(n["vx"])
    v_ropeq = n["vx"]

    g_wkld = load_sq_weight(wk_d, t_qproj)
    kproj_v = []
    for sub in range(2):
        @blk.tensor
        def _(t, sub=sub):
            t.wait_ge(vx, g_wkld)
            t.wait_ge(tx, t_qproj)
            if sub >= 2:
                t.wait_ge(vx, kproj_v[sub - 2])
            pofs = (sub % 2) * 1024
            for c in range(8):
                lhs = hnTk_v[:, c * KSL + 128 * sub: c * KSL + 128 * (sub + 1)]
                for j in range(2):
                    inc("tx", t.matmul(P2[:, pofs + 512 * j:pofs + 512 * (j + 1)], lhs,
                                       wproj_sb[:, c * D + 512 * j:c * D + 512 * (j + 1)],
                                       start=(c == 0), stop=(c == 7), skip_group_check=True))
        t_k1 = n["tx"]

        @blk.vector
        def _(v, sub=sub, t_k1=t_k1):
            v.wait_ge(tx, t_k1)
            v.wait_ge(vx, v_rope_t)
            pofs = (sub % 2) * 1024
            ct = cosk[:, 64 * sub:64 * (sub + 1)]
            st_ = sink[:, 64 * sub:64 * (sub + 1)]
            ctb = ct.rearrange("p (o e) -> p o e", o=1).to_broadcast([128, H, 64])
            V(v, v.tensor_copy(rop_h1[:, :], P2[:, pofs:pofs + 1024]))
            ps4 = rop_h1[:, :]
            V(v, v.tensor_tensor(rop_a[:, :].rearrange("p (h e) -> p h e", e=64),
                                 ps4.rearrange("p (h e) -> p h e", e=64), ctb, op=ALU.mult))
            q4 = ps4.rearrange("p (h a e) -> p h a e", a=2, e=32)
            a4 = rop_a[:, :].rearrange("p (h a e) -> p h a e", a=2, e=32)
            o4 = kr_v[:, sub * D:(sub + 1) * D].rearrange("p (h a e) -> p h a e", a=2, e=32)
            slb = st_[:, 0:32].rearrange("p (o e) -> p o e", o=1).to_broadcast([128, H, 32])
            shb = st_[:, 32:64].rearrange("p (o e) -> p o e", o=1).to_broadcast([128, H, 32])
            V(v, v.tensor_tensor(xns2[:, 0:512].rearrange("p (h e) -> p h e", e=32),
                                 q4[:, :, 1, :], slb, op=ALU.mult))
            V(v, v.tensor_tensor(o4[:, :, 0, :], a4[:, :, 0, :],
                                 xns2[:, 0:512].rearrange("p (h e) -> p h e", e=32), op=ALU.subtract))
            V(v, v.tensor_tensor(xns2[:, 0:512].rearrange("p (h e) -> p h e", e=32),
                                 q4[:, :, 0, :], shb, op=ALU.mult))
            V(v, v.tensor_tensor(o4[:, :, 1, :], a4[:, :, 1, :],
                                 xns2[:, 0:512].rearrange("p (h e) -> p h e", e=32), op=ALU.add))
        kproj_v.append(n["vx"])
    t_kproj = n["tx"]
    v_kr = n["vx"]

    g_wvld = load_sq_weight(wv_d, t_kproj)
    vproj_v = []
    for sub in range(2):
        @blk.tensor
        def _(t, sub=sub):
            t.wait_ge(vx, g_wvld)
            t.wait_ge(vx, v_kr)
            pofs = (sub % 2) * 1024
            for c in range(8):
                lhs = hnTk_v[:, c * KSL + 128 * sub: c * KSL + 128 * (sub + 1)]
                for j in range(2):
                    inc("tx", t.matmul(P2[:, pofs + 512 * j:pofs + 512 * (j + 1)], lhs,
                                       wproj_sb[:, c * D + 512 * j:c * D + 512 * (j + 1)],
                                       start=(c == 0), stop=(c == 7), skip_group_check=True))
        t_v1 = n["tx"]

        @blk.vector
        def _(v, sub=sub, t_v1=t_v1):
            v.wait_ge(tx, t_v1)
            pofs = (sub % 2) * 1024
            V(v, v.tensor_copy(vp_v[:, sub * D:(sub + 1) * D], P2[:, pofs:pofs + 1024]))
        vproj_v.append(n["vx"])
    t_vproj = n["tx"]
    v_vp = n["vx"]

    @blk.sync
    def _(s):
        s.wait_ge(vx, max(v_kr, v_vp))
        for c in range(2):
            inc("ld", s.dma_start(out=kvf_ag_in.ap()[128 * c:128 * (c + 1), 0:D], in_=kr_v[:, c * D:(c + 1) * D]), 16)
            inc("ld", s.dma_start(out=kvf_ag_in.ap()[128 * c:128 * (c + 1), D:2 * D], in_=vp_v[:, c * D:(c + 1) * D]), 16)
    l_kvfin = n["ld"]

    @blk.gpsimd
    def _(g):
        g.wait_ge(ld, l_kvfin)
        inc("cc", g.collective_compute("AllGather", ALU.bypass, replica_groups=GROUPS,
                                       ins=[kvf_ag_in.ap().opt()], outs=[kvf_ag_out.ap().opt()]))
    c_kvf = n["cc"]

    m_marks = []
    for c in range(3):
        @blk.gpsimd
        def _(g, c=c, pv=n["gx"]):
            g.wait_ge(tx, t_vproj)
            g.wait_ge(wr, w_base)
            g.wait_ge(gx, g_rhoi)
            if c > 0:
                g.wait_ge(gd, m_marks[c - 1][0])
            inc("gd", g.indirect_dma_start(mga_v, None, mask_f,
                                           IndirectOffsetOnAxis(ap=cq_i[:, c:c + 1], axis=0)), 16)
        g_m1 = n["gd"]

        @blk.gpsimd
        def _(g, c=c, g_m1=g_m1):
            g.wait_ge(gd, g_m1)
            inc("gd", g.dma_start(out=m_scr1.ap()[128 * c:128 * (c + 1), :], in_=mga_v), 16)
        m_marks.append((n["gd"], 0))
    g_mrows = n["gd"]

    @blk.scalar
    def _(s):
        s.wait_ge(gd, g_mrows)
        s.wait_ge(tx, t_qoall)
        bufs = [mtr[:, :], qtz_e[:, :], qtz_o[:, :]]
        wmk = {}
        for cc_ in range(32):
            buf = bufs[cc_ % 3]
            if cc_ >= 3:
                s.wait_ge(sd, wmk[cc_ - 3])
            inc("sd", s.dma_start_transpose(buf, m_scr1.ap()[:, 128 * cc_:128 * (cc_ + 1)]), 16)
            s.wait_ge(sd, n["sd"])
            inc("sd", s.dma_start(out=m_scr2.ap()[128 * cc_:128 * (cc_ + 1), :], in_=buf), 16)
            wmk[cc_] = n["sd"]
    s_mtr = n["sd"]

    @blk.vector
    def _(v):
        v.wait_ge(sd, s_mtr)
        V(v, v.memset(mtr[0:1, :], -1e9))
    v_mneg = n["vx"]

    @blk.sync
    def _(s):
        s.wait_ge(vx, v_mneg)
        s.wait_ge(sd, s_mtr)
        inc("ld", s.dma_start(out=m_scr2.ap()[4096:4097, :], in_=mtr[0:1, :]), 16)
    l_mneg = n["ld"]

    @blk.gpsimd
    def _(g):
        g.wait_ge(ld, l_mneg)
        for c in range(8):
            inc("gd", g.indirect_dma_start(e01T_v[:, c * QCAP:(c + 1) * QCAP], None, m_scr2.ap(),
                                           IndirectOffsetOnAxis(ap=ckf_i[:, c:c + 1], axis=0)), 16)
    g_mselT = n["gd"]

    @blk.scalar
    def _(s):
        s.wait_ge(gd, g_mselT)
        SS(s, s.activation(e01T_v, e01T_v, AF.Exp))
    s_e01 = n["sx"]

    @blk.scalar
    def _(s):
        s.wait_ge(cc, c_kvf)
        s.wait_ge(tx, t_vproj)
        for c in range(8):
            inc("sd", s.dma_start_transpose(kT_v[:, c * KCAP:(c + 1) * KCAP],
                                            kvf_ag_out.ap()[:, 128 * c:128 * (c + 1)]), 16)
    s_kT = n["sd"]

    @blk.vector
    def _(v):
        v.wait_ge(tx, t_qoall)
        ones_cols = vA_v.rearrange("p (t h c) -> p t h c", h=16, c=65)[:, :, :, 64:65]
        V(v, v.memset(ones_cols, 1.0))
    v_vones = n["vx"]

    @blk.sync
    def _(s):
        s.wait_ge(cc, c_kvf)
        s.wait_ge(vx, v_vones)
        for t in range(8):
            src = kvf_ag_out.ap()[128 * t:128 * (t + 1), D:2 * D].rearrange("p (h c) -> p h c", c=64)
            dst = vA_v.rearrange("p (t h c) -> p t h c", h=16, c=65)[:, t, :, 0:64]
            inc("ld", s.dma_start(out=dst, in_=src), 16)
    g_vA = n["ld"]

    @blk.sync
    def _(s):
        s.wait_ge(vx, v_ropeq)
        for c in range(3):
            inc("ld", s.dma_start(out=t_scr.ap()[128 * c:128 * (c + 1), :], in_=qr_v[:, c * D:(c + 1) * D]), 16)
    l_qw = n["ld"]

    @blk.scalar
    def _(s):
        s.wait_ge(ld, l_qw)
        for c in range(8):
            inc("sd", s.dma_start_transpose(qT_v[:, c * QCAP:(c + 1) * QCAP],
                                            t_scr.ap()[:, 128 * c:128 * (c + 1)]), 16)
    s_qT = n["sd"]

    @blk.vector
    def _(v):
        v.wait_ge(tx, t_qoall)
        v.wait_ge(sd, s_mtr)
        V(v, v.memset(qtz_e[64:128, :], 0.0))
        V(v, v.memset(qtz_o[0:64, :], 0.0))
    v_qtz0 = n["vx"]

    head_v = []
    head_s = []
    for h in range(H):
        ec = h // 2
        odd = h % 2
        qtz = qtz_o if odd else qtz_e

        @blk.vector
        def _(v, h=h, ec=ec, odd=odd, qtz=qtz):
            v.wait_ge(vx, v_qtz0)
            if h < 2:
                v.wait_ge(sd, s_qT)
            else:
                v.wait_ge(sx, head_s[h - 1])
            V(v, v.tensor_copy(qtz[64 * odd:64 * odd + 64, :],
                               qT_v[64 * odd:64 * odd + 64, ec * QCAP:(ec + 1) * QCAP]))
        v_qtzh = n["vx"]

        exp_marks = []
        for kt in range(8):
            @blk.tensor
            def _(t, h=h, kt=kt, ec=ec, odd=odd, qtz=qtz, v_qtzh=v_qtzh):
                if h == 0 and kt == 0:
                    t.wait_ge(sd, max(s_kT, s_qT))
                    t.wait_ge(ld, g_vA)
                    t.wait_ge(sx, s_e01)
                if kt == 0:
                    t.wait_ge(vx, v_qtzh)
                if kt >= 2:
                    t.wait_ge(sx, exp_marks[kt - 2])
                if h > 0 and kt < 2:
                    t.wait_ge(sx, head_s[h - 1])
                lhs = kT_v[:, ec * KCAP + 128 * kt: ec * KCAP + 128 * (kt + 1)]
                rhs = qtz[:, :]
                pofs = (kt % 2) * 512
                inc("tx", t.matmul(P3[:, pofs:pofs + QCAP], lhs, rhs, start=True, stop=True,
                                   skip_group_check=True))
            t_sc = n["tx"]

            @blk.scalar
            def _(s, h=h, kt=kt, t_sc=t_sc):
                s.wait_ge(tx, t_sc)
                pofs = (kt % 2) * 512
                SS(s, s.activation(expT_v[:, kt * QCAP:(kt + 1) * QCAP], P3[:, pofs:pofs + QCAP],
                                   AF.Exp, bias=c_sc[:, 6:7]))
            exp_marks.append(n["sx"])

        @blk.vector
        def _(v, h=h, em=exp_marks[7]):
            v.wait_ge(sx, em)
            V(v, v.tensor_tensor(expT_v, expT_v, e01T_v, op=ALU.mult))
        v_em = n["vx"]

        av_done = []
        for kt in range(8):
            @blk.tensor
            def _(t, h=h, kt=kt, v_em=v_em):
                t.wait_ge(vx, v_em)
                lhs = vA_v[:, kt * 1040 + 65 * h: kt * 1040 + 65 * (h + 1)]
                rhs = expT_v[:, kt * QCAP:(kt + 1) * QCAP]
                inc("tx", t.matmul(P4[0:65, 0:QCAP], lhs, rhs, start=(kt == 0), stop=(kt == 7),
                                   skip_group_check=True))
        t_av = n["tx"]

        @blk.vector
        def _(v, h=h, t_av=t_av):
            v.wait_ge(tx, t_av)
            V(v, v.reciprocal(rrec[:, :], P4[64:65, 0:QCAP]))
        v_rr = n["vx"]

        @blk.tensor
        def _(t, v_rr=v_rr):
            t.wait_ge(vx, v_rr)
            inc("tx", t.matmul(P1[0:64, 0:QCAP], c_ones[:, 0:64], rrec[:, :], start=True, stop=True))
        t_rr = n["tx"]

        @blk.vector
        def _(v, h=h, ec=ec, odd=odd, t_rr=t_rr):
            v.wait_ge(tx, t_rr)
            V(v, v.tensor_copy(rrep_sb[:, :], P1[0:64, 0:QCAP]))
            dst = oT_v[64 * odd:64 * odd + 64, ec * QCAP:(ec + 1) * QCAP]
            V(v, v.tensor_tensor(dst, P4[0:64, 0:QCAP], rrep_sb[:, :], op=ALU.mult))
        head_v.append(n["vx"])
        head_s.append(n["sx"])
    v_heads = n["vx"]

    g_wold = load_sq_weight(wo_d, n["tx"])
    wo_v = []
    for sub in range(3):
        @blk.tensor
        def _(t, sub=sub):
            t.wait_ge(vx, g_wold)
            t.wait_ge(vx, v_heads)
            if sub >= 2:
                t.wait_ge(vx, wo_v[sub - 2])
            pofs = (sub % 2) * 1024
            for c in range(8):
                lhs = oT_v[:, c * QCAP + 128 * sub: c * QCAP + 128 * (sub + 1)]
                for j in range(2):
                    inc("tx", t.matmul(P2[:, pofs + 512 * j:pofs + 512 * (j + 1)], lhs,
                                       wproj_sb[:, c * D + 512 * j:c * D + 512 * (j + 1)],
                                       start=(c == 0), stop=(c == 7), skip_group_check=True))
        t_wo = n["tx"]

        @blk.vector
        def _(v, sub=sub, t_wo=t_wo):
            v.wait_ge(tx, t_wo)
            pofs = (sub % 2) * 1024
            V(v, v.tensor_tensor(x1_v[:, sub * D:(sub + 1) * D], P2[:, pofs:pofs + 1024],
                                 xnq_v[:, sub * D:(sub + 1) * D], op=ALU.add))
        wo_v.append(n["vx"])
    t_woall = n["tx"]

    h1_marks = []
    for c in range(3):
        @blk.vector
        def _(v, c=c):
            v.wait_ge(vx, wo_v[c])
            xt = x1_v[:, c * D:(c + 1) * D]
            V(v, v.bn_stats(stat6[:, 0:6], (xt)[:, 0:512]))
            V(v, v.bn_stats(stat6[:, 6:12], (xt)[:, 512:1024]))
            V(v, v.bn_aggr(stat2[:, :], stat6[:, :]))
            V(v, v.tensor_tensor(negm[:, :], stat2[:, 0:1], stat2[:, 0:1], op=ALU.mult))
            V(v, v.tensor_tensor(negm[:, :], negm[:, :], stat2[:, 1:2], op=ALU.add))
        v_1 = n["vx"]

        @blk.scalar
        def _(s, v_1=v_1):
            s.wait_ge(vx, v_1)
            SS(s, s.activation(sqv[:, :], negm[:, :], AF.Sqrt, bias=c_sc[:, 5:6]))
        s_1 = n["sx"]

        @blk.vector
        def _(v, c=c, s_1=s_1):
            v.wait_ge(sx, s_1)
            xt = x1_v[:, c * D:(c + 1) * D]
            V(v, v.reciprocal(rstd[:, :], sqv[:, :]))
            V(v, v.tensor_scalar(rop_h1[:, :], xt, rstd[:, :], None, op0=ALU.mult))
            V(v, v.tensor_tensor(h1b_v[:, c * D:(c + 1) * D], rop_h1[:, :], ln2_rep[:, :], op=ALU.mult))
        h1_marks.append(n["vx"])
    v_h1 = n["vx"]

    @blk.sync
    def _(s):
        s.wait_ge(vx, v_h1)
        for c in range(3):
            inc("ld", s.dma_start(out=t_scr.ap()[128 * c:128 * (c + 1), :], in_=h1b_v[:, c * D:(c + 1) * D]), 16)
    l_h1w = n["ld"]

    @blk.scalar
    def _(s):
        s.wait_ge(ld, l_h1w)
        for c in range(8):
            inc("sd", s.dma_start_transpose(h1T_v[:, c * QCAP:(c + 1) * QCAP],
                                            t_scr.ap()[:, 128 * c:128 * (c + 1)]), 16)
    s_h1T = n["sd"]

    w1stage = astage
    w1cast = []
    w1_g = []
    gelu_marks = []
    for fb in range(8):
        bofs = (fb % 2) * 4096

        @blk.sync
        def _(s, fb=fb):
            if fb < 2:
                s.wait_ge(tx, t_woall)
                s.wait_ge(vx, v_heads)
            if fb >= 2:
                s.wait_ge(vx, w1cast[fb - 2])
            for c in range(8):
                inc("ld", s.dma_start(out=w1stage[:, (fb % 2) * 4096 + 512 * c:
                                                  (fb % 2) * 4096 + 512 * (c + 1)],
                                      in_=w1_d[128 * c:128 * (c + 1), 512 * fb:512 * (fb + 1)]), 16)

        @blk.vector
        def _(v, fb=fb, bofs=bofs, lm=n["ld"]):
            v.wait_ge(ld, lm)
            if fb < 2:
                v.wait_ge(tx, t_woall)
            if fb >= 2:
                v.wait_ge(tx, w1_g[4 * (fb - 2) + 3])
            V(v, v.tensor_copy(w1b_v[:, bofs:bofs + 4096],
                               w1stage[:, (fb % 2) * 4096:(fb % 2) * 4096 + 4096]))
        w1cast.append(n["vx"])
        g_w1 = n["vx"]

        for fc in range(4):
            @blk.tensor
            def _(t, fb=fb, fc=fc, bofs=bofs, g_w1=g_w1):
                t.wait_ge(vx, g_w1)
                t.wait_ge(sd, s_h1T)
                if len(gelu_marks) >= 2:
                    t.wait_ge(sx, gelu_marks[-2])
                fcg = fb * 4 + fc
                psl = (P1 if fcg % 2 == 0 else P4)[:, 0:QCAP]
                for c in range(8):
                    lhs = w1b_v[:, bofs + 512 * c + 128 * fc: bofs + 512 * c + 128 * (fc + 1)]
                    inc("tx", t.matmul(psl, lhs, h1T_v[:, c * QCAP:(c + 1) * QCAP],
                                       start=(c == 0), stop=(c == 7), skip_group_check=True))
            t_g = n["tx"]
            w1_g.append(t_g)

            @blk.scalar
            def _(s, fb=fb, fc=fc, t_g=t_g):
                s.wait_ge(tx, t_g)
                fcg = fb * 4 + fc
                psl = (P1 if fcg % 2 == 0 else P4)[:, 0:QCAP]
                SS(s, s.activation(gT_v[:, fcg * QCAP:(fcg + 1) * QCAP], psl,
                                   AF.Gelu_apprx_tanh))
            gelu_marks.append(n["sx"])
    s_gelu = n["sx"]
    t_w1 = n["tx"]

    w2cast = []
    w2_g = []
    for fc in range(32):
        @blk.sync
        def _(s, fc=fc):
            if fc < 2:
                s.wait_ge(vx, w1cast[-1])
            if fc >= 2:
                s.wait_ge(vx, w2cast[fc - 2])
            inc("ld", s.dma_start(out=w1stage[:, (fc % 2) * 1024:(fc % 2) * 1024 + 1024],
                                  in_=w2_d[128 * fc:128 * (fc + 1), :]), 16)

        @blk.vector
        def _(v, fc=fc, lm=n["ld"]):
            v.wait_ge(ld, lm)
            if fc < 2:
                v.wait_ge(tx, t_woall)
            if fc >= 2:
                v.wait_ge(tx, w2_g[fc - 2])
            bofs = (fc % 2) * 1024
            V(v, v.tensor_copy(w2b_v[:, bofs:bofs + 1024],
                               w1stage[:, (fc % 2) * 1024:(fc % 2) * 1024 + 1024]))
        w2cast.append(n["vx"])
        g_w2 = n["vx"]

        @blk.tensor
        def _(t, fc=fc, g_w2=g_w2):
            t.wait_ge(vx, g_w2)
            t.wait_ge(sx, gelu_marks[fc])
            bofs = (fc % 2) * 1024
            for sub in range(3):
                psl = P2[:, 1024 * (sub % 2):1024 * (sub % 2) + 1024] if sub < 2 else P3[:, 0:1024]
                lhs = gT_v[:, fc * QCAP + 128 * sub: fc * QCAP + 128 * (sub + 1)]
                for j in range(2):
                    inc("tx", t.matmul(psl[:, 512 * j:512 * (j + 1)], lhs,
                                       w2b_v[:, bofs + 512 * j:bofs + 512 * (j + 1)],
                                       start=(fc == 0), stop=(fc == 31), skip_group_check=True))
        w2_g.append(n["tx"])
    t_w2 = n["tx"]

    corr_v = []
    for sub in range(3):
        @blk.vector
        def _(v, sub=sub):
            v.wait_ge(tx, t_w2)
            psl = P2[:, 1024 * (sub % 2):1024 * (sub % 2) + 1024] if sub < 2 else P3[:, 0:1024]
            xt = x1_v[:, sub * D:(sub + 1) * D]
            V(v, v.tensor_tensor(xt, xt, psl, op=ALU.add))
            V(v, v.tensor_scalar(xt, xt, wself[:, sub:sub + 1], None, op0=ALU.mult))
            V(v, v.tensor_tensor(xt, xt, xnq_v[:, sub * D:(sub + 1) * D], op=ALU.subtract))
            V(v, v.tensor_scalar(xt, xt, qkeep[:, sub:sub + 1], None, op0=ALU.mult))
        corr_v.append(n["vx"])
    v_corr = n["vx"]

    @blk.gpsimd
    def _(g):
        g.wait_ge(vx, v_corr)
        g.wait_ge(wr, w_base)
        for sub in range(3):
            inc("ws", g.indirect_dma_start(out_d, IndirectOffsetOnAxis(ap=qrho_i[:, sub:sub + 1], axis=0),
                                           x1_v[:, sub * D:(sub + 1) * D], None, compute_op=ALU.add), 16)
    w_all = n["ws"]

    @blk.sync
    def _(s):
        s.wait_ge(wr, w_base)
        s.wait_ge(ws, w_all)
        s.wait_ge(gd, n["gd"])
        s.wait_ge(sd, n["sd"])
        s.wait_ge(ld, n["ld"])
        s.wait_ge(vx, n["vx"])
        s.wait_ge(sx, n["sx"])
        s.wait_ge(tx, n["tx"])
        s.wait_ge(gx, n["gx"])
        s.wait_ge(cc, n["cc"])

    es.close()
    return nc


def _host_constants(s):
    g8 = np.zeros((128, 8), np.float32)
    for p in range(128):
        g8[p, p // 16] = 1.0
    lt = ((np.arange(128)[:, None]) < (np.arange(128)[None, :])).astype(np.float32)
    onesr = np.ones((1, 128), np.float32)
    riota = np.stack([128.0 * c + np.arange(128) for c in range(8)], 1).astype(np.float32)
    dc = np.zeros((128, ROUNDS), np.float32)
    for r in range(ROUNDS):
        d = D0 / (NTHR ** r)
        for p in range(128):
            dc[p, r] = np.float32(d * ((p // 16) + 1) / NTHR)
    sc = np.zeros((128, 8), np.float32)
    sc[:, 0] = 256.0 * s
    sc[:, 1] = 128.0 * s
    sc[:, 2] = float(s)
    sc[:, 4] = 1e-5
    sc[:, 5] = 1e-6
    sc[:, 6] = EXPB
    member = np.array([1.0 if ((p % 8) // 2) == s else 0.0 for p in range(128)], np.float32).reshape(128, 1)
    return g8, lt, onesr, riota, dc, sc, member


_NC_CACHE = [None]


def kernel(**inputs):
    x = np.ascontiguousarray(np.asarray(inputs["L0_attn_H"], np.float32))
    Wqkv = np.asarray(inputs["L0_attn_W_qkv"], np.float32)
    bqkv = np.asarray(inputs["L0_attn_b_qkv"], np.float32).reshape(1, 3 * D)
    Wproj = np.asarray(inputs["L0_attn_W_proj"], np.float32)
    bproj = np.asarray(inputs["L0_attn_b_proj"], np.float32).reshape(1, D)
    gamma = np.asarray(inputs["L0_attn_gamma"], np.float32).reshape(1, D)
    beta = np.asarray(inputs["L0_attn_beta"], np.float32).reshape(1, D)
    mask = np.asarray(inputs["L0_ffn_attention_mask"], np.float32)
    pos = np.asarray(inputs["L0_ffn_position_ids"], np.int32)
    rw = np.asarray(inputs["L0_ffn_router_w"], np.float32)
    rb = np.asarray(inputs["L0_ffn_router_b"], np.float32).reshape(1, 1)
    wq = np.ascontiguousarray(np.asarray(inputs["L0_ffn_wq"], np.float32))
    wk = np.ascontiguousarray(np.asarray(inputs["L0_ffn_wk"], np.float32))
    wv = np.ascontiguousarray(np.asarray(inputs["L0_ffn_wv"], np.float32))
    wo = np.ascontiguousarray(np.asarray(inputs["L0_ffn_wo"], np.float32))
    w1 = np.ascontiguousarray(np.asarray(inputs["L0_ffn_w1"], np.float32))
    w2 = np.ascontiguousarray(np.asarray(inputs["L0_ffn_w2"], np.float32))
    ln1 = np.asarray(inputs["L0_ffn_ln1"], np.float32).reshape(1, D)
    ln2 = np.asarray(inputs["L0_ffn_ln2"], np.float32).reshape(1, D)

    wqkvT = np.ascontiguousarray(Wqkv.T)
    wprojT = np.ascontiguousarray(Wproj.T)
    rw8 = np.ascontiguousarray(rw.reshape(8, 128).T)

    if _NC_CACHE[0] is None:
        _NC_CACHE[0] = build_nc()
    nc = _NC_CACHE[0]

    srows = {}
    in_maps = []
    for core in range(8):
        b = core // 4
        s = core % 4
        g8, lt, onesr, riota, dc, sc, member = _host_constants(s)
        rows = np.array([512 * h + 128 * s + r for h in range(H) for r in range(128)], np.int64)
        srows[core] = rows
        im = {
            "x_T": np.ascontiguousarray(x[b, 2048 * s:2048 * (s + 1)]),
            "x_S": np.ascontiguousarray(x[b][rows]),
            "x_full": x[b],
            "mask_f": np.ascontiguousarray(mask[b, 0]),
            "pos_f": np.ascontiguousarray(pos[b].reshape(1, 4096)),
            "wqkvT": wqkvT, "bqkv": bqkv, "wprojT": wprojT, "bproj": bproj,
            "gamma_d": gamma, "beta_d": beta,
            "rw_d": rw8, "rb_d": rb,
            "wq_d": wq, "wk_d": wk, "wv_d": wv, "wo_d": wo, "w1_d": w1, "w2_d": w2,
            "ln1_d": ln1, "ln2_d": ln2,
            "g8_d": g8, "lt_d": lt, "onesr_d": onesr, "riota_d": riota,
            "dc_d": dc, "sc_d": sc, "member_d": member,
            "bqq_d": np.ascontiguousarray(bqkv[0, 0:D].reshape(8, 128).T),
        }
        in_maps.append(im)

    res = run_bass_kernel_spmd(nc, in_maps, core_ids=list(range(8)))
    out = np.empty((B, N, D), np.float32)
    for core in range(8):
        b = core // 4
        out[b][srows[core]] = res.results[core]["out_sh"][:2048]
    return out


# version 04
# speedup vs baseline: 1.0266x; 1.0219x over previous
import math
import numpy as np
import concourse.bass as bass
import concourse.mybir as mybir
from concourse.bass import IndirectOffsetOnAxis
from concourse.bass_utils import run_bass_kernel_spmd

F32 = mybir.dt.float32
BF16 = mybir.dt.bfloat16
I32 = mybir.dt.int32
ALU = mybir.AluOpType
AF = mybir.ActivationFunctionType
AXL = mybir.AxisListType

B, N, D, H, HD = 2, 8192, 1024, 16, 64
TOK, NTS = 2048, 16
QCAP, KCAP, KSL = 384, 1024, 256
ROUNDS, NTHR = 11, 9
LO0, D0 = -64.0, 128.0
PI = math.pi
TP = 2 * PI
C1 = 6.283203125
C2 = -1.7881393432617188e-05
C3 = 6.357301884918343e-08
LNI = math.log(10000.0) / 32.0
LNI_HI = float(np.float32(LNI))
LNI_LO = float(np.float32(LNI - LNI_HI))
EXPB = -20.0
GROUPS = [[0, 1, 2, 3], [4, 5, 6, 7]]


def build_nc():
    from contextlib import ExitStack
    nc = bass.Bass("TRN2", target_bir_lowering=False, debug=False)
    es = ExitStack()

    di = lambda nm, sh, dt=F32: nc.dram_tensor(nm, sh, dt, kind="ExternalInput").ap()
    x_T = di("x_T", [TOK, D])
    x_S = di("x_S", [TOK, D])
    x_full = di("x_full", [N, D])
    mask_f = di("mask_f", [4096, 4096])
    pos_f = di("pos_f", [1, 4096], I32)
    wqkvT_d = di("wqkvT", [D, 3 * D])
    bqkv_d = di("bqkv", [1, 3 * D])
    wprojT_d = di("wprojT", [D, D])
    bproj_d = di("bproj", [1, D])
    gamma_d = di("gamma_d", [1, D])
    beta_d = di("beta_d", [1, D])
    rw_d = di("rw_d", [128, 8])
    rb_d = di("rb_d", [1, 1])
    wq_d = di("wq_d", [D, D])
    wk_d = di("wk_d", [D, D])
    wv_d = di("wv_d", [D, D])
    wo_d = di("wo_d", [D, D])
    w1_d = di("w1_d", [D, 4 * D])
    w2_d = di("w2_d", [4 * D, D])
    ln1_d = di("ln1_d", [1, D])
    ln2_d = di("ln2_d", [1, D])
    g8_d = di("g8_d", [128, 8])
    lt_d = di("lt_d", [128, 128])
    onesr_d = di("onesr_d", [1, 128])
    riota_d = di("riota_d", [128, 8])
    dc_d = di("dc_d", [128, ROUNDS])
    sc_d = di("sc_d", [128, 8])
    member_d = di("member_d", [128, 1])
    bqq_d = di("bqq_d", [128, 8])

    out_d = nc.dram_tensor("out_sh", [TOK + 1, D], F32, kind="ExternalOutput").ap()

    dsc = lambda nm, sh, dt=F32: nc.dram_tensor(nm, sh, dt)
    w_ag_in = dsc("w_ag_in", [1, TOK])
    w_ag_out = dsc("w_ag_out", [4, TOK])
    kv_ar_in = dsc("kv_ar_in", [128, 520])
    kv_ar_out = dsc("kv_ar_out", [128, 520])
    kvf_ag_in = dsc("kvf_ag_in", [KSL, 2 * D], BF16)
    kvf_ag_out = dsc("kvf_ag_out", [KCAP, 2 * D], BF16)
    s_scr = dsc("s_scr", [1, N])
    scr_dram = dsc("scr_dram", [TOK, D], BF16)
    m_scr1 = dsc("m_scr1", [QCAP, 4096], BF16)
    m_scr2 = dsc("m_scr2", [4097, QCAP], BF16)
    t_scr = dsc("t_scr", [QCAP, D], BF16)
    tk_scr = dsc("tk_scr", [KSL, D], BF16)

    sbt = lambda nm, sh, dt=F32: es.enter_context(nc.sbuf_tensor(nm, sh, dt))
    pst = lambda nm, sh, dt=F32: es.enter_context(nc.psum_tensor(nm, sh, dt))

    c_g8 = sbt("c_g8", [128, 8])
    c_lt = sbt("c_lt", [128, 128])
    c_ones = sbt("c_ones", [1, 128])
    c_onesb = sbt("c_onesb", [1, 128], BF16)
    c_riota = sbt("c_riota", [128, 8])
    c_dc = sbt("c_dc", [128, ROUNDS])
    c_sc = sbt("c_sc", [128, 8])
    c_member = sbt("c_member", [128, 1])
    c_bqkv = sbt("c_bqkv", [1, 3 * D], BF16)
    c_bqq = sbt("c_bqq", [128, 8])
    c_bqqn = sbt("c_bqqn", [128, 8])
    c_bproj = sbt("c_bproj", [1, D], BF16)
    c_rw = sbt("c_rw", [128, 8])
    c_rw_hi = sbt("c_rw_hi", [128, 8], BF16)
    c_rw_lo = sbt("c_rw_lo", [128, 8], BF16)
    c_rb = sbt("c_rb", [1, 1])
    gam_rep = sbt("gam_rep", [128, D])
    bet_rep = sbt("bet_rep", [128, D])
    ln1_rep = sbt("ln1_rep", [128, D])
    ln2_rep = sbt("ln2_rep", [128, D])
    inv_rep = sbt("inv_rep", [128, 32])

    xt_buf = sbt("xt_buf", [128, 2 * D])
    stat6 = sbt("stat6", [128, 12])
    stat2 = sbt("stat2", [128, 2])
    negm = sbt("negm", [128, 1])
    sqv = sbt("sqv", [128, 1])
    rstd = sbt("rstd", [128, 1])
    arenaA = sbt("arenaA", [128, 8 * TOK], BF16)
    arenaB = sbt("arenaB", [128, 8 * TOK], BF16)
    A3 = sbt("A3", [128, 16384], BF16)
    A4 = sbt("A4", [128, 8192])
    A5 = sbt("A5", [128, 4096])
    A7 = sbt("A7", [128, 8192], BF16)
    xn_f32 = A5[:, 0:D]
    xn_bf = A7[:, 0:D]
    lob_bf = A7[:, D:2 * D]
    loT_t = A7[:, 2 * D:4 * D]
    kphi_m = A5[:, 0:D]
    kphi_e = A5[:, D:2 * D]
    qphi_m = A5[:, 0:512]
    qphi_e = A5[:, 512:1024]
    qphi_t = A5[:, 1024:1536]
    u_t = A5[:, 0:D]
    xns2 = A5[:, D:2 * D]
    acc1 = A5[:, 2 * D:3 * D]
    acc2 = A5[:, 3 * D:4 * D]
    outrow = A5[:, D:2 * D]
    S_half = A4[:, 0:4096]
    S_junk = A4[:, 4096:8192]

    w_sb = A5[0:1, 0:TOK]
    rowb = A5[0:1, 2048:3072]
    cj512 = A4[:, 4096:4608]
    w_rep = A4[:, 4608:5120]
    w_nat = A4[:, 5120:5184]
    rrec = A5[0:1, 0:QCAP]
    rrep_sb = A5[0:64, 512:512 + QCAP]
    G_bf = A7[:, 0:D]
    hn_q = A7[:, 0:3 * D]
    hn_k = A7[:, 3 * D:5 * D]
    rop_a = A5[:, 2 * D:3 * D]
    rop_h1 = A5[:, 3 * D:4 * D]
    xnq_v = A4[:, 0:3 * D]
    xnk_v = A4[:, 3 * D:5 * D]
    qp_v = A4[:, 5 * D:8 * D]
    x1_v = A5[:, 0:3 * D]
    mga_v = A5[:, 0:4096]
    qr_v = A7[:, 0:3 * D]
    kr_v = A7[:, 3 * D:5 * D]
    vp_v = A7[:, 5 * D:7 * D]
    h1b_v = A7[:, 0:3 * D]
    kT_v = arenaB[:, 0:8 * KCAP]
    hnTq_v = arenaB[:, 8 * KCAP:8 * KCAP + 8 * QCAP]
    hnTk_v = arenaB[:, 8 * KCAP + 8 * QCAP:8 * KCAP + 8 * QCAP + 8 * KSL]
    expT_v = arenaB[:, 8 * KCAP:8 * KCAP + 8 * QCAP]
    e01T_v = arenaB[:, 8 * KCAP + 8 * QCAP:8 * KCAP + 8 * QCAP + 8 * QCAP]
    w1b_v = arenaB[:, 0:8192]
    w2b_v = arenaB[:, 8192:10240]
    vA_v = arenaA[:, 0:8 * 1040]
    qT_v = arenaA[:, 8 * 1040:8 * 1040 + 8 * QCAP]
    oT_v = arenaA[:, 8 * 1040 + 8 * QCAP:8 * 1040 + 16 * QCAP]
    qtz_e = arenaA[:, 8 * 1040 + 16 * QCAP:8 * 1040 + 17 * QCAP]
    qtz_o = arenaA[:, 8 * 1040 + 17 * QCAP:8 * 1040 + 18 * QCAP]
    gT_v = A3[:, 0:32 * QCAP]
    h1T_v = A3[:, 32 * QCAP:32 * QCAP + 8 * QCAP]
    lo_s = sbt("lo_s", [1, 1])
    T_s = sbt("T_s", [128, 1])
    cj8 = sbt("cj8", [1, 8])
    cnt_s = sbt("cnt_s", [128, 1])
    nge_s = sbt("nge_s", [1, 1])
    thr_s = sbt("thr_s", [1, 1])
    keep_s = sbt("keep_s", [128, 64])
    keepq_s = sbt("keepq_s", [128, 64])
    zer64 = sbt("zer64", [128, 64])
    Sp_s = sbt("Sp_s", [128, 64])
    Sqp_s = sbt("Sqp_s", [128, 64])
    tot_s = sbt("tot_s", [128, 1])
    totq_s = sbt("totq_s", [128, 1])
    Sf_s = sbt("Sf_s", [128, 64])
    Sqf_s = sbt("Sqf_s", [128, 64])
    minrow = sbt("minrow", [1, 128])
    mv_s = sbt("mv_s", [128, 64])
    qsel = sbt("qsel", [128, 3])
    ksel = sbt("ksel", [128, 2])
    kself = sbt("kself", [128, 8])
    kkeepf = sbt("kkeepf", [128, 8])
    ckf_i = sbt("ckf_i", [128, 8], I32)
    tmpf8 = sbt("tmpf8", [128, 8])
    mtr = sbt("mtr", [128, QCAP], BF16)
    qkeep = sbt("qkeep", [128, 3])
    kkeep = sbt("kkeep", [128, 2])
    qn_i = sbt("qn_i", [128, 3], I32)
    kn_i = sbt("kn_i", [128, 2], I32)
    cq_i = sbt("cq_i", [128, 3], I32)
    ck_i = sbt("ck_i", [128, 2], I32)
    qrho_i = sbt("qrho_i", [128, 3], I32)
    tmpi = sbt("tmpi", [128, 3], I32)
    tmpf = sbt("tmpf", [128, 3])
    tmpf2 = sbt("tmpf2", [128, 3])
    wself = sbt("wself", [128, 3])
    posq_i = sbt("posq_i", [128, 3], I32)
    posk_i = sbt("posk_i", [128, 2], I32)
    posq_f = sbt("posq_f", [128, 3])
    posk_f = sbt("posk_f", [128, 2])

    k_bf = sbt("k_bf", [128, D], BF16)
    v_bf = sbt("v_bf", [128, H * 65], BF16)
    wkv_v = A3
    kv_sb = A5[:, 0:520]
    wproj_sb = A3[:, 8192:16384]
    kv2x = sbt("kv2x", [128, 1040], BF16)
    recips = sbt("recips", [128, 16])

    cosq = sbt("cosq", [128, 3 * 64])
    sinq = sbt("sinq", [128, 3 * 64])
    cosk = sbt("cosk", [128, 2 * 64])
    sink = sbt("sink", [128, 2 * 64])
    angf = sbt("angf", [128, 32])
    angr = sbt("angr", [128, 32])
    angm = sbt("angm", [128, 32])
    angk_i = sbt("angk_i", [128, 32], I32)
    angkf = sbt("angkf", [128, 32])
    ang2 = sbt("ang2", [128, 32])

    P1 = pst("P1", [128, 512])
    P2 = pst("P2", [128, 2048])
    P3 = pst("P3", [128, 1024])
    P4 = pst("P4", [128, 512])

    ld = es.enter_context(nc.semaphore("ld"))
    sd = es.enter_context(nc.semaphore("sd"))
    gd = es.enter_context(nc.semaphore("gd"))
    vx = es.enter_context(nc.semaphore("vx"))
    sx = es.enter_context(nc.semaphore("sx"))
    tx = es.enter_context(nc.semaphore("tx"))
    gx = es.enter_context(nc.semaphore("gx"))
    cc = es.enter_context(nc.semaphore("cc"))
    wr = es.enter_context(nc.semaphore("wr"))
    ws = es.enter_context(nc.semaphore("ws"))
    n = {"ld": 0, "sd": 0, "gd": 0, "vx": 0, "sx": 0, "tx": 0, "gx": 0, "cc": 0, "wr": 0, "ws": 0}
    sems = {"ld": ld, "sd": sd, "gd": gd, "vx": vx, "sx": sx, "tx": tx, "gx": gx, "cc": cc, "wr": wr, "ws": ws}

    def inc(key, inst, k=1):
        inst.then_inc(sems[key], k)
        n[key] += k
        return n[key]

    blk = es.enter_context(nc.Block())

    def V(v, inst):
        inc("vx", inst)
        v.wait_ge(vx, n["vx"])

    def SS(s, inst):
        inc("sx", inst)
        s.wait_ge(sx, n["sx"])

    @blk.sync
    def _(s):
        for a, b_ in [(c_g8, g8_d), (c_lt, lt_d), (c_ones, onesr_d), (c_riota, riota_d),
                      (c_dc, dc_d), (c_sc, sc_d), (c_member, member_d), (c_rw, rw_d),
                      (c_rb, rb_d)]:
            inc("ld", s.dma_start(out=a[:, :], in_=b_), 16)
        inc("ld", s.dma_start(out=c_bqq[:, :], in_=bqq_d), 16)
    l_consts = n["ld"]

    @blk.gpsimd
    def _(g):
        inc("gd", g.dma_start(out=c_bqkv[:, :], in_=bqkv_d), 16)
        inc("gd", g.dma_start(out=c_bproj[:, :], in_=bproj_d), 16)
        inc("gd", g.dma_start(out=c_rw_hi[:, :], in_=rw_d), 16)
        inc("gx", g.iota(angf[:, :], pattern=[[1, 32]], base=0, channel_multiplier=0,
                         allow_small_or_imprecise_dtypes=True))
    g_consts = n["gd"]
    g_iota = n["gx"]

    bstage = arenaB[:, :].bitcast(F32)
    a3kv_casts = []
    for c in range(8):
        @blk.sync
        def _(s, c=c):
            if c >= 2:
                s.wait_ge(vx, a3kv_casts[c - 2])
            inc("ld", s.dma_start(out=bstage[:, (c % 2) * 2048:(c % 2) * 2048 + 2048],
                                  in_=wqkvT_d[128 * c:128 * (c + 1), D:3 * D]), 16)

        @blk.vector
        def _(v, c=c, lm=n["ld"]):
            v.wait_ge(ld, lm)
            V(v, v.tensor_copy(A3[:, c * 2 * D:(c + 1) * 2 * D],
                               bstage[:, (c % 2) * 2048:(c % 2) * 2048 + 2048]))
        a3kv_casts.append(n["vx"])
    v_a3kv = n["vx"]

    @blk.vector
    def _(v):
        v.wait_ge(ld, l_consts)
        v.wait_ge(gd, g_consts)
        v.wait_ge(gx, g_iota)
        V(v, v.memset(c_onesb[:, :], 1.0))
        V(v, v.memset(lo_s[:, :], LO0))
        V(v, v.memset(zer64[:, :], 0.0))
        V(v, v.tensor_tensor(cj512[:, 0:8], c_rw[:, :], c_rw_hi[:, :], op=ALU.subtract))
        V(v, v.tensor_copy(c_rw_lo[:, :], cj512[:, 0:8]))
        V(v, v.tensor_scalar(c_bqqn[:, :], c_bqq[:, :], -1.0, None, op0=ALU.mult))
        V(v, v.tensor_scalar(angr[:, :], angf[:, :], -LNI_HI, None, op0=ALU.mult))
        V(v, v.tensor_scalar(angm[:, :], angf[:, :], -LNI_LO, None, op0=ALU.mult))
        V(v, v.tensor_tensor(angr[:, :], angr[:, :], angm[:, :], op=ALU.add))
    v_init = n["vx"]

    @blk.scalar
    def _(s):
        s.wait_ge(vx, v_init)
        SS(s, s.activation(inv_rep[:, :], angr[:, :], AF.Exp))
    s_inv = n["sx"]

    for src_d, dst in [(gamma_d, gam_rep), (beta_d, bet_rep), (ln1_d, ln1_rep), (ln2_d, ln2_rep)]:
        @blk.sync
        def _(s, src_d=src_d, vv=n["vx"]):
            s.wait_ge(vx, vv)
            inc("ld", s.dma_start(out=rowb[:, :], in_=src_d), 16)
        l_row = n["ld"]
        for half in range(2):
            @blk.tensor
            def _(t, l_row=l_row, half=half, vv=n["vx"]):
                t.wait_ge(ld, l_row)
                t.wait_ge(vx, max(vv, v_init))
                inc("tx", t.matmul(P1[:, :], c_ones[:, :], rowb[:, 512 * half:512 * (half + 1)],
                                   start=True, stop=True))
            t_h = n["tx"]

            @blk.vector
            def _(v, dst=dst, half=half, t_h=t_h):
                v.wait_ge(tx, t_h)
                V(v, v.tensor_copy(dst[:, 512 * half:512 * (half + 1)], P1[:, :]))
    v_reps = n["vx"]

    tile_ld = []

    @blk.sync
    def _(s):
        for t in range(2):
            inc("ld", s.dma_start(out=xt_buf[:, (t % 2) * D:(t % 2 + 1) * D],
                                  in_=x_T[128 * t:128 * (t + 1), :]), 16)
            tile_ld.append(n["ld"])

    s_tr_done = []
    rt_done = []
    for t in range(NTS):
        buf = (t % 2) * D

        @blk.vector
        def _(v, t=t, buf=buf):
            v.wait_ge(ld, tile_ld[t])
            if t == 0:
                v.wait_ge(vx, v_reps)
            xt = xt_buf[:, buf:buf + D]
            V(v, v.bn_stats(stat6[:, 0:6], (xt)[:, 0:512]))
            V(v, v.bn_stats(stat6[:, 6:12], (xt)[:, 512:1024]))
            V(v, v.bn_aggr(stat2[:, :], stat6[:, :]))
            V(v, v.tensor_scalar(negm[:, :], stat2[:, 0:1], -1.0, None, op0=ALU.mult))
        v_st = n["vx"]

        @blk.scalar
        def _(s, v_st=v_st):
            s.wait_ge(vx, v_st)
            SS(s, s.activation(sqv[:, :], stat2[:, 1:2], AF.Sqrt, bias=c_sc[:, 4:5]))
        s_sq = n["sx"]

        xnb_t = A7[:, (t % 2) * 4 * D:(t % 2) * 4 * D + D]
        lob_t = A7[:, (t % 2) * 4 * D + D:(t % 2) * 4 * D + 2 * D]

        @blk.vector
        def _(v, t=t, buf=buf, s_sq=s_sq, xnb_t=xnb_t, lob_t=lob_t):
            v.wait_ge(sx, s_sq)
            if t >= 2:
                v.wait_ge(sd, s_tr_done[t - 2])
            V(v, v.reciprocal(rstd[:, :], sqv[:, :]))
            xt = xt_buf[:, buf:buf + D]
            V(v, v.tensor_scalar(xn_f32[:, :], xt, negm[:, :], rstd[:, :], op0=ALU.add, op1=ALU.mult))
            V(v, v.tensor_copy(xnb_t[:, :], xn_f32[:, :]))
            V(v, v.tensor_tensor(lob_t[:, :], xn_f32[:, :], xnb_t[:, :], op=ALU.subtract))
        v_xn = n["vx"]

        @blk.scalar
        def _(s, t=t, v_xn=v_xn, xnb_t=xnb_t, lob_t=lob_t):
            s.wait_ge(vx, v_xn)
            if t >= 2:
                s.wait_ge(tx, rt_done[t - 2])
            dst = arenaA[:, :].rearrange("p (c k) -> p c k", k=TOK)[:, :, 128 * t:128 * (t + 1)]
            inc("sd", s.dma_start_transpose(dst, xnb_t[:, :]), 16)
            dst2 = loT_t[:, (t % 2) * D:(t % 2 + 1) * D].rearrange("p (c k) -> p c k", k=128)
            inc("sd", s.dma_start_transpose(dst2, lob_t[:, :]), 16)
        s_tr_done.append(n["sd"])

        @blk.tensor
        def _(tt, t=t, s_tr=s_tr_done[t]):
            tt.wait_ge(sd, s_tr)
            for c in range(8):
                hi = arenaA[:, :].rearrange("p (c k) -> p c k", k=TOK)[:, c, 128 * t:128 * (t + 1)]
                lo_ = loT_t[:, (t % 2) * D:(t % 2 + 1) * D].rearrange("p (c k) -> p c k", k=128)[:, c, :]
                inc("tx", tt.matmul(P2[0:1, 128 * t:128 * (t + 1)], c_rw_hi[:, c:c + 1], hi,
                                    start=(c == 0), stop=False, skip_group_check=True))
                inc("tx", tt.matmul(P2[0:1, 128 * t:128 * (t + 1)], c_rw_hi[:, c:c + 1], lo_,
                                    start=False, stop=False, skip_group_check=True))
                inc("tx", tt.matmul(P2[0:1, 128 * t:128 * (t + 1)], c_rw_lo[:, c:c + 1], hi,
                                    start=False, stop=(c == 7), skip_group_check=True))
        rt_done.append(n["tx"])

        if t + 2 < NTS:
            @blk.sync
            def _(s, t=t, v_xn=v_xn):
                s.wait_ge(vx, v_xn)
                inc("ld", s.dma_start(out=xt_buf[:, (t % 2) * D:(t % 2 + 1) * D],
                                      in_=x_T[128 * (t + 2):128 * (t + 3), :]), 16)
                tile_ld.append(n["ld"])
    t_router = n["tx"]

    @blk.vector
    def _(v):
        v.wait_ge(tx, t_router)
        V(v, v.tensor_scalar(w_sb[:, :], P2[0:1, :], c_rb[:, :], None, op0=ALU.add))
    v_w = n["vx"]

    @blk.sync
    def _(s):
        s.wait_ge(vx, v_w)
        inc("ld", s.dma_start(out=w_ag_in.ap(), in_=w_sb[:, :]), 16)
    l_wag = n["ld"]

    @blk.gpsimd
    def _(g):
        g.wait_ge(ld, l_wag)
        inc("cc", g.collective_compute("AllGather", ALU.bypass, replica_groups=GROUPS,
                                       ins=[w_ag_in.ap().opt()], outs=[w_ag_out.ap().opt()]))
    c_w = n["cc"]

    wflat = w_ag_out.ap().rearrange("a b -> (a b)").rearrange("(x m) -> x m", x=1)

    @blk.sync
    def _(s):
        s.wait_ge(cc, c_w)
        for k in range(8):
            inc("ld", s.dma_start(out=w_rep[16 * k:16 * (k + 1), :],
                                  in_=wflat.rearrange("x (a b) -> (x a) b", a=16)), 16)
        inc("ld", s.dma_start(out=w_nat[:, :], in_=wflat.rearrange("x (a b) -> (x a) b", a=128)), 16)
    l_wload = n["ld"]

    for r in range(ROUNDS):
        v_now = n["vx"]

        @blk.tensor
        def _(t, v_now=v_now, r=r):
            t.wait_ge(vx, v_now)
            if r == 0:
                t.wait_ge(ld, l_wload)
            inc("tx", t.matmul(P1[:, 0:1], c_ones[:, :], lo_s[:, :], start=True, stop=True))
        t_lo = n["tx"]

        @blk.vector
        def _(v, r=r, t_lo=t_lo):
            v.wait_ge(tx, t_lo)
            if r == 0:
                v.wait_ge(ld, l_wload)
            V(v, v.tensor_tensor(T_s[:, :], P1[:, 0:1], c_dc[:, r:r + 1], op=ALU.add))
            V(v, v.tensor_scalar(cj512[:, :], w_rep[:, :], T_s[:, :], None, op0=ALU.is_gt))
            V(v, v.tensor_reduce(cnt_s[:, :], cj512[:, :], axis=AXL.X, op=ALU.add))
        v_cnt = n["vx"]

        @blk.tensor
        def _(t, v_cnt=v_cnt):
            t.wait_ge(vx, v_cnt)
            inc("tx", t.matmul(P1[0:1, 8:16], cnt_s[:, :], c_g8[:, :], start=True, stop=True))
        t_c8 = n["tx"]

        @blk.vector
        def _(v, r=r, t_c8=t_c8):
            v.wait_ge(tx, t_c8)
            V(v, v.tensor_scalar(cj8[:, :], P1[0:1, 8:16], 1024.0, None, op0=ALU.is_ge))
            V(v, v.tensor_reduce(nge_s[:, :], cj8[:, :], axis=AXL.X, op=ALU.add))
            V(v, v.tensor_scalar(lo_s[:, :], nge_s[:, :], D0 / (NTHR ** (r + 1)), lo_s[:, :],
                                 op0=ALU.mult, op1=ALU.add))
    v_bis = n["vx"]

    @blk.tensor
    def _(t):
        t.wait_ge(vx, v_bis)
        inc("tx", t.matmul(P1[:, 0:1], c_ones[:, :], lo_s[:, :], start=True, stop=True))
    t_lof = n["tx"]

    @blk.vector
    def _(v):
        v.wait_ge(tx, t_lof)
        V(v, v.tensor_scalar(keep_s[:, :], w_nat[:, :], P1[:, 0:1], None, op0=ALU.is_gt))
        V(v, v.tensor_tensor(mv_s[:, :], w_nat[:, :], keep_s[:, :], op=ALU.mult))
        V(v, v.tensor_scalar(keepq_s[:, :], keep_s[:, :], -1e30, 1e30, op0=ALU.mult, op1=ALU.add))
        V(v, v.tensor_tensor(mv_s[:, :], mv_s[:, :], keepq_s[:, :], op=ALU.add))
        V(v, v.tensor_reduce(tot_s[:, :], mv_s[:, :], axis=AXL.X, op=ALU.min))
    v_min = n["vx"]

    @blk.sync
    def _(s):
        s.wait_ge(vx, v_min)
        inc("ld", s.dma_start(out=s_scr[:, 0:128].rearrange("x (a b) -> (x a) b", a=128), in_=tot_s[:, :]), 16)
        s.wait_ge(ld, n["ld"])
        inc("ld", s.dma_start(out=minrow[:, :], in_=s_scr[:, 0:128]), 16)
    l_minr = n["ld"]

    @blk.vector
    def _(v):
        v.wait_ge(ld, l_minr)
        V(v, v.tensor_reduce(thr_s[:, :], minrow[:, :], axis=AXL.X, op=ALU.min))
    v_thr = n["vx"]

    @blk.tensor
    def _(t):
        t.wait_ge(vx, v_thr)
        inc("tx", t.matmul(P1[:, 0:1], c_ones[:, :], thr_s[:, :], start=True, stop=True))
    t_thr = n["tx"]

    @blk.vector
    def _(v):
        v.wait_ge(tx, t_thr)
        V(v, v.tensor_scalar(keep_s[:, :], w_nat[:, :], P1[:, 0:1], None, op0=ALU.is_gt))
        V(v, v.tensor_tensor(keepq_s[:, :], keep_s[:, :], c_member[:, :].to_broadcast([128, 64]), op=ALU.mult))
        V(v, v.tensor_tensor_scan(Sp_s[:, :], keep_s[:, :], zer64[:, :], 0.0, op0=ALU.add, op1=ALU.add))
        V(v, v.tensor_tensor_scan(Sqp_s[:, :], keepq_s[:, :], zer64[:, :], 0.0, op0=ALU.add, op1=ALU.add))
        V(v, v.tensor_copy(tot_s[:, :], Sp_s[:, 63:64]))
        V(v, v.tensor_copy(totq_s[:, :], Sqp_s[:, 63:64]))
    v_scan = n["vx"]

    @blk.tensor
    def _(t):
        t.wait_ge(vx, v_scan)
        inc("tx", t.matmul(P1[:, 1:2], c_lt[:, :], tot_s[:, :], start=True, stop=True))
    t_off = n["tx"]

    @blk.vector
    def _(v):
        v.wait_ge(tx, t_off)
        V(v, v.tensor_scalar(Sf_s[:, :], Sp_s[:, :], P1[:, 1:2], None, op0=ALU.add))
    v_sf = n["vx"]

    @blk.tensor
    def _(t):
        t.wait_ge(vx, v_sf)
        inc("tx", t.matmul(P1[:, 1:2], c_lt[:, :], totq_s[:, :], start=True, stop=True))
    t_offq = n["tx"]

    @blk.vector
    def _(v):
        v.wait_ge(tx, t_offq)
        V(v, v.tensor_scalar(Sqf_s[:, :], Sqp_s[:, :], P1[:, 1:2], None, op0=ALU.add))
    v_sqf = n["vx"]

    ksel_parts = []
    for half in range(2):
        @blk.sync
        def _(s, half=half, vv=n["vx"]):
            s.wait_ge(vx, max(vv, v_sf))
            inc("ld", s.dma_start(out=s_scr.rearrange("x (a b) -> (x a) b", a=128), in_=Sf_s[:, :]), 16)
            s.wait_ge(ld, n["ld"])
            for k in range(4):
                inc("ld", s.dma_start(out=S_half[32 * k:32 * (k + 1), :],
                                      in_=s_scr[:, 4096 * half:4096 * (half + 1)].to_broadcast([32, 4096])), 16)
        l_h = n["ld"]

        @blk.vector
        def _(v, half=half, l_h=l_h):
            v.wait_ge(ld, l_h)
            for c in range(8):
                V(v, v.tensor_scalar(S_junk[:, :], S_half[:, :], c_riota[:, c:c + 1], None, op0=ALU.is_le))
                dst = kself[:, c:c + 1] if half == 0 else cnt_s[:, :]
                V(v, v.tensor_reduce(dst, S_junk[:, :], axis=AXL.X, op=ALU.add))
                if half == 1:
                    V(v, v.tensor_tensor(kself[:, c:c + 1], kself[:, c:c + 1], cnt_s[:, :], op=ALU.add))
            for c in range(2):
                V(v, v.tensor_tensor(T_s[:, :], c_riota[:, c:c + 1], c_sc[:, 0:1], op=ALU.add))
                V(v, v.tensor_scalar(S_junk[:, :], S_half[:, :], T_s[:, :], None, op0=ALU.is_le))
                dst = ksel[:, c:c + 1] if half == 0 else cnt_s[:, :]
                V(v, v.tensor_reduce(dst, S_junk[:, :], axis=AXL.X, op=ALU.add))
                if half == 1:
                    V(v, v.tensor_tensor(ksel[:, c:c + 1], ksel[:, c:c + 1], cnt_s[:, :], op=ALU.add))
    @blk.vector
    def _(v):
        V(v, v.tensor_scalar(kkeep[:, :], ksel[:, :], float(N), None, op0=ALU.is_lt))
        V(v, v.tensor_scalar(ksel[:, :], ksel[:, :], float(N - 1), None, op0=ALU.min))
        V(v, v.tensor_scalar(kkeepf[:, :], kself[:, :], float(N), None, op0=ALU.is_lt))
        V(v, v.tensor_scalar(kself[:, :], kself[:, :], float(N - 1), None, op0=ALU.min))
        V(v, v.tensor_scalar(tmpf8[:, :], kself[:, :], 4095.0, None, op0=ALU.min))
        V(v, v.tensor_tensor(tmpf8[:, :], tmpf8[:, :], kkeepf[:, :], op=ALU.mult))
        V(v, v.tensor_scalar(cj8[0:1, :], kkeepf[0:1, :], 0.0, None, op0=ALU.mult))
        V(v, v.tensor_scalar(cnt_s[:, 0:1], kkeepf[:, 0:1], 0.0, None, op0=ALU.mult))
        V(v, v.tensor_scalar(kkeepf[:, :], kkeepf[:, :], -4096.0, 4096.0, op0=ALU.mult, op1=ALU.add))
        V(v, v.tensor_tensor(tmpf8[:, :], tmpf8[:, :], kkeepf[:, :], op=ALU.add))
    v_ksel = n["vx"]

    for half in range(2):
        @blk.sync
        def _(s, half=half, vv=n["vx"]):
            s.wait_ge(vx, max(vv, v_sqf))
            inc("ld", s.dma_start(out=s_scr.rearrange("x (a b) -> (x a) b", a=128), in_=Sqf_s[:, :]), 16)
            s.wait_ge(ld, n["ld"])
            for k in range(4):
                inc("ld", s.dma_start(out=S_half[32 * k:32 * (k + 1), :],
                                      in_=s_scr[:, 4096 * half:4096 * (half + 1)].to_broadcast([32, 4096])), 16)
        l_h = n["ld"]

        @blk.vector
        def _(v, half=half, l_h=l_h):
            v.wait_ge(ld, l_h)
            for c in range(3):
                V(v, v.tensor_scalar(S_junk[:, :], S_half[:, :], c_riota[:, c:c + 1], None, op0=ALU.is_le))
                dst = qsel[:, c:c + 1] if half == 0 else cnt_s[:, :]
                V(v, v.tensor_reduce(dst, S_junk[:, :], axis=AXL.X, op=ALU.add))
                if half == 1:
                    V(v, v.tensor_tensor(qsel[:, c:c + 1], qsel[:, c:c + 1], cnt_s[:, :], op=ALU.add))
    @blk.vector
    def _(v):
        V(v, v.tensor_scalar(qkeep[:, :], qsel[:, :], float(N), None, op0=ALU.is_lt))
        V(v, v.tensor_scalar(qsel[:, :], qsel[:, :], float(N - 1), None, op0=ALU.min))
    v_qsel = n["vx"]

    @blk.gpsimd
    def _(g):
        g.wait_ge(vx, max(v_qsel, v_ksel))
        inc("gx", g.tensor_copy(qn_i[:, :], qsel[:, :]))
        inc("gx", g.tensor_copy(kn_i[:, :], ksel[:, :]))
    g_cast1 = n["gx"]

    @blk.vector
    def _(v):
        v.wait_ge(gx, g_cast1)
        V(v, v.tensor_scalar(cq_i[:, :], qn_i[:, :], 4095, None, op0=ALU.min))
        V(v, v.tensor_scalar(ck_i[:, :], kn_i[:, :], 4095, None, op0=ALU.min))
        V(v, v.tensor_scalar(tmpi[:, :], qn_i[:, :], 9, None, op0=ALU.logical_shift_right))
        V(v, v.tensor_scalar(tmpi[:, :], tmpi[:, :], 7, None, op0=ALU.logical_shift_left))
        V(v, v.tensor_scalar(qrho_i[:, :], qn_i[:, :], 511, None, op0=ALU.bitwise_and))
        V(v, v.tensor_tensor(qrho_i[:, :], qrho_i[:, :], tmpi[:, :], op=ALU.add))
    v_der1 = n["vx"]

    @blk.gpsimd
    def _(g):
        g.wait_ge(vx, v_der1)
        inc("gx", g.tensor_copy(ckf_i[:, :], tmpf8[:, :]))
        inc("gx", g.tensor_copy(tmpf2[:, :], qrho_i[:, :]))
    g_cast2 = n["gx"]

    @blk.vector
    def _(v):
        v.wait_ge(gx, g_cast2)
        V(v, v.tensor_scalar(tmpf2[:, :], tmpf2[:, :], c_sc[:, 1:2], None, op0=ALU.subtract))
        V(v, v.tensor_scalar(tmpf2[:, :], tmpf2[:, :], 0.0, 2047.0, op0=ALU.max, op1=ALU.min))
        V(v, v.tensor_scalar(tmpf2[:, :], tmpf2[:, :], float(TOK), None, op0=ALU.subtract))
        V(v, v.tensor_tensor(tmpf2[:, :], tmpf2[:, :], qkeep[:, :], op=ALU.mult))
        V(v, v.tensor_scalar(tmpf2[:, :], tmpf2[:, :], float(TOK), None, op0=ALU.add))
    v_rho = n["vx"]

    @blk.gpsimd
    def _(g):
        g.wait_ge(vx, v_rho)
        inc("gx", g.tensor_copy(qrho_i[:, :], tmpf2[:, :]))
        inc("gx", g.tensor_copy(posq_f[:, 0:1], qrho_i[:, 0:1]))
        for c in range(3):
            inc("gd", g.indirect_dma_start(wself[:, c:c + 1], None, wflat,
                                           IndirectOffsetOnAxis(ap=qn_i[:, c:c + 1], axis=1)), 16)
            inc("gd", g.indirect_dma_start(posq_i[:, c:c + 1], None, pos_f,
                                           IndirectOffsetOnAxis(ap=cq_i[:, c:c + 1], axis=1)), 16)
            inc("gd", g.indirect_dma_start(A4[:, 0:3 * D][:, c * D:(c + 1) * D], None, x_full,
                                           IndirectOffsetOnAxis(ap=qn_i[:, c:c + 1], axis=0)), 16)
        for c in range(2):
            inc("gd", g.indirect_dma_start(posk_i[:, c:c + 1], None, pos_f,
                                           IndirectOffsetOnAxis(ap=ck_i[:, c:c + 1], axis=1)), 16)
            inc("gd", g.indirect_dma_start(A4[:, 3 * D:5 * D][:, c * D:(c + 1) * D], None, x_full,
                                           IndirectOffsetOnAxis(ap=kn_i[:, c:c + 1], axis=0)), 16)
    g_gath = n["gd"]
    g_rhoi = n["gx"]

    kvts_v = []
    kvts_s = []
    for ts in range(NTS):
        @blk.tensor
        def _(t, ts=ts):
            if ts == 0:
                t.wait_ge(sd, s_tr_done[NTS - 1])
                t.wait_ge(gd, g_consts)
                t.wait_ge(vx, max(v_w, v_a3kv))
            else:
                t.wait_ge(vx, kvts_v[ts - 1])
                t.wait_ge(sx, kvts_s[ts - 1])
            for c in range(8):
                xs = arenaA[:, :].rearrange("p (c k) -> p c k", k=TOK)[:, c, 128 * ts:128 * (ts + 1)]
                for j in range(4):
                    rhs = A3[:, c * 2 * D + 512 * j: c * 2 * D + 512 * (j + 1)]
                    inc("tx", t.matmul(P2[:, 512 * j:512 * (j + 1)], xs, rhs,
                                       start=(c == 0), stop=False, skip_group_check=True))
            for j in range(4):
                inc("tx", t.matmul(P2[:, 512 * j:512 * (j + 1)], c_onesb[:, :],
                                   c_bqkv[:, D + 512 * j:D + 512 * (j + 1)], start=False, stop=True,
                                   skip_group_check=True))
        t_kvp = n["tx"]

        @blk.scalar
        def _(s, ts=ts, t_kvp=t_kvp):
            s.wait_ge(tx, t_kvp)
            if ts == 0:
                s.wait_ge(ld, l_wag)
                s.wait_ge(cc, c_w)
            SS(s, s.activation(kphi_m[:, :], P2[:, 0:D], AF.Relu, scale=-1.0))
            SS(s, s.activation(kphi_e[:, :], kphi_m[:, :], AF.Exp, scale=-1.0))
            SS(s, s.activation(kphi_m[:, :], P2[:, 0:D], AF.Relu))
            vdst = v_bf[:, :].rearrange("p (h c) -> p h c", c=65)[:, :, 0:64]
            SS(s, s.activation(vdst, P2[:, D:2 * D].rearrange("p (h c) -> p h c", c=64), AF.Copy))
        kvts_s.append(n["sx"])

        @blk.vector
        def _(v, ts=ts, s_now=kvts_s[ts]):
            v.wait_ge(sx, s_now)
            V(v, v.tensor_tensor(k_bf[:, :], kphi_m[:, :], kphi_e[:, :], op=ALU.add))
            ones_cols = v_bf[:, :].rearrange("p (h c) -> p h c", c=65)[:, :, 64:65]
            V(v, v.memset(ones_cols, 1.0))
        kvts_v.append(n["vx"])

        @blk.tensor
        def _(t, ts=ts, v_now=kvts_v[ts]):
            t.wait_ge(vx, v_now)
            for h in range(H):
                pair = h // 2
                odd = h % 2
                coff = (512 if pair >= 4 else 0) + (pair % 4) * 65
                lhs = k_bf[:, 64 * h:64 * (h + 1)]
                rhs = v_bf[:, 65 * h:65 * (h + 1)]
                tp = (0, 64 * odd) if odd else None
                inc("tx", t.matmul(P3[64 * odd:64 * (odd + 1), coff:coff + 65], lhs, rhs,
                                   start=(ts == 0 and pair % 4 == 0), stop=(ts == NTS - 1),
                                   tile_position=tp, skip_group_check=True))
    t_kvall = n["tx"]

    @blk.vector
    def _(v):
        v.wait_ge(tx, t_kvall)
        V(v, v.tensor_copy(kv_sb[:, 0:260], P3[:, 0:260]))
        V(v, v.tensor_copy(kv_sb[:, 260:520], P3[:, 512:772]))
    v_kvev = n["vx"]

    @blk.sync
    def _(s):
        s.wait_ge(vx, v_kvev)
        inc("ld", s.dma_start(out=kv_ar_in.ap(), in_=kv_sb[:, :]), 16)
    l_kvar = n["ld"]

    @blk.gpsimd
    def _(g):
        g.wait_ge(ld, l_kvar)
        inc("cc", g.collective_compute("AllReduce", ALU.add, replica_groups=GROUPS,
                                       ins=[kv_ar_in.ap().opt()], outs=[kv_ar_out.ap().opt()]))
    c_kv = n["cc"]

    @blk.vector
    def _(v):
        V(v, v.memset(kv2x[64:128, 0:520], 0.0))
        V(v, v.memset(kv2x[0:64, 520:1040], 0.0))
    v_kv2z = n["vx"]

    @blk.gpsimd
    def _(g):
        g.wait_ge(cc, c_kv)
        g.wait_ge(vx, v_kv2z)
        inc("gd", g.dma_start(out=kv2x[0:64, 0:520], in_=kv_ar_out.ap()[0:64, :]), 16)
        inc("gd", g.dma_start(out=kv2x[64:128, 520:1040], in_=kv_ar_out.ap()[64:128, :]), 16)
    g_kv2x = n["gd"]

    qcol_casts = []
    for c in range(8):
        @blk.sync
        def _(s, c=c):
            if c < 2:
                s.wait_ge(tx, t_kvall)
            if c >= 2:
                s.wait_ge(vx, qcol_casts[c - 2])
            inc("ld", s.dma_start(out=bstage[:, (c % 2) * 1024:(c % 2) * 1024 + 1024],
                                  in_=wqkvT_d[128 * c:128 * (c + 1), 0:D]), 16)

        @blk.vector
        def _(v, c=c, lm=n["ld"]):
            v.wait_ge(ld, lm)
            if c < 2:
                v.wait_ge(tx, t_kvall)
            V(v, v.tensor_copy(A3[:, c * D:(c + 1) * D],
                               bstage[:, (c % 2) * 1024:(c % 2) * 1024 + 1024]))
        qcol_casts.append(n["vx"])
    g_wq = n["vx"]

    q_ec_v = []
    for ec in range(8):
        @blk.tensor
        def _(t, ec=ec):
            if ec == 0:
                t.wait_ge(tx, t_kvall)
                t.wait_ge(vx, g_wq)
                t.wait_ge(sx, kvts_s[NTS - 1])
            else:
                t.wait_ge(vx, q_ec_v[ec - 1])
            for c in range(8):
                lhs = A3[:, c * D + 128 * ec: c * D + 128 * (ec + 1)]
                for j in range(4):
                    rhs = arenaA[:, :].rearrange("p (c k) -> p c k", k=TOK)[:, c, 512 * j:512 * (j + 1)]
                    inc("tx", t.matmul(P2[:, 512 * j:512 * (j + 1)], lhs, rhs,
                                       start=(c == 0), stop=(c == 7), skip_group_check=True))
        t_qp = n["tx"]

        for j in range(4):
            @blk.scalar
            def _(s, ec=ec, j=j, t_qp=t_qp):
                s.wait_ge(tx, t_qp)
                if ec == 0 and j == 0:
                    s.wait_ge(ld, l_kvar)
                psl = P2[:, 512 * j:512 * (j + 1)]
                SS(s, s.activation(qphi_m[:, :], psl, AF.Relu, scale=-1.0, bias=c_bqqn[:, ec:ec + 1]))
                SS(s, s.activation(qphi_e[:, :], qphi_m[:, :], AF.Exp, scale=-1.0))
                SS(s, s.activation(qphi_t[:, :], psl, AF.Relu, bias=c_bqq[:, ec:ec + 1]))
            s_qj = n["sx"]

            @blk.vector
            def _(v, ec=ec, j=j, s_qj=s_qj):
                v.wait_ge(sx, s_qj)
                dst = arenaB[:, ec * TOK + 512 * j: ec * TOK + 512 * (j + 1)]
                V(v, v.tensor_tensor(dst, qphi_t[:, :], qphi_e[:, :], op=ALU.add))
        q_ec_v.append(n["vx"])
    t_qall = n["tx"]
    v_qall = n["vx"]

    qo_v = []
    scr_l = []
    for ts in range(NTS):
        Gb = A7[:, (ts % 2) * D:(ts % 2) * D + D]

        @blk.tensor
        def _(t, ts=ts):
            if ts == 0:
                t.wait_ge(vx, v_qall)
                t.wait_ge(gd, g_kv2x)
            else:
                t.wait_ge(vx, qo_v[ts - 1])
            for h in range(H):
                ec = h // 2
                odd = h % 2
                p0 = h // 2
                c0 = 520 * odd + (260 if (p0 >= 4) else 0) + (p0 % 4) * 65
                lhs = arenaB[:, ec * TOK + 128 * ts: ec * TOK + 128 * (ts + 1)]
                rhs = kv2x[:, c0:c0 + 65]
                g4 = h // 4
                off = 512 * g4 + 65 * (h % 4)
                inc("tx", t.matmul(P2[:, off:off + 65], lhs, rhs, start=True, stop=True,
                                   skip_group_check=True))
        t_qo = n["tx"]

        @blk.vector
        def _(v, ts=ts, t_qo=t_qo, Gb=Gb):
            v.wait_ge(tx, t_qo)
            if ts >= 2:
                v.wait_ge(ld, scr_l[ts - 2])
            for g4 in range(4):
                qk1c = P2[:, 512 * g4:512 * g4 + 260].rearrange("p (s c) -> p s c", c=65)[:, :, 64:65]
                V(v, v.tensor_scalar(recips[:, 4 * g4:4 * g4 + 4],
                                     qk1c.rearrange("p s c -> p (s c)"), 1e-6, None, op0=ALU.max))
            V(v, v.reciprocal(recips[:, :], recips[:, :]))
            for g4 in range(4):
                src = P2[:, :].rearrange("p (g q) -> p g q", q=512)[:, g4, 0:260]
                src = src.rearrange("p (s c) -> p s c", c=65)[:, :, 0:64]
                dst = Gb[:, 256 * g4:256 * (g4 + 1)].rearrange("p (s e) -> p s e", e=64)
                rb_ = recips[:, 4 * g4:4 * g4 + 4].rearrange("p (s o) -> p s o", o=1).to_broadcast([128, 4, 64])
                V(v, v.tensor_tensor(dst, src, rb_, op=ALU.mult))
        qo_v.append(n["vx"])

        @blk.sync
        def _(s, ts=ts, v_now=qo_v[ts], Gb=Gb):
            s.wait_ge(vx, v_now)
            for a in range(8):
                srcA = Gb[16 * a:16 * (a + 1), :].rearrange("p (h e) -> p h e", e=64)
                dstA = scr_dram.ap().rearrange("(h ra) (j e) -> h ra j e", h=16, j=16)[:, 8 * ts + a]
                dstA = dstA.transpose([1, 0, 2])
                inc("ld", s.dma_start(out=dstA, in_=srcA, allow_slow_non_contiguous=True), 16)
        scr_l.append(n["ld"])
    l_scr = n["ld"]
    t_qoall = n["tx"]

    @blk.scalar
    def _(s):
        s.wait_ge(ld, l_scr)
        s.wait_ge(tx, t_qoall)
        for c in range(8):
            inc("sd", s.dma_start_transpose(arenaB[:, c * TOK:(c + 1) * TOK],
                                            scr_dram.ap()[:, 128 * c:128 * (c + 1)]), 16)
    s_scrT = n["sd"]

    astage = arenaA[:, :].bitcast(F32)
    wproj_casts = []
    for c in range(8):
        @blk.sync
        def _(s, c=c):
            if c < 2:
                s.wait_ge(tx, t_qall)
            if c >= 2:
                s.wait_ge(vx, wproj_casts[c - 2])
            inc("ld", s.dma_start(out=astage[:, (c % 2) * 1024:(c % 2) * 1024 + 1024],
                                  in_=wprojT_d[128 * c:128 * (c + 1), :]), 16)

        @blk.vector
        def _(v, c=c, lm=n["ld"]):
            v.wait_ge(ld, lm)
            if c < 2:
                v.wait_ge(tx, t_kvall)
            V(v, v.tensor_copy(wproj_sb[:, c * D:(c + 1) * D],
                               astage[:, (c % 2) * 1024:(c % 2) * 1024 + 1024]))
        wproj_casts.append(n["vx"])
    g_wproj = n["vx"]

    xs_ld = []
    wr_marks = []
    @blk.sync
    def _(s):
        for t in range(2):
            inc("ld", s.dma_start(out=xt_buf[:, (t % 2) * D:(t % 2 + 1) * D],
                                  in_=x_S[128 * t:128 * (t + 1), :]), 16)
            xs_ld.append(n["ld"])

    pr_v = []
    for rs in range(NTS):
        pofs = (rs % 2) * 1024

        @blk.tensor
        def _(t, rs=rs, pofs=pofs):
            if rs == 0:
                t.wait_ge(sd, s_scrT)
                t.wait_ge(vx, g_wproj)
            if rs >= 2:
                t.wait_ge(vx, pr_v[rs - 2])
            for c in range(8):
                lhs = arenaB[:, :].rearrange("p (c k) -> p c k", k=TOK)[:, c, 128 * rs:128 * (rs + 1)]
                for j in range(2):
                    rhs = wproj_sb[:, c * D + 512 * j:c * D + 512 * (j + 1)]
                    inc("tx", t.matmul(P2[:, pofs + 512 * j:pofs + 512 * (j + 1)], lhs, rhs,
                                       start=(c == 0), stop=False, skip_group_check=True))
            for j in range(2):
                inc("tx", t.matmul(P2[:, pofs + 512 * j:pofs + 512 * (j + 1)], c_onesb[:, :],
                                   c_bproj[:, 512 * j:512 * (j + 1)], start=False, stop=True,
                                   skip_group_check=True))
        t_pr = n["tx"]

        @blk.vector
        def _(v, rs=rs, pofs=pofs, t_pr=t_pr):
            v.wait_ge(tx, t_pr)
            v.wait_ge(ld, xs_ld[rs])
            psl = P2[:, pofs:pofs + 1024]
            V(v, v.bn_stats(stat6[:, 0:6], (psl)[:, 0:512]))
            V(v, v.bn_stats(stat6[:, 6:12], (psl)[:, 512:1024]))
            V(v, v.bn_aggr(stat2[:, :], stat6[:, :]))
            V(v, v.tensor_scalar(negm[:, :], stat2[:, 0:1], -1.0, None, op0=ALU.mult))
        v_s1 = n["vx"]

        @blk.scalar
        def _(s, v_s1=v_s1):
            s.wait_ge(vx, v_s1)
            SS(s, s.activation(sqv[:, :], stat2[:, 1:2], AF.Sqrt, bias=c_sc[:, 4:5]))
        s_s1 = n["sx"]

        @blk.vector
        def _(v, rs=rs, pofs=pofs, s_s1=s_s1):
            v.wait_ge(sx, s_s1)
            xt = xt_buf[:, (rs % 2) * D:(rs % 2 + 1) * D]
            psl = P2[:, pofs:pofs + 1024]
            V(v, v.reciprocal(rstd[:, :], sqv[:, :]))
            V(v, v.tensor_scalar(u_t[:, :], psl, negm[:, :], rstd[:, :], op0=ALU.add, op1=ALU.mult))
            V(v, v.bn_stats(stat6[:, 0:6], (xt)[:, 0:512]))
            V(v, v.bn_stats(stat6[:, 6:12], (xt)[:, 512:1024]))
            V(v, v.bn_aggr(stat2[:, :], stat6[:, :]))
            V(v, v.tensor_scalar(negm[:, :], stat2[:, 0:1], -1.0, None, op0=ALU.mult))
        v_s2 = n["vx"]

        @blk.scalar
        def _(s, v_s2=v_s2):
            s.wait_ge(vx, v_s2)
            SS(s, s.activation(sqv[:, :], stat2[:, 1:2], AF.Sqrt, bias=c_sc[:, 4:5]))
        s_s2 = n["sx"]

        orow = acc1 if rs % 2 == 0 else acc2

        @blk.vector
        def _(v, rs=rs, pofs=pofs, s_s2=s_s2, orow=orow):
            v.wait_ge(sx, s_s2)
            if rs >= 2:
                v.wait_ge(wr, wr_marks[rs - 2])
            xt = xt_buf[:, (rs % 2) * D:(rs % 2 + 1) * D]
            V(v, v.reciprocal(rstd[:, :], sqv[:, :]))
            V(v, v.tensor_scalar(rstd[:, :], rstd[:, :], 2.0, None, op0=ALU.mult))
            V(v, v.tensor_scalar(xns2[:, :], xt, negm[:, :], rstd[:, :], op0=ALU.add, op1=ALU.mult))
            V(v, v.tensor_tensor(xns2[:, :], xns2[:, :], xt, op=ALU.add))
            V(v, v.tensor_tensor(xns2[:, :], xns2[:, :], bet_rep[:, :], op=ALU.add))
            V(v, v.tensor_tensor(orow[:, :], u_t[:, :], gam_rep[:, :], op=ALU.mult))
            V(v, v.tensor_tensor(orow[:, :], orow[:, :], xns2[:, :], op=ALU.add))
        pr_v.append(n["vx"])

        @blk.sync
        def _(s, rs=rs, v_now=pr_v[rs], orow=orow):
            s.wait_ge(vx, v_now)
            inc("wr", s.dma_start(out=out_d[128 * rs:128 * (rs + 1), :], in_=orow[:, :]), 16)
            wr_marks.append(n["wr"])
            if rs + 2 < NTS:
                inc("ld", s.dma_start(out=xt_buf[:, (rs % 2) * D:(rs % 2 + 1) * D],
                                      in_=x_S[128 * (rs + 2):128 * (rs + 3), :]), 16)
                xs_ld.append(n["ld"])
    w_base = n["wr"]
    t_projall = n["tx"]


    @blk.gpsimd
    def _(g):
        g.wait_ge(gd, g_gath)
        inc("gx", g.tensor_copy(posq_f[:, :], posq_i[:, :]))
        inc("gx", g.tensor_copy(posk_f[:, :], posk_i[:, :]))
    g_posf = n["gx"]

    def rope_tables(tag, posf, col, cosT, sinT, scale):
        gprev0 = n["gx"]
        sprev0 = n["sx"]
        @blk.vector
        def _(v, posf=posf, col=col, gprev0=gprev0):
            v.wait_ge(gx, max(gprev0, g_posf))
            v.wait_ge(sx, s_inv)
            V(v, v.tensor_scalar(angf[:, :], inv_rep[:, :], posf[:, col:col + 1], None, op0=ALU.mult))
            V(v, v.tensor_scalar(angm[:, :], angf[:, :], 1.0 / TP, None, op0=ALU.mult))
        v_a = n["vx"]

        @blk.gpsimd
        def _(g, v_a=v_a):
            g.wait_ge(vx, v_a)
            inc("gx", g.tensor_copy(angk_i[:, :], angm[:, :]))
            inc("gx", g.tensor_copy(angkf[:, :], angk_i[:, :]))
        g_a = n["gx"]

        @blk.vector
        def _(v, g_a=g_a, sprev0=sprev0):
            v.wait_ge(gx, g_a)
            v.wait_ge(sx, sprev0)
            V(v, v.tensor_scalar(ang2[:, :], angkf[:, :], -C1, None, op0=ALU.mult))
            V(v, v.tensor_tensor(angr[:, :], ang2[:, :], angf[:, :], op=ALU.add))
            V(v, v.tensor_scalar(ang2[:, :], angkf[:, :], -C2, None, op0=ALU.mult))
            V(v, v.tensor_tensor(angr[:, :], ang2[:, :], angr[:, :], op=ALU.add))
            V(v, v.tensor_scalar(ang2[:, :], angkf[:, :], -C3, None, op0=ALU.mult))
            V(v, v.tensor_tensor(angr[:, :], ang2[:, :], angr[:, :], op=ALU.add))
            V(v, v.tensor_scalar(angm[:, :], angr[:, :], PI, None, op0=ALU.is_gt))
            V(v, v.tensor_scalar(angm[:, :], angm[:, :], -TP, None, op0=ALU.mult))
            V(v, v.tensor_tensor(ang2[:, :], angm[:, :], angr[:, :], op=ALU.add))
        v_b = n["vx"]

        @blk.scalar
        def _(s, cosT=cosT, col=col, v_b=v_b):
            s.wait_ge(vx, v_b)
            SS(s, s.activation(sinT[:, 64 * col:64 * col + 32], ang2[:, :], AF.Sin))
            SS(s, s.activation(sinT[:, 64 * col + 32:64 * col + 64], ang2[:, :], AF.Sin))
        s_b = n["sx"]

        @blk.vector
        def _(v, v_b=v_b, s_b=s_b):
            v.wait_ge(sx, s_b)
            V(v, v.tensor_scalar(ang2[:, :], angr[:, :], PI / 2, None, op0=ALU.add))
            V(v, v.tensor_scalar(angm[:, :], ang2[:, :], PI, None, op0=ALU.is_gt))
            V(v, v.tensor_scalar(angm[:, :], angm[:, :], -TP, None, op0=ALU.mult))
            V(v, v.tensor_tensor(ang2[:, :], angm[:, :], ang2[:, :], op=ALU.add))
        v_c = n["vx"]

        @blk.scalar
        def _(s, cosT=cosT, col=col, v_c=v_c):
            s.wait_ge(vx, v_c)
            SS(s, s.activation(cosT[:, 64 * col:64 * col + 32], ang2[:, :], AF.Sin))
            SS(s, s.activation(cosT[:, 64 * col + 32:64 * col + 64], ang2[:, :], AF.Sin))
        s_c = n["sx"]

        if scale != 1.0:
            @blk.vector
            def _(v, cosT=cosT, col=col, s_c=s_c, scale=scale):
                v.wait_ge(sx, s_c)
                V(v, v.tensor_scalar(cosT[:, 64 * col:64 * (col + 1)], cosT[:, 64 * col:64 * (col + 1)],
                                     scale, None, op0=ALU.mult))
                V(v, v.tensor_scalar(sinT[:, 64 * col:64 * (col + 1)], sinT[:, 64 * col:64 * (col + 1)],
                                     scale, None, op0=ALU.mult))
        return n["vx"], n["sx"]

    for col in range(3):
        rope_tables("q", posq_f, col, cosq, sinq, 0.125)
    for col in range(2):
        rope_tables("k", posk_f, col, cosk, sink, 1.0)
    v_rope_t = n["vx"]
    s_rope_t = n["sx"]

    def ln_rms(xgv, hnv, ncols, lnrep):
        marks = []
        for c in range(ncols):
            @blk.vector
            def _(v, c=c, xgv=xgv):
                v.wait_ge(gd, g_gath)
                v.wait_ge(tx, t_qoall)
                v.wait_ge(ld, l_scr)
                v.wait_ge(wr, w_base)
                xt = xgv[:, c * D:(c + 1) * D]
                V(v, v.bn_stats(stat6[:, 0:6], (xt)[:, 0:512]))
                V(v, v.bn_stats(stat6[:, 6:12], (xt)[:, 512:1024]))
                V(v, v.bn_aggr(stat2[:, :], stat6[:, :]))
                V(v, v.tensor_scalar(negm[:, :], stat2[:, 0:1], -1.0, None, op0=ALU.mult))
            v_1 = n["vx"]

            @blk.scalar
            def _(s, v_1=v_1):
                s.wait_ge(vx, v_1)
                SS(s, s.activation(sqv[:, :], stat2[:, 1:2], AF.Sqrt, bias=c_sc[:, 4:5]))
            s_1 = n["sx"]

            @blk.vector
            def _(v, c=c, xgv=xgv, s_1=s_1):
                v.wait_ge(sx, s_1)
                xt = xgv[:, c * D:(c + 1) * D]
                V(v, v.reciprocal(rstd[:, :], sqv[:, :]))
                V(v, v.tensor_scalar(xt, xt, negm[:, :], rstd[:, :], op0=ALU.add, op1=ALU.mult))
                V(v, v.bn_stats(stat6[:, 0:6], (xt)[:, 0:512]))
                V(v, v.bn_stats(stat6[:, 6:12], (xt)[:, 512:1024]))
                V(v, v.bn_aggr(stat2[:, :], stat6[:, :]))
                V(v, v.tensor_tensor(negm[:, :], stat2[:, 0:1], stat2[:, 0:1], op=ALU.mult))
                V(v, v.tensor_tensor(negm[:, :], negm[:, :], stat2[:, 1:2], op=ALU.add))
            v_2 = n["vx"]

            @blk.scalar
            def _(s, v_2=v_2):
                s.wait_ge(vx, v_2)
                SS(s, s.activation(sqv[:, :], negm[:, :], AF.Sqrt, bias=c_sc[:, 5:6]))
            s_2 = n["sx"]

            @blk.vector
            def _(v, c=c, xgv=xgv, hnv=hnv, lnrep=lnrep, s_2=s_2):
                v.wait_ge(sx, s_2)
                xt = xgv[:, c * D:(c + 1) * D]
                V(v, v.reciprocal(rstd[:, :], sqv[:, :]))
                V(v, v.tensor_scalar(rop_a[:, :], xt, rstd[:, :], None, op0=ALU.mult))
                V(v, v.tensor_tensor(hnv[:, c * D:(c + 1) * D], rop_a[:, :], lnrep[:, :], op=ALU.mult))
            marks.append(n["vx"])
        return marks

    hnq_marks = ln_rms(xnq_v, hn_q, 3, ln1_rep)
    hnk_marks = ln_rms(xnk_v, hn_k, 2, ln1_rep)
    v_hn = n["vx"]

    @blk.sync
    def _(s):
        s.wait_ge(vx, v_hn)
        for c in range(3):
            inc("ld", s.dma_start(out=t_scr.ap()[128 * c:128 * (c + 1), :], in_=hn_q[:, c * D:(c + 1) * D]), 16)
        for c in range(2):
            inc("ld", s.dma_start(out=tk_scr.ap()[128 * c:128 * (c + 1), :], in_=hn_k[:, c * D:(c + 1) * D]), 16)
    l_hnw = n["ld"]

    @blk.scalar
    def _(s):
        s.wait_ge(ld, l_hnw)
        s.wait_ge(tx, n["tx"])
        for c in range(8):
            inc("sd", s.dma_start_transpose(hnTq_v[:, c * QCAP:(c + 1) * QCAP],
                                            t_scr.ap()[:, 128 * c:128 * (c + 1)]), 16)
            inc("sd", s.dma_start_transpose(hnTk_v[:, c * KSL:(c + 1) * KSL],
                                            tk_scr.ap()[:, 128 * c:128 * (c + 1)]), 16)
    s_hnT = n["sd"]

    sqstage = A3[:, 0:8192].bitcast(F32)
    sq_prev = [None]

    def load_sq_weight(wd, marker):
        casts = []
        for c in range(8):
            @blk.sync
            def _(s, wd=wd, c=c, marker=marker, prev=sq_prev[0]):
                if c == 0:
                    if marker is not None:
                        s.wait_ge(tx, marker)
                    if prev is not None:
                        s.wait_ge(vx, prev)
                if c >= 2:
                    s.wait_ge(vx, casts[c - 2])
                inc("ld", s.dma_start(out=sqstage[:, (c % 2) * 2048:(c % 2) * 2048 + 1024],
                                      in_=wd[128 * c:128 * (c + 1), :]), 16)

            @blk.vector
            def _(v, c=c, lm=n["ld"], marker=marker):
                v.wait_ge(ld, lm)
                if c == 0 and marker is not None:
                    v.wait_ge(tx, marker)
                V(v, v.tensor_copy(wproj_sb[:, c * D:(c + 1) * D],
                                   sqstage[:, (c % 2) * 2048:(c % 2) * 2048 + 1024]))
            casts.append(n["vx"])
        sq_prev[0] = n["vx"]
        return n["vx"]

    g_wqld = load_sq_weight(wq_d, n["tx"])

    qproj_v = []
    for sub in range(3):
        @blk.tensor
        def _(t, sub=sub):
            t.wait_ge(sd, s_hnT)
            t.wait_ge(vx, g_wqld)
            if sub >= 2:
                t.wait_ge(vx, qproj_v[sub - 2])
            pofs = (sub % 2) * 1024
            for c in range(8):
                lhs = hnTq_v[:, c * QCAP + 128 * sub: c * QCAP + 128 * (sub + 1)]
                for j in range(2):
                    inc("tx", t.matmul(P2[:, pofs + 512 * j:pofs + 512 * (j + 1)], lhs,
                                       wproj_sb[:, c * D + 512 * j:c * D + 512 * (j + 1)],
                                       start=(c == 0), stop=(c == 7), skip_group_check=True))
        t_q1 = n["tx"]

        @blk.vector
        def _(v, sub=sub, t_q1=t_q1):
            v.wait_ge(tx, t_q1)
            pofs = (sub % 2) * 1024
            V(v, v.tensor_copy(qp_v[:, sub * D:(sub + 1) * D], P2[:, pofs:pofs + 1024]))
        qproj_v.append(n["vx"])
    t_qproj = n["tx"]

    rope_q_v = []
    for sub in range(3):
        @blk.vector
        def _(v, sub=sub):
            v.wait_ge(vx, max(qproj_v[sub], v_rope_t))
            qt = qp_v[:, sub * D:(sub + 1) * D]
            ct = cosq[:, 64 * sub:64 * (sub + 1)]
            st_ = sinq[:, 64 * sub:64 * (sub + 1)]
            ctb = ct.rearrange("p (o e) -> p o e", o=1).to_broadcast([128, H, 64])
            V(v, v.tensor_tensor(rop_a[:, :].rearrange("p (h e) -> p h e", e=64),
                                 qt.rearrange("p (h e) -> p h e", e=64), ctb, op=ALU.mult))
            q4 = qt.rearrange("p (h a e) -> p h a e", a=2, e=32)
            a4 = rop_a[:, :].rearrange("p (h a e) -> p h a e", a=2, e=32)
            o4 = qr_v[:, sub * D:(sub + 1) * D].rearrange("p (h a e) -> p h a e", a=2, e=32)
            slb = st_[:, 0:32].rearrange("p (o e) -> p o e", o=1).to_broadcast([128, H, 32])
            shb = st_[:, 32:64].rearrange("p (o e) -> p o e", o=1).to_broadcast([128, H, 32])
            V(v, v.tensor_tensor(ang2[:, :].rearrange("p (o e) -> p o e", o=1)[:, 0:1, 0:32], angf[:, 0:1].rearrange("p (o e) -> p o e", o=1)[:, :, 0:1].to_broadcast([128, 1, 32]), angf[:, 0:1].rearrange("p (o e) -> p o e", o=1)[:, :, 0:1].to_broadcast([128, 1, 32]), op=ALU.mult)) if False else None
            V(v, v.tensor_tensor(xns2[:, 0:512].rearrange("p (h e) -> p h e", e=32),
                                 q4[:, :, 1, :], slb, op=ALU.mult))
            V(v, v.tensor_tensor(o4[:, :, 0, :], a4[:, :, 0, :],
                                 xns2[:, 0:512].rearrange("p (h e) -> p h e", e=32), op=ALU.subtract))
            V(v, v.tensor_tensor(xns2[:, 0:512].rearrange("p (h e) -> p h e", e=32),
                                 q4[:, :, 0, :], shb, op=ALU.mult))
            V(v, v.tensor_tensor(o4[:, :, 1, :], a4[:, :, 1, :],
                                 xns2[:, 0:512].rearrange("p (h e) -> p h e", e=32), op=ALU.add))
        rope_q_v.append(n["vx"])
    v_ropeq = n["vx"]

    g_wkld = load_sq_weight(wk_d, t_qproj)
    kproj_v = []
    for sub in range(2):
        @blk.tensor
        def _(t, sub=sub):
            t.wait_ge(vx, g_wkld)
            t.wait_ge(tx, t_qproj)
            if sub >= 2:
                t.wait_ge(vx, kproj_v[sub - 2])
            pofs = (sub % 2) * 1024
            for c in range(8):
                lhs = hnTk_v[:, c * KSL + 128 * sub: c * KSL + 128 * (sub + 1)]
                for j in range(2):
                    inc("tx", t.matmul(P2[:, pofs + 512 * j:pofs + 512 * (j + 1)], lhs,
                                       wproj_sb[:, c * D + 512 * j:c * D + 512 * (j + 1)],
                                       start=(c == 0), stop=(c == 7), skip_group_check=True))
        t_k1 = n["tx"]

        @blk.vector
        def _(v, sub=sub, t_k1=t_k1):
            v.wait_ge(tx, t_k1)
            v.wait_ge(vx, v_rope_t)
            pofs = (sub % 2) * 1024
            ct = cosk[:, 64 * sub:64 * (sub + 1)]
            st_ = sink[:, 64 * sub:64 * (sub + 1)]
            ctb = ct.rearrange("p (o e) -> p o e", o=1).to_broadcast([128, H, 64])
            V(v, v.tensor_copy(rop_h1[:, :], P2[:, pofs:pofs + 1024]))
            ps4 = rop_h1[:, :]
            V(v, v.tensor_tensor(rop_a[:, :].rearrange("p (h e) -> p h e", e=64),
                                 ps4.rearrange("p (h e) -> p h e", e=64), ctb, op=ALU.mult))
            q4 = ps4.rearrange("p (h a e) -> p h a e", a=2, e=32)
            a4 = rop_a[:, :].rearrange("p (h a e) -> p h a e", a=2, e=32)
            o4 = kr_v[:, sub * D:(sub + 1) * D].rearrange("p (h a e) -> p h a e", a=2, e=32)
            slb = st_[:, 0:32].rearrange("p (o e) -> p o e", o=1).to_broadcast([128, H, 32])
            shb = st_[:, 32:64].rearrange("p (o e) -> p o e", o=1).to_broadcast([128, H, 32])
            V(v, v.tensor_tensor(xns2[:, 0:512].rearrange("p (h e) -> p h e", e=32),
                                 q4[:, :, 1, :], slb, op=ALU.mult))
            V(v, v.tensor_tensor(o4[:, :, 0, :], a4[:, :, 0, :],
                                 xns2[:, 0:512].rearrange("p (h e) -> p h e", e=32), op=ALU.subtract))
            V(v, v.tensor_tensor(xns2[:, 0:512].rearrange("p (h e) -> p h e", e=32),
                                 q4[:, :, 0, :], shb, op=ALU.mult))
            V(v, v.tensor_tensor(o4[:, :, 1, :], a4[:, :, 1, :],
                                 xns2[:, 0:512].rearrange("p (h e) -> p h e", e=32), op=ALU.add))
        kproj_v.append(n["vx"])
    t_kproj = n["tx"]
    v_kr = n["vx"]

    g_wvld = load_sq_weight(wv_d, t_kproj)
    vproj_v = []
    for sub in range(2):
        @blk.tensor
        def _(t, sub=sub):
            t.wait_ge(vx, g_wvld)
            t.wait_ge(vx, v_kr)
            pofs = (sub % 2) * 1024
            for c in range(8):
                lhs = hnTk_v[:, c * KSL + 128 * sub: c * KSL + 128 * (sub + 1)]
                for j in range(2):
                    inc("tx", t.matmul(P2[:, pofs + 512 * j:pofs + 512 * (j + 1)], lhs,
                                       wproj_sb[:, c * D + 512 * j:c * D + 512 * (j + 1)],
                                       start=(c == 0), stop=(c == 7), skip_group_check=True))
        t_v1 = n["tx"]

        @blk.vector
        def _(v, sub=sub, t_v1=t_v1):
            v.wait_ge(tx, t_v1)
            pofs = (sub % 2) * 1024
            V(v, v.tensor_copy(vp_v[:, sub * D:(sub + 1) * D], P2[:, pofs:pofs + 1024]))
        vproj_v.append(n["vx"])
    t_vproj = n["tx"]
    v_vp = n["vx"]

    @blk.sync
    def _(s):
        s.wait_ge(vx, max(v_kr, v_vp))
        for c in range(2):
            inc("ld", s.dma_start(out=kvf_ag_in.ap()[128 * c:128 * (c + 1), 0:D], in_=kr_v[:, c * D:(c + 1) * D]), 16)
            inc("ld", s.dma_start(out=kvf_ag_in.ap()[128 * c:128 * (c + 1), D:2 * D], in_=vp_v[:, c * D:(c + 1) * D]), 16)
    l_kvfin = n["ld"]

    @blk.gpsimd
    def _(g):
        g.wait_ge(ld, l_kvfin)
        inc("cc", g.collective_compute("AllGather", ALU.bypass, replica_groups=GROUPS,
                                       ins=[kvf_ag_in.ap().opt()], outs=[kvf_ag_out.ap().opt()]))
    c_kvf = n["cc"]

    mga_b = arenaB[:, 0:8192].bitcast(F32)
    m_marks = []
    for c in range(3):
        @blk.gpsimd
        def _(g, c=c, pv=n["gx"]):
            g.wait_ge(tx, t_projall)
            g.wait_ge(gx, g_rhoi)
            if c > 0:
                g.wait_ge(gd, m_marks[c - 1][0])
            inc("gd", g.indirect_dma_start(mga_b, None, mask_f,
                                           IndirectOffsetOnAxis(ap=cq_i[:, c:c + 1], axis=0)), 16)
        g_m1 = n["gd"]

        @blk.gpsimd
        def _(g, c=c, g_m1=g_m1):
            g.wait_ge(gd, g_m1)
            inc("gd", g.dma_start(out=m_scr1.ap()[128 * c:128 * (c + 1), :], in_=mga_b), 16)
        m_marks.append((n["gd"], 0))
    g_mrows = n["gd"]

    @blk.scalar
    def _(s):
        s.wait_ge(gd, g_mrows)
        s.wait_ge(tx, t_qoall)
        bufs = [mtr[:, :], qtz_e[:, :], qtz_o[:, :]]
        wmk = {}
        for cc_ in range(32):
            buf = bufs[cc_ % 3]
            if cc_ >= 3:
                s.wait_ge(sd, wmk[cc_ - 3])
            inc("sd", s.dma_start_transpose(buf, m_scr1.ap()[:, 128 * cc_:128 * (cc_ + 1)]), 16)
            s.wait_ge(sd, n["sd"])
            inc("sd", s.dma_start(out=m_scr2.ap()[128 * cc_:128 * (cc_ + 1), :], in_=buf), 16)
            wmk[cc_] = n["sd"]
    s_mtr = n["sd"]

    @blk.vector
    def _(v):
        v.wait_ge(sd, s_mtr)
        V(v, v.memset(mtr[0:1, :], -1e9))
    v_mneg = n["vx"]

    @blk.sync
    def _(s):
        s.wait_ge(vx, v_mneg)
        s.wait_ge(sd, s_mtr)
        inc("ld", s.dma_start(out=m_scr2.ap()[4096:4097, :], in_=mtr[0:1, :]), 16)
    l_mneg = n["ld"]

    @blk.gpsimd
    def _(g):
        g.wait_ge(ld, l_mneg)
        g.wait_ge(tx, t_vproj)
        for c in range(8):
            inc("gd", g.indirect_dma_start(e01T_v[:, c * QCAP:(c + 1) * QCAP], None, m_scr2.ap(),
                                           IndirectOffsetOnAxis(ap=ckf_i[:, c:c + 1], axis=0)), 16)
    g_mselT = n["gd"]

    @blk.scalar
    def _(s):
        s.wait_ge(gd, g_mselT)
        SS(s, s.activation(e01T_v, e01T_v, AF.Exp))
    s_e01 = n["sx"]

    @blk.scalar
    def _(s):
        s.wait_ge(cc, c_kvf)
        s.wait_ge(tx, t_vproj)
        s.wait_ge(gd, g_mrows)
        for c in range(8):
            inc("sd", s.dma_start_transpose(kT_v[:, c * KCAP:(c + 1) * KCAP],
                                            kvf_ag_out.ap()[:, 128 * c:128 * (c + 1)]), 16)
    s_kT = n["sd"]

    @blk.vector
    def _(v):
        v.wait_ge(tx, t_qoall)
        ones_cols = vA_v.rearrange("p (t h c) -> p t h c", h=16, c=65)[:, :, :, 64:65]
        V(v, v.memset(ones_cols, 1.0))
    v_vones = n["vx"]

    @blk.sync
    def _(s):
        s.wait_ge(cc, c_kvf)
        s.wait_ge(vx, v_vones)
        for t in range(8):
            src = kvf_ag_out.ap()[128 * t:128 * (t + 1), D:2 * D].rearrange("p (h c) -> p h c", c=64)
            dst = vA_v.rearrange("p (t h c) -> p t h c", h=16, c=65)[:, t, :, 0:64]
            inc("ld", s.dma_start(out=dst, in_=src), 16)
    g_vA = n["ld"]

    @blk.sync
    def _(s):
        s.wait_ge(vx, v_ropeq)
        for c in range(3):
            inc("ld", s.dma_start(out=t_scr.ap()[128 * c:128 * (c + 1), :], in_=qr_v[:, c * D:(c + 1) * D]), 16)
    l_qw = n["ld"]

    @blk.scalar
    def _(s):
        s.wait_ge(ld, l_qw)
        for c in range(8):
            inc("sd", s.dma_start_transpose(qT_v[:, c * QCAP:(c + 1) * QCAP],
                                            t_scr.ap()[:, 128 * c:128 * (c + 1)]), 16)
    s_qT = n["sd"]

    @blk.vector
    def _(v):
        v.wait_ge(tx, t_qoall)
        v.wait_ge(sd, s_mtr)
        V(v, v.memset(qtz_e[64:128, :], 0.0))
        V(v, v.memset(qtz_o[0:64, :], 0.0))
    v_qtz0 = n["vx"]

    head_v = []
    head_s = []
    for h in range(H):
        ec = h // 2
        odd = h % 2
        qtz = qtz_o if odd else qtz_e

        @blk.vector
        def _(v, h=h, ec=ec, odd=odd, qtz=qtz):
            v.wait_ge(vx, v_qtz0)
            if h < 2:
                v.wait_ge(sd, s_qT)
            else:
                v.wait_ge(sx, head_s[h - 1])
            V(v, v.tensor_copy(qtz[64 * odd:64 * odd + 64, :],
                               qT_v[64 * odd:64 * odd + 64, ec * QCAP:(ec + 1) * QCAP]))
        v_qtzh = n["vx"]

        exp_marks = []
        for kt in range(8):
            @blk.tensor
            def _(t, h=h, kt=kt, ec=ec, odd=odd, qtz=qtz, v_qtzh=v_qtzh):
                if h == 0 and kt == 0:
                    t.wait_ge(sd, max(s_kT, s_qT))
                    t.wait_ge(ld, g_vA)
                    t.wait_ge(sx, s_e01)
                if kt == 0:
                    t.wait_ge(vx, v_qtzh)
                if kt >= 2:
                    t.wait_ge(sx, exp_marks[kt - 2])
                if h > 0 and kt < 2:
                    t.wait_ge(sx, head_s[h - 1])
                lhs = kT_v[:, ec * KCAP + 128 * kt: ec * KCAP + 128 * (kt + 1)]
                rhs = qtz[:, :]
                pofs = (kt % 2) * 512
                inc("tx", t.matmul(P3[:, pofs:pofs + QCAP], lhs, rhs, start=True, stop=True,
                                   skip_group_check=True))
            t_sc = n["tx"]

            @blk.scalar
            def _(s, h=h, kt=kt, t_sc=t_sc):
                s.wait_ge(tx, t_sc)
                pofs = (kt % 2) * 512
                SS(s, s.activation(expT_v[:, kt * QCAP:(kt + 1) * QCAP], P3[:, pofs:pofs + QCAP],
                                   AF.Exp, bias=c_sc[:, 6:7]))
            exp_marks.append(n["sx"])

        @blk.vector
        def _(v, h=h, em=exp_marks[7]):
            v.wait_ge(sx, em)
            V(v, v.tensor_tensor(expT_v, expT_v, e01T_v, op=ALU.mult))
        v_em = n["vx"]

        av_done = []
        for kt in range(8):
            @blk.tensor
            def _(t, h=h, kt=kt, v_em=v_em):
                t.wait_ge(vx, v_em)
                lhs = vA_v[:, kt * 1040 + 65 * h: kt * 1040 + 65 * (h + 1)]
                rhs = expT_v[:, kt * QCAP:(kt + 1) * QCAP]
                inc("tx", t.matmul(P4[0:65, 0:QCAP], lhs, rhs, start=(kt == 0), stop=(kt == 7),
                                   skip_group_check=True))
        t_av = n["tx"]

        @blk.vector
        def _(v, h=h, t_av=t_av):
            v.wait_ge(tx, t_av)
            V(v, v.reciprocal(rrec[:, :], P4[64:65, 0:QCAP]))
        v_rr = n["vx"]

        @blk.tensor
        def _(t, v_rr=v_rr):
            t.wait_ge(vx, v_rr)
            inc("tx", t.matmul(P1[0:64, 0:QCAP], c_ones[:, 0:64], rrec[:, :], start=True, stop=True))
        t_rr = n["tx"]

        @blk.vector
        def _(v, h=h, ec=ec, odd=odd, t_rr=t_rr):
            v.wait_ge(tx, t_rr)
            V(v, v.tensor_copy(rrep_sb[:, :], P1[0:64, 0:QCAP]))
            dst = oT_v[64 * odd:64 * odd + 64, ec * QCAP:(ec + 1) * QCAP]
            V(v, v.tensor_tensor(dst, P4[0:64, 0:QCAP], rrep_sb[:, :], op=ALU.mult))
        head_v.append(n["vx"])
        head_s.append(n["sx"])
    v_heads = n["vx"]

    g_wold = load_sq_weight(wo_d, n["tx"])
    wo_v = []
    for sub in range(3):
        @blk.tensor
        def _(t, sub=sub):
            t.wait_ge(vx, g_wold)
            t.wait_ge(vx, v_heads)
            if sub >= 2:
                t.wait_ge(vx, wo_v[sub - 2])
            pofs = (sub % 2) * 1024
            for c in range(8):
                lhs = oT_v[:, c * QCAP + 128 * sub: c * QCAP + 128 * (sub + 1)]
                for j in range(2):
                    inc("tx", t.matmul(P2[:, pofs + 512 * j:pofs + 512 * (j + 1)], lhs,
                                       wproj_sb[:, c * D + 512 * j:c * D + 512 * (j + 1)],
                                       start=(c == 0), stop=(c == 7), skip_group_check=True))
        t_wo = n["tx"]

        @blk.vector
        def _(v, sub=sub, t_wo=t_wo):
            v.wait_ge(tx, t_wo)
            pofs = (sub % 2) * 1024
            V(v, v.tensor_tensor(x1_v[:, sub * D:(sub + 1) * D], P2[:, pofs:pofs + 1024],
                                 xnq_v[:, sub * D:(sub + 1) * D], op=ALU.add))
        wo_v.append(n["vx"])
    t_woall = n["tx"]

    h1_marks = []
    for c in range(3):
        @blk.vector
        def _(v, c=c):
            v.wait_ge(vx, wo_v[c])
            xt = x1_v[:, c * D:(c + 1) * D]
            V(v, v.bn_stats(stat6[:, 0:6], (xt)[:, 0:512]))
            V(v, v.bn_stats(stat6[:, 6:12], (xt)[:, 512:1024]))
            V(v, v.bn_aggr(stat2[:, :], stat6[:, :]))
            V(v, v.tensor_tensor(negm[:, :], stat2[:, 0:1], stat2[:, 0:1], op=ALU.mult))
            V(v, v.tensor_tensor(negm[:, :], negm[:, :], stat2[:, 1:2], op=ALU.add))
        v_1 = n["vx"]

        @blk.scalar
        def _(s, v_1=v_1):
            s.wait_ge(vx, v_1)
            SS(s, s.activation(sqv[:, :], negm[:, :], AF.Sqrt, bias=c_sc[:, 5:6]))
        s_1 = n["sx"]

        @blk.vector
        def _(v, c=c, s_1=s_1):
            v.wait_ge(sx, s_1)
            xt = x1_v[:, c * D:(c + 1) * D]
            V(v, v.reciprocal(rstd[:, :], sqv[:, :]))
            V(v, v.tensor_scalar(rop_h1[:, :], xt, rstd[:, :], None, op0=ALU.mult))
            V(v, v.tensor_tensor(h1b_v[:, c * D:(c + 1) * D], rop_h1[:, :], ln2_rep[:, :], op=ALU.mult))
        h1_marks.append(n["vx"])
    v_h1 = n["vx"]

    @blk.sync
    def _(s):
        s.wait_ge(vx, v_h1)
        for c in range(3):
            inc("ld", s.dma_start(out=t_scr.ap()[128 * c:128 * (c + 1), :], in_=h1b_v[:, c * D:(c + 1) * D]), 16)
    l_h1w = n["ld"]

    @blk.scalar
    def _(s):
        s.wait_ge(ld, l_h1w)
        for c in range(8):
            inc("sd", s.dma_start_transpose(h1T_v[:, c * QCAP:(c + 1) * QCAP],
                                            t_scr.ap()[:, 128 * c:128 * (c + 1)]), 16)
    s_h1T = n["sd"]

    w1stage = astage
    w1cast = []
    w1_g = []
    gelu_marks = []
    for fb in range(8):
        bofs = (fb % 2) * 4096

        @blk.sync
        def _(s, fb=fb):
            if fb < 2:
                s.wait_ge(tx, t_woall)
                s.wait_ge(vx, v_heads)
            if fb >= 2:
                s.wait_ge(vx, w1cast[fb - 2])
            for c in range(8):
                inc("ld", s.dma_start(out=w1stage[:, (fb % 2) * 4096 + 512 * c:
                                                  (fb % 2) * 4096 + 512 * (c + 1)],
                                      in_=w1_d[128 * c:128 * (c + 1), 512 * fb:512 * (fb + 1)]), 16)

        @blk.vector
        def _(v, fb=fb, bofs=bofs, lm=n["ld"]):
            v.wait_ge(ld, lm)
            if fb < 2:
                v.wait_ge(tx, t_woall)
            if fb >= 2:
                v.wait_ge(tx, w1_g[4 * (fb - 2) + 3])
            V(v, v.tensor_copy(w1b_v[:, bofs:bofs + 4096],
                               w1stage[:, (fb % 2) * 4096:(fb % 2) * 4096 + 4096]))
        w1cast.append(n["vx"])
        g_w1 = n["vx"]

        for fc in range(4):
            @blk.tensor
            def _(t, fb=fb, fc=fc, bofs=bofs, g_w1=g_w1):
                t.wait_ge(vx, g_w1)
                t.wait_ge(sd, s_h1T)
                if len(gelu_marks) >= 2:
                    t.wait_ge(sx, gelu_marks[-2])
                fcg = fb * 4 + fc
                psl = (P1 if fcg % 2 == 0 else P4)[:, 0:QCAP]
                for c in range(8):
                    lhs = w1b_v[:, bofs + 512 * c + 128 * fc: bofs + 512 * c + 128 * (fc + 1)]
                    inc("tx", t.matmul(psl, lhs, h1T_v[:, c * QCAP:(c + 1) * QCAP],
                                       start=(c == 0), stop=(c == 7), skip_group_check=True))
            t_g = n["tx"]
            w1_g.append(t_g)

            @blk.scalar
            def _(s, fb=fb, fc=fc, t_g=t_g):
                s.wait_ge(tx, t_g)
                fcg = fb * 4 + fc
                psl = (P1 if fcg % 2 == 0 else P4)[:, 0:QCAP]
                SS(s, s.activation(gT_v[:, fcg * QCAP:(fcg + 1) * QCAP], psl,
                                   AF.Gelu_apprx_tanh))
            gelu_marks.append(n["sx"])
    s_gelu = n["sx"]
    t_w1 = n["tx"]

    w2cast = []
    w2_g = []
    for fc in range(32):
        @blk.sync
        def _(s, fc=fc):
            if fc < 2:
                s.wait_ge(vx, w1cast[-1])
            if fc >= 2:
                s.wait_ge(vx, w2cast[fc - 2])
            inc("ld", s.dma_start(out=w1stage[:, (fc % 2) * 1024:(fc % 2) * 1024 + 1024],
                                  in_=w2_d[128 * fc:128 * (fc + 1), :]), 16)

        @blk.vector
        def _(v, fc=fc, lm=n["ld"]):
            v.wait_ge(ld, lm)
            if fc < 2:
                v.wait_ge(tx, t_woall)
            if fc >= 2:
                v.wait_ge(tx, w2_g[fc - 2])
            bofs = (fc % 2) * 1024
            V(v, v.tensor_copy(w2b_v[:, bofs:bofs + 1024],
                               w1stage[:, (fc % 2) * 1024:(fc % 2) * 1024 + 1024]))
        w2cast.append(n["vx"])
        g_w2 = n["vx"]

        @blk.tensor
        def _(t, fc=fc, g_w2=g_w2):
            t.wait_ge(vx, g_w2)
            t.wait_ge(sx, gelu_marks[fc])
            bofs = (fc % 2) * 1024
            for sub in range(3):
                psl = P2[:, 1024 * (sub % 2):1024 * (sub % 2) + 1024] if sub < 2 else P3[:, 0:1024]
                lhs = gT_v[:, fc * QCAP + 128 * sub: fc * QCAP + 128 * (sub + 1)]
                for j in range(2):
                    inc("tx", t.matmul(psl[:, 512 * j:512 * (j + 1)], lhs,
                                       w2b_v[:, bofs + 512 * j:bofs + 512 * (j + 1)],
                                       start=(fc == 0), stop=(fc == 31), skip_group_check=True))
        w2_g.append(n["tx"])
    t_w2 = n["tx"]

    corr_v = []
    for sub in range(3):
        @blk.vector
        def _(v, sub=sub):
            v.wait_ge(tx, t_w2)
            psl = P2[:, 1024 * (sub % 2):1024 * (sub % 2) + 1024] if sub < 2 else P3[:, 0:1024]
            xt = x1_v[:, sub * D:(sub + 1) * D]
            V(v, v.tensor_tensor(xt, xt, psl, op=ALU.add))
            V(v, v.tensor_scalar(xt, xt, wself[:, sub:sub + 1], None, op0=ALU.mult))
            V(v, v.tensor_tensor(xt, xt, xnq_v[:, sub * D:(sub + 1) * D], op=ALU.subtract))
            V(v, v.tensor_scalar(xt, xt, qkeep[:, sub:sub + 1], None, op0=ALU.mult))
        corr_v.append(n["vx"])
    v_corr = n["vx"]

    @blk.gpsimd
    def _(g):
        g.wait_ge(vx, v_corr)
        g.wait_ge(wr, w_base)
        for sub in range(3):
            inc("ws", g.indirect_dma_start(out_d, IndirectOffsetOnAxis(ap=qrho_i[:, sub:sub + 1], axis=0),
                                           x1_v[:, sub * D:(sub + 1) * D], None, compute_op=ALU.add), 16)
    w_all = n["ws"]

    @blk.sync
    def _(s):
        s.wait_ge(wr, w_base)
        s.wait_ge(ws, w_all)
        s.wait_ge(gd, n["gd"])
        s.wait_ge(sd, n["sd"])
        s.wait_ge(ld, n["ld"])
        s.wait_ge(vx, n["vx"])
        s.wait_ge(sx, n["sx"])
        s.wait_ge(tx, n["tx"])
        s.wait_ge(gx, n["gx"])
        s.wait_ge(cc, n["cc"])

    es.close()
    return nc


def _host_constants(s):
    g8 = np.zeros((128, 8), np.float32)
    for p in range(128):
        g8[p, p // 16] = 1.0
    lt = ((np.arange(128)[:, None]) < (np.arange(128)[None, :])).astype(np.float32)
    onesr = np.ones((1, 128), np.float32)
    riota = np.stack([128.0 * c + np.arange(128) for c in range(8)], 1).astype(np.float32)
    dc = np.zeros((128, ROUNDS), np.float32)
    for r in range(ROUNDS):
        d = D0 / (NTHR ** r)
        for p in range(128):
            dc[p, r] = np.float32(d * ((p // 16) + 1) / NTHR)
    sc = np.zeros((128, 8), np.float32)
    sc[:, 0] = 256.0 * s
    sc[:, 1] = 128.0 * s
    sc[:, 2] = float(s)
    sc[:, 4] = 1e-5
    sc[:, 5] = 1e-6
    sc[:, 6] = EXPB
    member = np.array([1.0 if ((p % 8) // 2) == s else 0.0 for p in range(128)], np.float32).reshape(128, 1)
    return g8, lt, onesr, riota, dc, sc, member


_NC_CACHE = [None]


def kernel(**inputs):
    x = np.ascontiguousarray(np.asarray(inputs["L0_attn_H"], np.float32))
    Wqkv = np.asarray(inputs["L0_attn_W_qkv"], np.float32)
    bqkv = np.asarray(inputs["L0_attn_b_qkv"], np.float32).reshape(1, 3 * D)
    Wproj = np.asarray(inputs["L0_attn_W_proj"], np.float32)
    bproj = np.asarray(inputs["L0_attn_b_proj"], np.float32).reshape(1, D)
    gamma = np.asarray(inputs["L0_attn_gamma"], np.float32).reshape(1, D)
    beta = np.asarray(inputs["L0_attn_beta"], np.float32).reshape(1, D)
    mask = np.asarray(inputs["L0_ffn_attention_mask"], np.float32)
    pos = np.asarray(inputs["L0_ffn_position_ids"], np.int32)
    rw = np.asarray(inputs["L0_ffn_router_w"], np.float32)
    rb = np.asarray(inputs["L0_ffn_router_b"], np.float32).reshape(1, 1)
    wq = np.ascontiguousarray(np.asarray(inputs["L0_ffn_wq"], np.float32))
    wk = np.ascontiguousarray(np.asarray(inputs["L0_ffn_wk"], np.float32))
    wv = np.ascontiguousarray(np.asarray(inputs["L0_ffn_wv"], np.float32))
    wo = np.ascontiguousarray(np.asarray(inputs["L0_ffn_wo"], np.float32))
    w1 = np.ascontiguousarray(np.asarray(inputs["L0_ffn_w1"], np.float32))
    w2 = np.ascontiguousarray(np.asarray(inputs["L0_ffn_w2"], np.float32))
    ln1 = np.asarray(inputs["L0_ffn_ln1"], np.float32).reshape(1, D)
    ln2 = np.asarray(inputs["L0_ffn_ln2"], np.float32).reshape(1, D)

    wqkvT = np.ascontiguousarray(Wqkv.T)
    wprojT = np.ascontiguousarray(Wproj.T)
    rw8 = np.ascontiguousarray(rw.reshape(8, 128).T)

    if _NC_CACHE[0] is None:
        _NC_CACHE[0] = build_nc()
    nc = _NC_CACHE[0]

    srows = {}
    in_maps = []
    for core in range(8):
        b = core // 4
        s = core % 4
        g8, lt, onesr, riota, dc, sc, member = _host_constants(s)
        rows = np.array([512 * h + 128 * s + r for h in range(H) for r in range(128)], np.int64)
        srows[core] = rows
        im = {
            "x_T": np.ascontiguousarray(x[b, 2048 * s:2048 * (s + 1)]),
            "x_S": np.ascontiguousarray(x[b][rows]),
            "x_full": x[b],
            "mask_f": np.ascontiguousarray(mask[b, 0]),
            "pos_f": np.ascontiguousarray(pos[b].reshape(1, 4096)),
            "wqkvT": wqkvT, "bqkv": bqkv, "wprojT": wprojT, "bproj": bproj,
            "gamma_d": gamma, "beta_d": beta,
            "rw_d": rw8, "rb_d": rb,
            "wq_d": wq, "wk_d": wk, "wv_d": wv, "wo_d": wo, "w1_d": w1, "w2_d": w2,
            "ln1_d": ln1, "ln2_d": ln2,
            "g8_d": g8, "lt_d": lt, "onesr_d": onesr, "riota_d": riota,
            "dc_d": dc, "sc_d": sc, "member_d": member,
            "bqq_d": np.ascontiguousarray(bqkv[0, 0:D].reshape(8, 128).T),
        }
        in_maps.append(im)

    res = run_bass_kernel_spmd(nc, in_maps, core_ids=list(range(8)))
    out = np.empty((B, N, D), np.float32)
    for core in range(8):
        b = core // 4
        out[b][srows[core]] = res.results[core]["out_sh"][:2048]
    return out
